# version 41
# baseline (speedup 1.0000x reference)
import numpy as np
import ml_dtypes
from contextlib import ExitStack

import concourse.bass as bass
import concourse.mybir as mybir
from concourse.bass_utils import run_bass_kernel_spmd

F32 = mybir.dt.float32
BF16 = mybir.dt.bfloat16
AF = mybir.ActivationFunctionType
ALU = mybir.AluOpType
AX = mybir.AxisListType

NCORES = 8
T = 2048
TG = T + 1
D = 1024
DI = 2048
NH = 8
DH = 64
WIN = 512
HID = 2730
NPAIR = 22
HIDP = NPAIR * 128
EPS_LN = 1e-5
EPS_LA = 1e-6
BIG = 3.0e38

ENGINES = ("sync", "gpsimd", "tensor", "vector", "scalar")


class Prog:
    def __init__(self, nc):
        self.nc = nc
        self.stream = {e: [] for e in ENGINES}
        self.semh = {}
        self.semc = {}
        self.last_wait = {}

    def sem(self, name, stack):
        self.semh[name] = stack.enter_context(self.nc.semaphore(name))
        self.semc[name] = 0

    def tok(self, name):
        return (name, self.semc[name])

    def op(self, eng, build, waits=(), inc=None, inck=1):
        ws = []
        for t in waits:
            if t is None:
                continue
            s, v = t
            if v <= 0:
                continue
            key = (eng, s)
            if self.last_wait.get(key, -1) >= v:
                continue
            self.last_wait[key] = v
            ws.append((s, v))
        token = None
        if inc is not None:
            self.semc[inc] += inck
            token = (inc, self.semc[inc])
        semh = self.semh

        def f(e, ws=tuple(ws), build=build, inc=inc, inck=inck):
            for s, v in ws:
                e.wait_ge(semh[s], v)
            inst = build(e)
            if inc is not None:
                inst.then_inc(semh[inc], inck)
        self.stream[eng].append(f)
        return token

    def dma(self, eng, out, in_, waits=(), inc="sd"):
        return self.op(eng, lambda e: e.dma_start(out=out, in_=in_), waits=waits,
                       inc=inc, inck=16)

    def finalize(self, stack):
        blk = stack.enter_context(self.nc.Block())
        streams = self.stream

        @blk.sync
        def _(e):
            for f in streams["sync"]:
                f(e)

        @blk.gpsimd
        def _(e):
            for f in streams["gpsimd"]:
                f(e)

        @blk.tensor
        def _(e):
            for f in streams["tensor"]:
                f(e)

        @blk.vector
        def _(e):
            for f in streams["vector"]:
                f(e)

        @blk.scalar
        def _(e):
            for f in streams["scalar"]:
                f(e)


def build_constants():
    c = {}
    half = np.arange(0, DH, 2, dtype=np.float64)
    inv_freq = 1.0 / 10000.0 ** (half / DH)
    t = np.arange(T, dtype=np.float64)
    fr = t[:, None] * inv_freq[None, :]
    fr = np.concatenate([fr, fr], axis=1)
    base = (half + 0.4 * DH) / (1.4 * DH)
    power = (t - T // 2) / float(WIN // 2)
    sc = base[None, :] ** power[:, None]
    sc = np.concatenate([sc, sc], axis=1)
    sign = np.where(np.arange(DH) < 32, -1.0, 1.0)[None, :]
    cq = 8.0 * np.cos(fr) * sc
    sq = 8.0 * np.sin(fr) * sc * sign
    ck = np.cos(fr) / sc
    sk = np.sin(fr) / sc * sign

    def tmajor(a):
        return np.ascontiguousarray(
            a.reshape(16, 128, DH).transpose(1, 0, 2).reshape(128, 16 * DH)
        ).astype(np.float32)

    c["tcq"], c["tsq"], c["tck"], c["tsk"] = map(tmajor, (cq, sq, ck, sk))
    r = np.arange(128)[:, None]
    cc = np.arange(128)[None, :]
    c["mask_l"] = np.where(cc < r, -BIG, 0.0).astype(np.float32)
    c["mask_r"] = np.where(cc > r, -BIG, 0.0).astype(np.float32)
    c["mask_lr"] = np.concatenate([c["mask_l"], c["mask_r"]], axis=1)
    c["triu01"] = (cc >= r).astype(ml_dtypes.bfloat16)
    c["ident"] = np.eye(128, dtype=ml_dtypes.bfloat16)
    return c


def build_vecblob(w):
    idx = {}
    cols = []

    def put(name, cols_list):
        idx[name] = len(cols)
        cols.extend(cols_list)

    def chunks(v, n):
        v = np.asarray(v, dtype=np.float32).ravel()
        v = np.pad(v, (0, n * 128 - v.size))
        return [v[i * 128:(i + 1) * 128] for i in range(n)]

    put("bq", chunks(w["L0_attn_bq"], 16))
    put("bk", chunks(w["L0_attn_bk"], 16))
    put("bo", chunks(w["L0_attn_bo"], 8))
    for l in range(2):
        put(f"ln_g{l}", chunks(w["L0_ffn_ln_g"][l], 8))
        put(f"ln_b{l}", chunks(w["L0_ffn_ln_b"][l], 8))
        b1 = np.asarray(w["L0_ffn_b1"][l], dtype=np.float32)
        put(f"b1a{l}", chunks(np.pad(b1[:HID], (0, HIDP - HID)), NPAIR))
        put(f"b1g{l}", chunks(np.pad(b1[HID:], (0, HIDP - HID)), NPAIR))
        put(f"b2{l}", chunks(w["L0_ffn_b2"][l], 8))
    blob = np.stack(cols, axis=1).astype(np.float32)
    return blob, idx


def host_prepare(inputs):
    h = {}
    x = np.asarray(inputs["L0_attn_x"], dtype=np.float32)
    xs = []
    for b in range(NCORES):
        xt = np.zeros((D, TG), dtype=np.float32)
        xt[:, 1:] = x[b].T
        xs.append(np.ascontiguousarray(xt.reshape(8, 128, TG)))
    h["x_cores"] = xs

    h["wq"] = np.ascontiguousarray(inputs["L0_attn_Wq"], dtype=np.float32)
    h["wk"] = np.ascontiguousarray(inputs["L0_attn_Wk"], dtype=np.float32)
    h["wv"] = np.ascontiguousarray(inputs["L0_attn_Wv"], dtype=np.float32)
    h["woA"] = np.ascontiguousarray(inputs["L0_attn_Wo"], dtype=np.float32)
    h["wqkv"] = np.ascontiguousarray(inputs["L0_ffn_wqkv"], dtype=np.float32)
    h["woF"] = np.ascontiguousarray(inputs["L0_ffn_wo"], dtype=np.float32)

    w1 = np.asarray(inputs["L0_ffn_w1"], dtype=np.float32)
    w1r = np.zeros((2, NPAIR, 128, 2048), np.float32)
    for l in range(2):
        w1a = np.zeros((D, HIDP), np.float32)
        w1g = np.zeros((D, HIDP), np.float32)
        w1a[:, :HID] = w1[l][:, :HID]
        w1g[:, :HID] = w1[l][:, HID:]
        for p in range(NPAIR):
            blk = np.concatenate([w1a[:, 128 * p:128 * p + 128],
                                  w1g[:, 128 * p:128 * p + 128]], axis=1)
            w1r[l, p] = blk.reshape(8, 128, 256).transpose(1, 0, 2).reshape(128, 2048)
    h["w1r"] = np.ascontiguousarray(w1r)

    w2 = np.asarray(inputs["L0_ffn_w2"], dtype=np.float32)
    w2p = np.zeros((2, HIDP, D), np.float32)
    w2p[:, :HID, :] = w2
    h["w2p"] = np.ascontiguousarray(w2p)

    h["qsks"] = np.concatenate([
        np.asarray(inputs["L0_ffn_q_scale"], np.float32),
        np.asarray(inputs["L0_ffn_k_scale"], np.float32)], axis=1).reshape(1, 256)
    h["bvrow"] = np.asarray(inputs["L0_attn_bv"], np.float32).reshape(1, 2048)

    h["blob"], h["blob_idx"] = build_vecblob(inputs)
    h.update(build_constants())
    return h


DBG_ATT = False
DBG_F = False


def build_program(h, parts=("ff1", "att", "ff2")):
    nc = bass.Bass("TRN2", target_bir_lowering=False)
    P = Prog(nc)
    st = ExitStack()

    dr = {}

    def din(name, shape, dt=F32):
        dr[name] = nc.dram_tensor(name, shape, dt, kind="ExternalInput").ap()

    din("xin", [8, 128, TG])
    din("wq", [D, DI]); din("wk", [D, DI]); din("wv", [D, DI]); din("woA", [DI, D])
    din("wqkv", [2, D, 1536]); din("woF", [2, 512, D])
    din("w1r", [2, NPAIR, 128, 2048]); din("w2p", [2, HIDP, D])
    nb = h["blob"].shape[1]
    din("blob", [128, nb])
    din("qsks", [1, 256]); din("bvrow", [1, 2048])
    din("tcq", [128, 1024]); din("tsq", [128, 1024])
    din("tck", [128, 1024]); din("tsk", [128, 1024])
    din("mask_l", [128, 128]); din("mask_r", [128, 128])
    din("triu01", [128, 128], BF16); din("ident", [128, 128], BF16)
    y = nc.dram_tensor("y", [8, 128, T], F32, kind="ExternalOutput").ap()
    ydbg = (nc.dram_tensor("ydbg", [48, 128, 512], F32, kind="ExternalOutput").ap()
            if (DBG_ATT or DBG_F) else None)
    xo_scr = nc.dram_tensor("xo_scr", [8, 128, TG], F32, kind="Internal").ap()
    q_scr = nc.dram_tensor("q_scr", [16, 128, T], BF16, kind="Internal").ap()
    k_scr = nc.dram_tensor("k_scr", [16, 16, 128, 128], BF16, kind="Internal").ap()
    v_scr = nc.dram_tensor("v_scr", [16, 16, 128, 128], BF16, kind="Internal").ap()

    for s in ("wd", "sd", "te", "ve", "ac", "ge"):
        P.sem(s, st)

    sb = {}

    def stile(name, shape, dt=F32):
        sb[name] = st.enter_context(nc.sbuf_tensor("sb_" + name, shape, dt))

    stile("ident", [128, 128], BF16)
    stile("mask_l", [128, 128]); stile("mask_r", [128, 128])
    stile("triu01", [128, 128], BF16)
    stile("tcq", [128, 1024], BF16); stile("tsq", [128, 1024], BF16)
    stile("tck", [128, 1024], BF16); stile("tsk", [128, 1024], BF16)
    stile("blob", [128, nb])
    stile("qsks", [1, 256])
    stile("ones_col", [128, 1]); stile("ones_col_bf", [128, 1], BF16)
    stile("ones_row", [1, 128]); stile("ones_row_bf", [1, 128], BF16)
    stile("rowbf", [1, 1024], BF16)
    stile("bvbf", [1, 2048], BF16); stile("denbf", [1, 512], BF16)
    stile("qsB", [128, 64]); stile("ksB", [128, 64])
    stile("rn2", [128, 32]); stile("eps24", [128, 1])
    stile("rowstat", [1, 1024]); stile("rowstat2", [1, 512])
    stile("nmax", [128, 16]); stile("nmax2", [128, 16])
    stile("rsum", [128, 16]); stile("rsum2", [128, 16]); stile("rrec", [128, 16])
    stile("scr_a", [128, 512]); stile("scr_b", [128, 512])
    stile("scr_c", [128, 512])
    stile("qf", [128, 2048], BF16); stile("kf", [128, 2048], BF16)
    stile("diag", [128, 1024], BF16)

    NCOLS = 42000
    st.enter_context(nc.sbuf_tensor("arena_pad", [128, 183], F32))
    arena = st.enter_context(nc.sbuf_tensor("arena", [128, NCOLS], F32))
    ar_t = arena[:, :].tensor
    ar_tb = arena[:, :].bitcast(BF16).tensor

    def af32(col, n):
        assert 0 <= col and col + n <= NCOLS, (col, n)
        return bass.AP(ar_t, col, [[NCOLS, 128], [1, n]])

    def abf(col32, e0, n, pbase=0, pn=128):
        assert 0 <= col32 and col32 * 2 + e0 + n <= NCOLS * 2, (col32, e0, n)
        return bass.AP(ar_tb, col32 * 2 + e0 + pbase * NCOLS * 2,
                       [[NCOLS * 2, pn], [1, n]])

    R_S = 0
    R_XBF = 16392
    R_W = R_XBF + 8200
    R_QT = R_W + 5120
    R_KT = R_QT + 4096
    R_VT = R_KT + 4096
    assert R_VT + 4096 <= NCOLS

    def S_(dt_, c0, n):
        return af32(R_S + dt_ * TG + c0, n)

    def XBF_(dt_, c0, n):
        return abf(R_XBF + dt_ * 1025, c0, n)

    def QT_(fs, c0, n, pbase=0, pn=128):
        return abf(R_QT + fs * 1024, c0, n, pbase, pn)

    def KT_(fs, c0, n, pbase=0, pn=128):
        return abf(R_KT + fs * 1024, c0, n, pbase, pn)

    def VT_(tt, c0, n):
        return abf(R_VT + tt * 256, c0, n)

    def AOUT_(pair, c0, n, pbase=0, pn=128):
        return abf(R_XBF + pair * 1024, c0, n, pbase, pn)

    def WSLOT(col32, e0, n):
        return abf(col32, e0, n)

    def AN(e0, n):
        return abf(R_W, e0, n)

    def ATT(e0, n):
        return abf(R_W + 1280, e0, n)

    STG = None

    PS = [st.enter_context(nc.psum_tensor(f"ps{i}", [128, 512], F32)) for i in range(4)]
    PD = [st.enter_context(nc.psum_tensor(f"pd{i}", [128, 1024], F32)) for i in range(2)]
    PSN = ["ps0", "ps1", "ps2", "ps3", "pd0", "pd1"]
    PSAP = {"ps0": PS[0], "ps1": PS[1], "ps2": PS[2], "ps3": PS[3],
            "pd0": PD[0], "pd1": PD[1]}
    pfree = {}

    bidx = h["blob_idx"]

    def bcol(name, i=0):
        return sb["blob"][:, bidx[name] + i: bidx[name] + i + 1]

    P.dma("sync", sb["ident"][:, :], dr["ident"])
    P.dma("sync", sb["mask_l"][:, :], dr["mask_l"])
    P.dma("sync", sb["mask_r"][:, :], dr["mask_r"])
    P.dma("sync", sb["triu01"][:, :], dr["triu01"])
    P.dma("sync", sb["blob"][:, :], dr["blob"])
    P.dma("sync", sb["qsks"][:, :], dr["qsks"])
    cst_tok = P.dma("gpsimd", sb["bvbf"][0:1, :], dr["bvrow"], inc="wd")
    for nm in ("tcq", "tsq", "tck", "tsk"):
        tab_tok = P.dma("gpsimd", sb[nm][:, :], dr[nm], inc="wd")
    xin_toks = [P.dma("sync", S_(dt_, 0, TG), dr["xin"][dt_]) for dt_ in range(8)]
    P.op("vector", lambda e: e.memset(sb["ones_col"][:, :], 1.0))
    P.op("vector", lambda e: e.memset(sb["ones_col_bf"][:, :], 1.0))
    P.op("vector", lambda e: e.memset(sb["ones_row_bf"][:, :], 1.0))
    P.op("vector", lambda e: e.memset(sb["eps24"][:, :], 1e-24))
    ones_tok = P.op("vector", lambda e: e.memset(sb["ones_row"][:, :], 1.0), inc="ve")

    def scrbf(name):
        t = sb[name][:, :].bitcast(BF16).tensor
        return bass.AP(t, 0, [[1024, 128], [1, 512]])

    def emit_ln(shift, mode, deps):
        deps = [d for d in deps if d is not None]
        c0s = [0 if (shift and dt_ >= 4) else 1 for dt_ in range(8)]
        spill_toks = [None] * 8
        if mode == "stream":
            for dt_ in range(8):
                eng = "sync" if dt_ % 2 == 0 else "gpsimd"
                spill_toks[dt_] = P.dma(eng, xo_scr[dt_], S_(dt_, 0, TG),
                                        waits=deps,
                                        inc="sd" if eng == "sync" else "wd")
        xbf_tok = None
        f32_tok = None
        for tc in reversed(range(4)):
            cast_toks = [None] * 8
            for dt_ in range(8):
                if dt_ % 2 == 0:
                    cast_toks[dt_] = P.op("vector", lambda e, dt_=dt_, tc=tc: e.tensor_copy(
                        XBF_(dt_, c0s[dt_] + tc * 512, 512),
                        S_(dt_, c0s[dt_] + tc * 512, 512)),
                        waits=deps, inc="ve")
                else:
                    cast_toks[dt_] = P.op("scalar", lambda e, dt_=dt_, tc=tc: e.activation(
                        XBF_(dt_, c0s[dt_] + tc * 512, 512),
                        S_(dt_, c0s[dt_] + tc * 512, 512), AF.Copy),
                        waits=deps, inc="ac")
            w_ps2 = deps + [pfree.get("ps2")]
            w_ps3 = deps + [pfree.get("ps3")]
            t_sum = None
            t_ssq = None
            for dt_ in range(8):
                sn = "scr_a" if dt_ % 2 == 0 else "scr_b"
                t_sq = P.op("scalar", lambda e, dt_=dt_, tc=tc, sn=sn: e.activation(
                    scrbf(sn), S_(dt_, c0s[dt_] + tc * 512, 512), AF.Square),
                    waits=deps + [pfree.get(sn)], inc="ac")
                t_sum = P.op("tensor", lambda e, dt_=dt_, tc=tc: e.matmul(
                    PS[2][0:1, :], sb["ones_col_bf"][:, :],
                    XBF_(dt_, c0s[dt_] + tc * 512, 512),
                    start=(dt_ == 0), stop=(dt_ == 7)),
                    waits=[cast_toks[dt_]] + (w_ps2 if dt_ == 0 else []),
                    inc=("te" if dt_ == 7 else None))
                t_ssq = P.op("tensor", lambda e, dt_=dt_, sn=sn: e.matmul(
                    PS[3][0:1, :], sb["ones_col_bf"][:, :], scrbf(sn),
                    start=(dt_ == 0), stop=(dt_ == 7)),
                    waits=[t_sq] + (w_ps3 if dt_ == 0 else []), inc="te")
                pfree[sn] = t_ssq
            rs = sb["rowstat"][0:1, 0:512]
            rs2 = sb["rowstat2"][0:1, 0:512]
            t1 = P.op("vector", lambda e, rs2=rs2: e.tensor_scalar(
                rs2, PS[2][0:1, :], -1.0 / D, None, ALU.mult),
                waits=[t_sum, pfree.get("rowmath")], inc="ve")
            pfree["ps2"] = t1
            t2 = P.op("vector", lambda e, rs=rs, rs2=rs2: e.tensor_tensor(
                rs, rs2, rs2, ALU.mult), waits=[t1], inc="ve")
            t3 = P.op("vector", lambda e, rs=rs: e.scalar_tensor_tensor(
                rs, rs, -float(D), PS[3][0:1, :], ALU.mult, ALU.add),
                waits=[t2, t_ssq], inc="ve")
            pfree["ps3"] = t3
            t3b = P.op("vector", lambda e, rs=rs: e.tensor_scalar(
                rs, rs, 1.0 / D, float(EPS_LN), ALU.mult, ALU.add),
                waits=[t3], inc="ve")
            t4 = P.op("scalar", lambda e, rs=rs: e.activation(
                rs, rs, AF.Ln), waits=[t3b], inc="ac")
            t5 = P.op("scalar", lambda e, rs=rs: e.activation(
                rs, rs, AF.Exp, scale=-0.5), waits=[t4], inc="ac")
            if mode == "stream":
                tA = P.op("tensor", lambda e, rs=rs: e.matmul(
                    PD[0][:, 0:512], sb["ones_row"][:, :], rs, start=True, stop=True),
                    waits=[t5, ones_tok, pfree.get("pd0")], inc="te")
                tB = P.op("tensor", lambda e, rs2=rs2: e.matmul(
                    PD[1][:, 0:512], sb["ones_row"][:, :], rs2, start=True, stop=True),
                    waits=[t1, pfree.get("pd1")], inc="te")
            else:
                tbf1 = P.op("vector", lambda e, rs=rs: e.tensor_copy(
                    sb["rowbf"][0:1, 0:512], rs),
                    waits=[t5, pfree.get("rowbf")], inc="ve")
                tbf2 = P.op("vector", lambda e, rs2=rs2: e.tensor_copy(
                    sb["rowbf"][0:1, 512:1024], rs2), waits=[t1], inc="ve")
                tA = P.op("tensor", lambda e: e.matmul(
                    PD[0][:, 0:512], sb["ones_row_bf"][:, :],
                    sb["rowbf"][0:1, 0:512], start=True, stop=True),
                    waits=[tbf1, ones_tok, pfree.get("pd0")], inc="te")
                tB = P.op("tensor", lambda e: e.matmul(
                    PD[1][:, 0:512], sb["ones_row_bf"][:, :],
                    sb["rowbf"][0:1, 512:1024], start=True, stop=True),
                    waits=[tbf2, pfree.get("pd1")], inc="te")
                pfree["rowbf"] = tB
            pfree["rowmath"] = tB
            for dt_ in range(8):
                p1 = P.op("vector", lambda e, dt_=dt_, tc=tc: e.tensor_tensor(
                    sb["scr_b"][:, :], S_(dt_, c0s[dt_] + tc * 512, 512),
                    PD[1][:, 0:512], ALU.add),
                    waits=[tB, pfree.get("scr_b")], inc="ve")
                if mode == "stream":
                    p2 = P.op("vector", lambda e, dt_=dt_, tc=tc: e.tensor_tensor(
                        S_(dt_, 1 + tc * 512, 512), sb["scr_b"][:, :],
                        PD[0][:, 0:512], ALU.mult),
                        waits=[tA, p1, spill_toks[dt_]], inc="ve")
                    f32_tok = p2
                    p2 = P.op("vector", lambda e, dt_=dt_, tc=tc: e.tensor_copy(
                        XBF_(dt_, 1 + tc * 512, 512), S_(dt_, 1 + tc * 512, 512)),
                        waits=[p2], inc="ve")
                elif mode == "bf":
                    p2 = P.op("vector", lambda e, dt_=dt_, tc=tc: e.tensor_tensor(
                        XBF_(dt_, 1 + tc * 512, 512), sb["scr_b"][:, :],
                        PD[0][:, 0:512], ALU.mult),
                        waits=[tA, p1], inc="ve")
                else:
                    l = int(mode.split(":")[1])
                    p2 = P.op("vector", lambda e, dt_=dt_, tc=tc: e.tensor_tensor(
                        sb["scr_c"][:, :], sb["scr_b"][:, :], PD[0][:, 0:512],
                        ALU.mult), waits=[tA, p1, pfree.get("scr_c")], inc="ve")
                    p2 = P.op("vector", lambda e, dt_=dt_, tc=tc, l=l: e.tensor_scalar(
                        XBF_(dt_, 1 + tc * 512, 512), sb["scr_c"][:, :],
                        bcol(f"ln_g{l}", dt_), bcol(f"ln_b{l}", dt_),
                        ALU.mult, ALU.add), waits=[p2], inc="ve")
                    pfree["scr_c"] = p2
                pfree["scr_b"] = p2
                xbf_tok = p2
            pfree["pd0"] = xbf_tok
            pfree["pd1"] = xbf_tok
        if mode == "stream":
            for dt_ in range(8):
                xbf_tok = P.op("vector", lambda e, dt_=dt_: e.tensor_copy(
                    XBF_(dt_, 0, 1), S_(dt_, 0, 1)), waits=deps, inc="ve")
        return f32_tok, xbf_tok

    def emit_f_layer(l, xbf_tok, stream_tok):
        wqk_toks = []
        for kt in range(8):
            wqk_toks.append(P.dma(
                "gpsimd", WSLOT(R_W + kt * 512, 0, 1024),
                dr["wqkv"][l][kt * 128:(kt + 1) * 128, 0:1024],
                waits=[pfree.get("r_w")], inc="wd"))
        wqk_tok = wqk_toks[-1]
        tq1 = P.op("tensor", lambda e: e.matmul(
            PD[0][:, 0:64], sb["ones_row"][:, :],
            sb["qsks"][0:1, 128 * l:128 * l + 64], start=True, stop=True),
            waits=[ones_tok, cst_tok, pfree.get("pd0")], inc="te")
        tq2 = P.op("vector", lambda e: e.tensor_copy(sb["qsB"][:, :], PD[0][:, 0:64]),
                   waits=[tq1], inc="ve")
        tk1 = P.op("tensor", lambda e: e.matmul(
            PD[0][:, 64:128], sb["ones_row"][:, :],
            sb["qsks"][0:1, 128 * l + 64:128 * l + 128], start=True, stop=True),
            waits=[tq2], inc="te")
        tk2 = P.op("vector", lambda e: e.tensor_copy(sb["ksB"][:, :], PD[0][:, 64:128]),
                   waits=[tk1], inc="ve")
        pfree["pd0"] = tk2
        qsb_tok = tk2

        qsw = bass.AP(sb["qsB"][:, :].tensor, 32, [[64, 128], [0, 16], [-32, 2], [1, 32]])
        ksw = bass.AP(sb["ksB"][:, :].tensor, 32, [[64, 128], [0, 16], [-32, 2], [1, 32]])
        qsb_b = bass.AP(sb["qsB"][:, :].tensor, 0, [[64, 128], [0, 16], [1, 64]])
        ksb_b = bass.AP(sb["ksB"][:, :].tensor, 0, [[64, 128], [0, 16], [1, 64]])
        tab_build = None
        for srcnm, mul, tofs, swap in (("tcq", qsb_b, 0, False), ("tsq", qsw, 1024, True),
                                       ("tck", ksb_b, 2048, False), ("tsk", ksw, 3072, True)):
            tcp = P.op("vector", lambda e, srcnm=srcnm, tofs=tofs: e.tensor_copy(
                af32(R_VT + tofs, 1024), sb[srcnm][:, :]),
                waits=[tab_tok, qsb_tok, pfree.get("vtreg")], inc="ve")
            if swap:
                tab_build = P.op("vector", lambda e, mul=mul, tofs=tofs: e.tensor_tensor(
                    bass.AP(ar_t, R_VT + tofs, [[NCOLS, 128], [64, 16], [32, 2], [1, 32]]),
                    bass.AP(ar_t, R_VT + tofs, [[NCOLS, 128], [64, 16], [32, 2], [1, 32]]),
                    mul, ALU.mult), waits=[tcp], inc="ve")
            else:
                tab_build = P.op("vector", lambda e, mul=mul, tofs=tofs: e.tensor_tensor(
                    bass.AP(ar_t, R_VT + tofs, [[NCOLS, 128], [64, 16], [1, 64]]),
                    bass.AP(ar_t, R_VT + tofs, [[NCOLS, 128], [64, 16], [1, 64]]),
                    mul, ALU.mult), waits=[tcp], inc="ve")
        pfree["vtreg"] = None

        tr_evac = None
        last_qk_mm = None
        def halfbf(name, half):
            t = sb[name][:, :].bitcast(BF16).tensor
            return bass.AP(t, 512 * half, [[1024, 128], [1, 512]])

        sq_bufs = [
            (halfbf("scr_a", 0), halfbf("scr_b", 0), halfbf("scr_c", 0),
             "scr_a", "scr_b", "scr_c"),
            (halfbf("scr_a", 1), halfbf("scr_b", 1), halfbf("scr_c", 1),
             "scr_ao", "scr_bo", "scr_co"),
        ]

        def ap3(ap):
            return bass.AP(ap.tensor, ap.offset, [[ap.ap[0][0], 128], [64, 8], [1, 64]])

        for g4 in range(4):
            for ts in range(4):
                tt = g4 * 4 + ts
                par = tt % 2
                bqa, bka, bca, kqa, kka, kca = sq_bufs[par]
                rb0 = 16 * par
                qpsn, kpsn = ("ps0", "ps2") if par == 0 else ("ps1", "ps3")
                qps, kps = PSAP[qpsn], PSAP[kpsn]
                for ci, psn, ps in ((0, qpsn, qps), (1, kpsn, kps)):
                    w0 = [xbf_tok, wqk_tok, pfree.get(psn)]
                    for kt in range(8):
                        c0 = (1 if kt < 4 else 0) + 128 * tt
                        last_qk_mm = P.op("tensor", lambda e, kt=kt, c0=c0, ci=ci, ps=ps: e.matmul(
                            ps[:, :], XBF_(kt, c0, 128),
                            WSLOT(R_W + kt * 512, ci * 512, 512),
                            start=(kt == 0), stop=(kt == 7)),
                            waits=(w0 if kt == 0 else ()),
                            inc=("te" if kt == 7 else None))
                    if ci == 0:
                        qmm = last_qk_mm
                    else:
                        kmm = last_qk_mm
                sqq = P.op("scalar", lambda e, qps=qps, bqa=bqa: e.activation(
                    bqa, qps[:, :], AF.Square),
                    waits=[qmm, pfree.get(kqa)], inc="ac")
                sqk = P.op("scalar", lambda e, kps=kps, bka=bka: e.activation(
                    bka, kps[:, :], AF.Square),
                    waits=[kmm, pfree.get(kka)], inc="ac")
                rq = P.op("vector", lambda e, bqa=bqa, rb0=rb0: e.tensor_reduce(
                    sb["rn2"][:, rb0:rb0 + 8], ap3(bqa),
                    AX.X, ALU.add), waits=[sqq], inc="ve")
                rk = P.op("vector", lambda e, bka=bka, rb0=rb0: e.tensor_reduce(
                    sb["rn2"][:, rb0 + 8:rb0 + 16], ap3(bka),
                    AX.X, ALU.add), waits=[sqk], inc="ve")
                pfree[kqa] = rq
                pfree[kka] = rk
                rr = P.op("scalar", lambda e, rb0=rb0: e.activation(
                    sb["rn2"][:, rb0:rb0 + 16], sb["rn2"][:, rb0:rb0 + 16],
                    AF.Ln, bias=sb["eps24"][:, :]),
                    waits=[rq, rk], inc="ac")
                rr = P.op("scalar", lambda e, rb0=rb0: e.activation(
                    sb["rn2"][:, rb0:rb0 + 16], sb["rn2"][:, rb0:rb0 + 16],
                    AF.Exp, scale=-0.5),
                    waits=[rr], inc="ac")
                outs = []
                for half, tofs, dst, wkey, psx, rnlo in (
                        (0, 0, sb["qf"], "qfw", qps, 0),
                        (1, 2048, sb["kf"], "kfw", kps, 8)):
                    dstap = dst[:, :]
                    t1 = P.op("vector", lambda e, psx=psx, tofs=tofs, tt=tt, bca=bca: e.tensor_tensor(
                        ap3(bca),
                        psx[:, :].rearrange("p (h d) -> p h d", h=8),
                        bass.AP(ar_t, R_VT + tofs + tt * 64, [[NCOLS, 128], [0, 8], [1, 64]]),
                        ALU.mult), waits=[tab_build, pfree.get(kca) if half == 0 else None],
                        inc="ve")
                    t2 = P.op("vector", lambda e, psx=psx, tofs=tofs, tt=tt, ts=ts,
                              dstap=dstap: e.tensor_tensor(
                        bass.AP(dstap.tensor, ts * 512, [[2048, 128], [64, 8], [32, 2], [1, 32]]),
                        bass.AP(psx[:, :].tensor, 32,
                                [[512, 128], [64, 8], [-32, 2], [1, 32]]),
                        bass.AP(ar_t, R_VT + tofs + 1024 + tt * 64,
                                [[NCOLS, 128], [0, 8], [32, 2], [1, 32]]),
                        ALU.mult), waits=[pfree.get(wkey)], inc="ve")
                    t3 = P.op("vector", lambda e, dst=dst, ts=ts, bca=bca: e.tensor_tensor(
                        dst[:, ts * 512:(ts + 1) * 512], dst[:, ts * 512:(ts + 1) * 512],
                        bca, ALU.add), waits=[t1, t2], inc="ve")
                    pfree[psx is qps and qpsn or kpsn] = t3
                    t4 = P.op("vector", lambda e, dst=dst, ts=ts, rnlo=rnlo, rb0=rb0: e.tensor_tensor(
                        dst[:, ts * 512:(ts + 1) * 512].rearrange("p (h d) -> p h d", h=8),
                        dst[:, ts * 512:(ts + 1) * 512].rearrange("p (h d) -> p h d", h=8),
                        sb["rn2"][:, rb0 + rnlo:rb0 + rnlo + 8].broadcast_to((128, 8, DH)),
                        ALU.mult),
                        waits=[t3, rr], inc="ve")
                    outs.append(t4)
                pfree[kca] = outs[1]
                qf_tok, kf_tok = outs
            for srcbuf, dstfn, ready, wkey in ((sb["qf"], QT_, qf_tok, "qfw"),
                                               (sb["kf"], KT_, kf_tok, "kfw")):
                for fs in range(4):
                    tmm = None
                    for ts in range(4):
                        tmm = P.op("tensor", lambda e, fs=fs, ts=ts, srcbuf=srcbuf: e.matmul(
                            PD[1][:, ts * 128:(ts + 1) * 128],
                            srcbuf[:, ts * 512 + fs * 128: ts * 512 + fs * 128 + 128],
                            sb["ident"][:, :], start=(ts == 0), stop=(ts == 3)),
                            waits=([ready, pfree.get("pd1")] if ts == 0 else ()),
                            inc=("te" if ts == 3 else None))
                    tev = P.op("vector", lambda e, fs=fs, g4=g4, dstfn=dstfn: e.tensor_copy(
                        dstfn(fs, g4 * 512, 512), PD[1][:, 0:512]),
                        waits=[tmm], inc="ve")
                    pfree["pd1"] = tev
                pfree[wkey] = tmm
            tr_evac = tev
        qkT_tok = tr_evac
        for k0, k1 in (("scr_a", "scr_ao"), ("scr_b", "scr_bo"),
                       ("scr_c", "scr_co")):
            a, b = pfree.get(k0), pfree.get(k1)
            if a is None or (b is not None and b[1] > a[1]):
                pfree[k0] = b

        wv_toks = []
        for kt in range(8):
            wv_toks.append(P.dma(
                "gpsimd", WSLOT(R_W + kt * 256, 0, 512),
                dr["wqkv"][l][kt * 128:(kt + 1) * 128, 1024:1536],
                waits=[qkT_tok if kt == 0 else None], inc="wd"))
        wv_tok = wv_toks[-1]
        vt_tok = None
        for tt in range(16):
            psn = "ps0" if tt % 2 == 0 else "ps1"
            ps = PSAP[psn]
            w0 = [xbf_tok, wv_tok, pfree.get(psn), qkT_tok]
            vmm = None
            for kt in range(8):
                c0 = (1 if kt < 4 else 0) + 128 * tt
                vmm = P.op("tensor", lambda e, kt=kt, c0=c0, ps=ps: e.matmul(
                    ps[:, :], XBF_(kt, c0, 128), WSLOT(R_W + kt * 256, 0, 512),
                    start=(kt == 0), stop=(kt == 7)),
                    waits=(w0 if kt == 0 else ()), inc=("te" if kt == 7 else None))
            vt_tok = P.op("vector", lambda e, tt=tt, ps=ps: e.tensor_copy(
                VT_(tt, 0, 512), ps[:, :]), waits=[vmm], inc="ve")
            pfree[psn] = vt_tok

        if DBG_F and l == 0:
            for fs in range(4):
                P.dma("gpsimd", ydbg[fs], QT_(fs, 0, 512), waits=[qkT_tok], inc="wd")
                P.dma("gpsimd", ydbg[4 + fs], KT_(fs, 0, 512), waits=[qkT_tok], inc="wd")
            for ttd in range(4):
                P.dma("gpsimd", ydbg[8 + ttd], VT_(ttd, 0, 512), waits=[vt_tok], inc="wd")

        AN_BASE = (R_W, R_W + 1280, R_XBF + 4096, R_XBF + 5376)
        ATT_BASE = (R_XBF + 6656, R_XBF + 7168)

        def ANp(par, e0, n):
            return abf(AN_BASE[par], e0, n)

        def ATTp(par, e0, n):
            return abf(ATT_BASE[par], e0, n)

        ao_tok = None
        prev_ex = [[None] * 4 for _ in range(4)]
        prev_rc = [[None] * 4 for _ in range(4)]
        anfree = [[None] * 4 for _ in range(4)]
        state_attfree = {}
        seq = [(hh, g) for g in range(4) for hh in range(NH)]
        saved = None
        pfree["a_s0"] = pfree["a_s1"] = vt_tok
        pfree["a_s2"] = pfree.get("ps2")
        pfree["a_s3"] = pfree.get("ps3")
        for s in range(4):
            pfree[f"a_x{s}"] = pfree.get("pd0")
        pfree["a_td0"] = pfree.get("pd0")
        pfree["a_td1"] = pfree.get("pd1")
        pfree["a_pv"] = pfree.get("pd1")

        def emit_scores(hh, g, par4):
            pair, rb = hh // 2, 64 * (hh % 2)
            info = {}
            exts = [s for s in range(4) if 512 * g + 128 * s >= WIN]
            for s in range(4):
                qg = 512 * g + 128 * s
                kstart = max(0, qg - WIN)
                klm = min(512, qg + 128 - kstart)
                has_ext = qg >= WIN
                ps = PS[s]
                smm = P.op("tensor", lambda e, pair=pair, rb=rb, qg=qg,
                           kstart=kstart, klm=klm, ps=ps: e.matmul(
                    ps[:, 0:klm], QT_(pair, qg, 128, rb, 64),
                    KT_(pair, kstart, klm, rb, 64), start=True, stop=True),
                    waits=[qkT_tok, pfree.get(f"a_s{s}")], inc="te")
                xmm = None
                if has_ext:
                    first_ext = s == exts[0]
                    xw = ([pfree.get(f"a_x{t}") for t in range(4)]
                          if first_ext else [pfree.get(f"a_x{s}")])
                    xmm = P.op("tensor", lambda e, pair=pair, rb=rb, qg=qg, s=s,
                               st=first_ext, sp=(s == exts[-1]): e.matmul(
                        PD[0][:, 128 * s:128 * s + 128], QT_(pair, qg, 128, rb, 64),
                        KT_(pair, qg, 128, rb, 64), start=st, stop=sp),
                        waits=xw, inc="te")
                info[s] = (qg, kstart, klm, has_ext, smm, xmm, ps)
            return info, exts

        def emit_softmax(hh, g, par4, info, exts, an_toks, an_rdy):
            xmm_last = info[exts[-1]][5] if exts else None
            for s in range(4):
                qg, kstart, klm, has_ext, smm, xmm, ps = info[s]
                if has_ext:
                    cm = P.op("vector", lambda e, ps=ps: e.tensor_tensor(
                        ps[:, 0:128], ps[:, 0:128], sb["mask_l"][:, :], ALU.add),
                        waits=[smm, cst_tok], inc="ve")
                    cmx = P.op("vector", lambda e, s=s: e.tensor_tensor(
                        PD[0][:, 128 * s:128 * s + 128],
                        PD[0][:, 128 * s:128 * s + 128],
                        sb["mask_r"][:, :], ALU.add), waits=[xmm_last], inc="ve")
                else:
                    cm = P.op("vector", lambda e, ps=ps, klm=klm: e.tensor_tensor(
                        ps[:, klm - 128:klm], ps[:, klm - 128:klm],
                        sb["mask_r"][:, :], ALU.add), waits=[smm, cst_tok], inc="ve")
                    cmx = None
                info[s] = info[s] + (cm, cmx)
            nmcs = {}
            for s in range(4):
                qg, kstart, klm, has_ext, smm, xmm, ps, cm, cmx = info[s]
                sc = 4 * par4 + s
                if has_ext:
                    nm1 = P.op("vector", lambda e, ps=ps, sc=sc: e.tensor_reduce(
                        sb["nmax"][:, sc:sc + 1], ps[:, 0:512], AX.X, ALU.max,
                        negate=True), waits=[cm, prev_ex[par4][s]], inc="ve")
                    nm2 = P.op("vector", lambda e, s=s, sc=sc: e.tensor_reduce(
                        sb["nmax2"][:, sc:sc + 1], PD[0][:, 128 * s:128 * s + 128],
                        AX.X, ALU.max, negate=True), waits=[cmx], inc="ve")
                    nmcs[s] = P.op("vector", lambda e, sc=sc: e.tensor_tensor(
                        sb["nmax"][:, sc:sc + 1], sb["nmax"][:, sc:sc + 1],
                        sb["nmax2"][:, sc:sc + 1], ALU.min),
                        waits=[nm1, nm2], inc="ve")
                else:
                    nmcs[s] = P.op("vector", lambda e, ps=ps, klm=klm, sc=sc:
                                   e.tensor_reduce(
                        sb["nmax"][:, sc:sc + 1], ps[:, 0:klm], AX.X, ALU.max,
                        negate=True), waits=[cm, prev_ex[par4][s]], inc="ve")
            ex2s = {}
            for s in range(4):
                qg, kstart, klm, has_ext, smm, xmm, ps, cm, cmx = info[s]
                sc = 4 * par4 + s
                if has_ext:
                    ex1 = P.op("scalar", lambda e, sc=sc, s=s, par4=par4: e.activation(
                        ANp(par4, 640 * s, 512), PS[s][:, :], AF.Exp,
                        bias=sb["nmax"][:, sc:sc + 1], scale=1.0,
                        accum_out=sb["rsum"][:, sc:sc + 1]),
                        waits=[nmcs[s], prev_rc[par4][s], anfree[par4][s]], inc="ac")
                    ex2 = P.op("scalar", lambda e, sc=sc, s=s, par4=par4: e.activation(
                        ANp(par4, 640 * s + 512, 128),
                        PD[0][:, 128 * s:128 * s + 128], AF.Exp,
                        bias=sb["nmax"][:, sc:sc + 1], scale=1.0,
                        accum_out=sb["rsum2"][:, sc:sc + 1]), waits=(), inc="ac")
                    pfree[f"a_s{s}"] = ex1
                    pfree[f"a_x{s}"] = ex2
                    prev_ex[par4][s] = ex2
                    ex2s[s] = (ex1, ex2)
                    an_rdy[s] = ex2
                else:
                    ex1 = P.op("scalar", lambda e, sc=sc, s=s, klm=klm, par4=par4:
                               e.activation(
                        ANp(par4, 640 * s, klm), PS[s][:, 0:klm], AF.Exp,
                        bias=sb["nmax"][:, sc:sc + 1], scale=1.0,
                        accum_out=sb["rsum"][:, sc:sc + 1]),
                        waits=[nmcs[s], prev_rc[par4][s], anfree[par4][s]], inc="ac")
                    pfree[f"a_s{s}"] = ex1
                    prev_ex[par4][s] = ex1
                    ex2s[s] = None
                    an_toks[s] = ex1
                    an_rdy[s] = ex1
            return ex2s

        dgfree = [None, None]

        def emit_vlate(idx, par4, ex2s, an_toks, an_rdy):
            for s in range(4):
                if ex2s[s] is not None:
                    ex1, ex2 = ex2s[s]
                    sc = 4 * par4 + s
                    an_toks[s] = P.op("vector", lambda e, sc=sc: e.tensor_tensor(
                        sb["rsum"][:, sc:sc + 1], sb["rsum"][:, sc:sc + 1],
                        sb["rsum2"][:, sc:sc + 1], ALU.add),
                        waits=[ex1, ex2], inc="ve")
            rc = P.op("vector", lambda e, par4=par4: e.reciprocal(
                sb["rrec"][:, 4 * par4:4 * par4 + 4],
                sb["rsum"][:, 4 * par4:4 * par4 + 4]),
                waits=list(an_toks), inc="ve")
            out = []
            for s in range(4):
                prev_rc[par4][s] = rc
                jj = 4 * (idx % 2) + s
                dg = P.op("vector", lambda e, jj=jj, sc=4 * par4 + s: e.tensor_scalar(
                    sb["diag"][:, 128 * jj:128 * jj + 128], sb["ident"][:, :],
                    sb["rrec"][:, sc:sc + 1], None, ALU.mult),
                    waits=[rc, dgfree[idx % 2]], inc="ve")
                out.append((an_rdy[s], dg))
            return out

        pending_ao = [None]
        ao_by_g = [[] for _ in range(4)]

        def flush_ao():
            nonlocal ao_tok
            if pending_ao[0] is None:
                return
            pair1, rb1, g1, pv_tok = pending_ao[0]
            pending_ao[0] = None
            ao_tok = P.op("vector", lambda e, pair1=pair1, rb1=rb1, g1=g1:
                          e.tensor_copy(
                AOUT_(pair1, 512 * g1, 512, rb1, 64),
                PD[1][0:64, 512:1024]),
                waits=[pv_tok, vt_tok], inc="ve")
            pfree["a_pv"] = ao_tok
            ao_by_g[g1].append(ao_tok)

        def emit_stage_b(idx1, hh1, g1, par1, an1):
            pair1, rb1 = hh1 // 2, 64 * (hh1 % 2)
            qg01 = 512 * g1
            jts = list(range(max(0, 4 * g1 - 4), 4 * g1 + 4))
            if 4 * g1 in jts:
                jts = [4 * g1] + [j for j in jts if j != 4 * g1]
            pv_w0 = [vt_tok, pfree.get("a_pv")]
            pv_tok = None
            last_tmm = None
            attfree = state_attfree
            pend_pv = None

            def emit_pv(args):
                nonlocal pv_tok
                jt, cr0, cr1, so, tev, st0, sp0, slot = args
                pv_tok = P.op("tensor", lambda e, jt=jt, hh1=hh1, cr0=cr0, cr1=cr1,
                              so=so, sl=idx1 % 2, st0=st0, sp0=sp0: e.matmul(
                    PD[1][0:64, 512 + cr0:512 + cr1],
                    VT_(jt, 64 * hh1, 64),
                    ATTp(sl, so + cr0, cr1 - cr0),
                    start=st0, stop=sp0),
                    waits=[tev] + (pv_w0 if st0 else []), inc="te")
                attfree[slot] = pv_tok

            for pos, jt in enumerate(jts):
                ss = []
                for s in range(4):
                    qg = qg01 + 128 * s
                    jstart = max(0, qg - WIN) // 128
                    nkb = (qg + 128 - max(0, qg - WIN)) // 128
                    if jstart <= jt < jstart + nkb:
                        ss.append(s)
                smin, smax = ss[0], ss[-1]
                tdn = "a_td0" if pos % 2 == 0 else "a_td1"
                td = PD[0][:, 512:1024] if pos % 2 == 0 else PD[1][:, 0:512]
                tmm = None
                for i, s in enumerate(ss):
                    qg = qg01 + 128 * s
                    jstart = max(0, qg - WIN) // 128
                    jbl = jt - jstart
                    acol = 640 * s + (512 if (qg >= WIN and jbl == 4) else 128 * jbl)
                    jj = 4 * (idx1 % 2) + s
                    tmm = P.op("tensor", lambda e, s=s, acol=acol, i=i, ns=len(ss),
                               td=td, par1=par1, jj=jj: e.matmul(
                        td[:, 128 * s:128 * s + 128],
                        ANp(par1, acol, 128),
                        sb["diag"][:, 128 * jj:128 * jj + 128],
                        start=(i == 0), stop=(i == ns - 1)),
                        waits=[an1[s][0], an1[s][1]] + ([pfree.get(tdn)] if i == 0 else []),
                        inc=("te" if i == len(ss) - 1 else None))
                last_tmm = tmm
                cr0, cr1 = 128 * smin, 128 * (smax + 1)
                slot = (idx1 % 2, pos % 2)
                tev = P.op("scalar", lambda e, cr0=cr0, cr1=cr1, td=td,
                           so=512 * (pos % 2), sl=idx1 % 2: e.activation(
                    ATTp(sl, so + cr0, cr1 - cr0),
                    td[:, cr0:cr1], AF.Copy),
                    waits=[tmm, attfree.get(slot)], inc="ac")
                pfree[tdn] = tev
                emit_pv((jt, cr0, cr1, 512 * (pos % 2), tev,
                         pos == 0, pos == len(jts) - 1, slot))
            for s in range(4):
                anfree[par1][s] = last_tmm
            dgfree[idx1 % 2] = last_tmm
            pending_ao[0] = (pair1, rb1, g1, pv_tok)

        wo_toks = []
        for kt in range(4):
            wo_toks.append(P.dma("gpsimd", WSLOT(R_W + 2560 + kt * 512, 0, 1024),
                                 dr["woF"][l][kt * 128:(kt + 1) * 128, :],
                                 waits=[last_qk_mm], inc="wd"))
        wo_tok = wo_toks[-1]
        op_tok = None

        def emit_oproj(tc):
            nonlocal op_tok
            aog = ao_by_g[tc][-1]
            for dt_ in range(8):
                bank = f"a_s{dt_ % 4}"
                ps = PS[dt_ % 4]
                omm = None
                for kt in range(4):
                    omm = P.op("tensor", lambda e, kt=kt, dt_=dt_, tc=tc, ps=ps: e.matmul(
                        ps[:, :], WSLOT(R_W + 2560 + kt * 512, 128 * dt_, 128),
                        AOUT_(kt, 512 * tc, 512),
                        start=(kt == 0), stop=(kt == 3)),
                        waits=([aog, wo_tok, pfree.get(bank)] if kt == 0 else ()),
                        inc=("te" if kt == 3 else None))
                op_tok = P.op("vector", lambda e, dt_=dt_, tc=tc, ps=ps: e.tensor_tensor(
                    S_(dt_, 1 + 512 * tc, 512), ps[:, :], S_(dt_, 1 + 512 * tc, 512),
                    ALU.add), waits=[omm, stream_tok], inc="ve")
                pfree[bank] = op_tok

        op_emitted = [False] * 4
        for idx in range(len(seq) + 1):
            if idx < len(seq):
                hh, g = seq[idx]
                par4 = idx % 4
                info, exts = emit_scores(hh, g, par4)
            if saved is not None:
                flush_ao()
                emit_stage_b(*saved)
                saved = None
            if idx < len(seq):
                an_toks = [None] * 4
                an_rdy = [None] * 4
                ex2s = emit_softmax(hh, g, par4, info, exts, an_toks, an_rdy)
                pairs = emit_vlate(idx, par4, ex2s, an_toks, an_rdy)
                saved = (idx, hh, g, par4, pairs)
            for tc in range(4):
                if not op_emitted[tc] and len(ao_by_g[tc]) == 8:
                    emit_oproj(tc)
                    op_emitted[tc] = True
        flush_ao()
        for tc in range(4):
            if not op_emitted[tc] and len(ao_by_g[tc]) == 8:
                emit_oproj(tc)
                op_emitted[tc] = True
        assert all(op_emitted)
        for s in range(4):
            pfree[f"ps{s}"] = pfree.get(f"a_s{s}")
        pfree["pd0"] = P.tok("ac")
        pfree["pd1"] = ao_tok

        _, xln2_tok = emit_ln(True, f"affine:{l}", [op_tok])

        glu_tok = None
        w2_tok = None
        for th in range(2):
            pair_toks = [None, None]
            slot_free = [P.tok("te"), P.tok("te")]
            for p in range(NPAIR + 1):
                if p < NPAIR:
                    slot = p % 2
                    pair_toks[slot] = P.dma(
                        "gpsimd", WSLOT(R_W + slot * 1024, 0, 2048),
                        dr["w1r"][l][p], waits=[slot_free[slot]], inc="wd")
                if p == 0:
                    continue
                pp = p - 1
                slot = pp % 2
                for tci in range(2):
                    tc = 2 * th + tci
                    psa = PSAP["ps0" if tci == 0 else "ps2"]
                    psg = PSAP["ps1" if tci == 0 else "ps3"]
                    psan = "ps0" if tci == 0 else "ps2"
                    psgn = "ps1" if tci == 0 else "ps3"
                    amm = gmm = None
                    for kt in range(8):
                        amm = P.op("tensor", lambda e, kt=kt, tc=tc, slot=slot, psa=psa: e.matmul(
                            psa[:, :], WSLOT(R_W + slot * 1024, 256 * kt, 128),
                            XBF_(kt, 1 + 512 * tc, 512), start=(kt == 0), stop=(kt == 7)),
                            waits=([xln2_tok, pair_toks[slot], pfree.get(psan)]
                                   if kt == 0 else ()),
                            inc=("te" if kt == 7 else None))
                    for kt in range(8):
                        gmm = P.op("tensor", lambda e, kt=kt, tc=tc, slot=slot, psg=psg: e.matmul(
                            psg[:, :], WSLOT(R_W + slot * 1024, 256 * kt + 128, 128),
                            XBF_(kt, 1 + 512 * tc, 512), start=(kt == 0), stop=(kt == 7)),
                            waits=([pfree.get(psgn)] if kt == 0 else ()),
                            inc=("te" if kt == 7 else None))
                    if tci == 1:
                        slot_free[slot] = gmm
                    gg = P.op("scalar", lambda e, pp=pp, l=l, tci=tci, psg=psg: e.activation(
                        sb["scr_a" if tci == 0 else "scr_c"][:, :], psg[:, :], AF.Gelu,
                        bias=bcol(f"b1g{l}", pp), scale=1.0),
                        waits=[gmm, pfree.get("scr_a" if tci == 0 else "scr_c")], inc="ac")
                    glu_tok = P.op("vector", lambda e, pp=pp, l=l, tci=tci, psa=psa: e.scalar_tensor_tensor(
                        abf(R_QT + pp * 512, 512 * tci, 512),
                        psa[:, :], bcol(f"b1a{l}", pp),
                        sb["scr_a" if tci == 0 else "scr_c"][:, :],
                        ALU.add, ALU.mult),
                        waits=[gg, (w2_tok if th == 1 else None)], inc="ve")
                    pfree[psan] = glu_tok
                    pfree[psgn] = glu_tok
                    pfree["scr_a" if tci == 0 else "scr_c"] = glu_tok
            p8 = [PS[0][:, :], PS[1][:, :], PS[2][:, :], PS[3][:, :],
                  PD[0][:, 0:512], PD[0][:, 512:1024], PD[1][:, 0:512],
                  PD[1][:, 512:1024]]
            p8n = ["ps0", "ps1", "ps2", "ps3", "pd0", "pd0", "pd1", "pd1"]
            for tci in range(2):
                tc = 2 * th + tci
                kslot_free = [P.tok("te"), P.tok("te")]
                k_toks = [None, None]
                wmm = None
                for kt in range(NPAIR):
                    slot = kt % 2
                    k_toks[slot] = P.dma(
                        "gpsimd", WSLOT(R_W + 4096 + slot * 512, 0, 1024),
                        dr["w2p"][l][128 * kt:128 * kt + 128, :],
                        waits=[kslot_free[slot],
                               glu_tok if kt == 0 else None], inc="wd")
                    for dt_ in range(8):
                        w0 = [k_toks[slot]]
                        if kt == 0:
                            w0 += [glu_tok] + [pfree.get(n) for n in set(p8n)]
                        wmm = P.op("tensor", lambda e, kt=kt, dt_=dt_, tci=tci,
                                   slot=slot: e.matmul(
                            p8[dt_], WSLOT(R_W + 4096 + slot * 512, 128 * dt_, 128),
                            abf(R_QT + kt * 512, 512 * tci, 512),
                            start=(kt == 0), stop=(kt == NPAIR - 1)),
                            waits=(w0 if dt_ == 0 else ()),
                            inc=("te" if dt_ == 7 else None))
                    kslot_free[slot] = wmm
                for dt_ in range(8):
                    w2_tok = P.op("vector", lambda e, dt_=dt_, tc=tc, l=l:
                                  e.scalar_tensor_tensor(
                        S_(dt_, 1 + 512 * tc, 512), p8[dt_], bcol(f"b2{l}", dt_),
                        S_(dt_, 1 + 512 * tc, 512), ALU.add, ALU.add),
                        waits=[wmm], inc="ve")
                    pfree[p8n[dt_]] = w2_tok
        pfree["r_w"] = w2_tok
        return w2_tok


    deps0 = xin_toks + [cst_tok, ones_tok]
    cur = None

    def emit_ffmod(deps):
        _, xbf_tok = emit_ln(False, "stream", deps)
        stream_tok = xbf_tok
        t = emit_f_layer(0, xbf_tok, stream_tok)
        xb = None
        for dt_ in range(8):
            xb = P.op("vector", lambda e, dt_=dt_: e.tensor_copy(
                XBF_(dt_, 0, TG), S_(dt_, 0, TG)), waits=[t], inc="ve")
        t = emit_f_layer(1, xb, t)
        mt = None
        prev_mt = [None, None]
        mts = [None] * 8
        for dt_ in range(8):
            so = R_W + 2304 * (dt_ % 2)
            ft = P.dma("sync", af32(so, TG), xo_scr[dt_],
                       waits=[t, prev_mt[dt_ % 2]], inc="sd")
            mt = P.op("vector", lambda e, dt_=dt_, so=so: e.scalar_tensor_tensor(
                S_(dt_, 0, TG), S_(dt_, 0, TG), 0.5, af32(so, TG),
                ALU.mult, ALU.add), waits=[ft, t], inc="ve")
            prev_mt[dt_ % 2] = mt
            mts[dt_] = mt
        return mt, mts

    def emit_A(deps):
        _, xbf_tok = emit_ln(False, "bf", deps)
        def proj_w_load(name, tok0):
            toks = []
            for kt in range(8):
                dst = (WSLOT(R_W + kt * 1024, 0, 2048) if kt < 5 else
                       WSLOT(R_QT + (kt - 5) * 1024, 0, 2048))
                toks.append(P.dma("gpsimd", dst, dr[name][kt * 128:(kt + 1) * 128, :],
                                  waits=[tok0], inc="wd"))
            return toks[-1], [
                (WSLOT(R_W + kt * 1024, 0, 2048) if kt < 5 else
                 WSLOT(R_QT + (kt - 5) * 1024, 0, 2048)) for kt in range(8)]

        last_mm = None
        spill_toks = []
        for which, wname, scr, bias in (("q", "wq", q_scr, "bq"), ("k", "wk", k_scr, "bk")):
            wtok, wslots = proj_w_load(wname, xbf_tok if which == "q" else last_mm)
            for fc in range(16):
                stg = sb["qf"] if which == "q" else sb["kf"]
                ev = None
                for tc in range(4):
                    psn = PSN[tc % 2]
                    ps = PSAP[psn]
                    mm = None
                    for kt in range(8):
                        mm = P.op("tensor", lambda e, kt=kt, fc=fc, tc=tc, ps=ps,
                                  wslots=wslots: e.matmul(
                            ps[:, :],
                            bass.AP(wslots[kt].tensor, wslots[kt].offset + 128 * fc,
                                    [[NCOLS * 2, 128], [1, 128]]),
                            XBF_(kt, 1 + 512 * tc, 512),
                            start=(kt == 0), stop=(kt == 7)),
                            waits=([wtok, xbf_tok, pfree.get(psn)] if kt == 0 else ()),
                            inc=("te" if kt == 7 else None))
                    last_mm = mm
                    ee = P.op("scalar", lambda e, fc=fc, ps=ps, bias=bias: e.activation(
                        sb["scr_a"][:, :], ps[:, :], AF.Exp, bias=bcol(bias, fc)),
                        waits=[mm, pfree.get("scr_a")], inc="ac")
                    rr = P.op("scalar", lambda e, fc=fc, ps=ps, bias=bias: e.activation(
                        sb["scr_b"][:, :], ps[:, :], AF.Relu, bias=bcol(bias, fc)),
                        waits=[ee, pfree.get("scr_b")], inc="ac")
                    ev = P.op("vector", lambda e, tc=tc, stg=stg: e.scalar_tensor_tensor(
                        stg[:, 512 * tc:512 * tc + 512], sb["scr_a"][:, :], 1.0,
                        sb["scr_b"][:, :], ALU.min, ALU.add),
                        waits=[rr, spill_toks[-1] if spill_toks else None],
                        inc="ve")
                    pfree[psn] = ev
                    pfree["scr_a"] = ev
                    pfree["scr_b"] = ev
                if which == "q":
                    spill_toks.append(P.dma("sync", q_scr[fc], stg[:, :], waits=[ev]))
                else:
                    dst = bass.AP(k_scr.tensor, fc * 16384,
                                  [[128, 128], [262144, 16], [1, 128]])
                    src = stg[:, :].rearrange("p (j c) -> p j c", j=16)
                    spill_toks.append(P.dma("sync", dst, src, waits=[ev]))
        wtok, wslots = proj_w_load("wv", last_mm)
        for tt in range(16):
            ev = None
            for ec in range(4):
                psn = PSN[ec % 2]
                ps = PSAP[psn]
                mm = None
                for kt in range(8):
                    mm = P.op("tensor", lambda e, kt=kt, tt=tt, ec=ec, ps=ps,
                              wslots=wslots: e.matmul(
                        ps[:, :], XBF_(kt, 1 + 128 * tt, 128),
                        bass.AP(wslots[kt].tensor, wslots[kt].offset + 512 * ec,
                                [[NCOLS * 2, 128], [1, 512]]),
                        start=(kt == 0), stop=(kt == 7)),
                        waits=([wtok, pfree.get(psn)] if kt == 0 else ()),
                        inc=("te" if kt == 7 else None))
                last_mm = mm
                ev = P.op("vector", lambda e, ec=ec, ps=ps: e.tensor_copy(
                    sb["qf"][:, 512 * ec:512 * ec + 512], ps[:, :]),
                    waits=[mm, spill_toks[-1] if ec == 0 else None], inc="ve")
                pfree[psn] = ev
            dst = bass.AP(v_scr.tensor, tt * 16384, [[128, 128], [262144, 16], [1, 128]])
            src = sb["qf"][:, :].rearrange("p (b c) -> p b c", b=16)
            spill_toks.append(P.dma("sync", dst, src, waits=[ev]))
        proj_done = spill_toks[-1]
        if DBG_ATT:
            P.dma("gpsimd", ydbg[47].rearrange("p (j c) -> p j c", j=4),
                  bass.AP(v_scr.tensor, 5 * 262144, [[128, 128], [16384, 4], [1, 128]]),
                  waits=[proj_done], inc="wd")

        wa_slots = []
        wa_toks = []
        for kt in range(16):
            col = (R_W + kt * 512) if kt < 10 else (R_XBF + 4096 + (kt - 10) * 512)
            wa_slots.append(col)
            wa_toks.append(P.dma("gpsimd", WSLOT(col, 0, 1024),
                                 dr["woA"][kt * 128:(kt + 1) * 128, :],
                                 waits=[last_mm if kt == 0 else None], inc="wd"))
        wa_tok = wa_toks[-1]

        at_tok = None
        kb_free = [None, None]
        vb_free = [None, None]
        qb_free = None
        sT_free = None
        for Tq in range(4):
            qg = 512 * Tq
            J = 4 * Tq + 4
            qb_toks = []
            for fc in range(16):
                qb_toks.append(P.dma("sync",
                                     abf(R_QT + fc * 256, 0, 512),
                                     q_scr[fc][:, qg:qg + 512],
                                     waits=[proj_done, qb_free], inc="sd"))
            qb_tok = qb_toks[-1]
            den_mm = None
            sT_toks = [None] * J
            for jt in range(J):
                slot = jt % 2
                kb = P.dma("sync",
                           bass.AP(ar_tb, (R_VT + 2048 + slot * 1024) * 2,
                                   [[NCOLS * 2, 128], [128, 16], [1, 128]]),
                           bass.AP(k_scr.tensor, jt * 262144,
                                   [[128, 128], [16384, 16], [1, 128]]),
                           waits=[proj_done, kb_free[slot]], inc="sd")
                cb = jt - 4 * Tq
                c_lo = 128 * cb if cb >= 0 else 0
                pdn = "pd0" if jt % 2 == 0 else "pd1"
                pd = PSAP[pdn]
                mm = None
                for kt in range(16):
                    mm = P.op("tensor", lambda e, kt=kt, slot=slot, c_lo=c_lo, pd=pd: e.matmul(
                        pd[:, c_lo:512],
                        abf(R_VT + 2048 + slot * 1024, 128 * kt, 128),
                        abf(R_QT + kt * 256, c_lo, 512 - c_lo),
                        start=(kt == 0), stop=(kt == 15)),
                        waits=([kb, qb_tok, pfree.get(pdn), sT_free] if kt == 0 else ()),
                        inc=("te" if kt == 15 else None))
                kb_free[slot] = mm
                tev = P.op("vector", lambda e, jt=jt, c_lo=c_lo, pd=pd: e.tensor_copy(
                    abf(R_KT + jt * 256, c_lo, 512 - c_lo), pd[:, c_lo:512]),
                    waits=[mm], inc="ve")
                pfree[pdn] = tev
                if cb >= 0:
                    tev = P.op("vector", lambda e, jt=jt, cb=cb: e.tensor_tensor(
                        abf(R_KT + jt * 256, 128 * cb, 128),
                        abf(R_KT + jt * 256, 128 * cb, 128),
                        sb["triu01"][:, :], ALU.mult), waits=[tev], inc="ve")
                sT_toks[jt] = tev
                if DBG_ATT and Tq == 0:
                    P.dma("gpsimd", ydbg[jt], abf(R_KT + jt * 256, 0, 512),
                          waits=[tev], inc="wd")
                den_mm = P.op("tensor", lambda e, jt=jt, c_lo=c_lo: e.matmul(
                    PS[2][0:1, c_lo:512], sb["ones_col_bf"][:, :],
                    abf(R_KT + jt * 256, c_lo, 512 - c_lo),
                    start=(jt == 0), stop=False),
                    waits=[tev] + ([pfree.get("ps2")] if jt == 0 else []), inc="te")
            epsrow = P.op("vector", lambda e: e.memset(
                sb["rowbf"][0:1, 0:512], float(EPS_LA)),
                waits=[pfree.get("rowbf")], inc="ve")
            den_mm = P.op("tensor", lambda e: e.matmul(
                PS[2][0:1, 0:512], sb["ones_col_bf"][0:1, 0:1],
                sb["rowbf"][0:1, 0:512], start=False, stop=True),
                waits=[den_mm, epsrow], inc="te")
            pfree["rowbf"] = den_mm
            dsb = P.op("vector", lambda e: e.tensor_copy(
                sb["rowstat"][0:1, 0:512], PS[2][0:1, :]), waits=[den_mm], inc="ve")
            denbf_tok = P.op("vector", lambda e: e.tensor_copy(
                sb["denbf"][0:1, :], PS[2][0:1, :]), waits=[den_mm], inc="ve")
            pfree["ps2"] = denbf_tok
            dr2 = P.op("vector", lambda e: e.reciprocal(
                sb["rowstat"][0:1, 512:1024], sb["rowstat"][0:1, 0:512]),
                waits=[dsb], inc="ve")
            bc = P.op("tensor", lambda e: e.matmul(
                PD[0][:, 0:512], sb["ones_row"][:, :], sb["rowstat"][0:1, 512:1024],
                start=True, stop=True), waits=[dr2, pfree.get("pd0")], inc="te")
            rB = P.op("vector", lambda e: e.tensor_copy(
                sb["scr_c"][:, :], PD[0][:, 0:512]), waits=[bc, pfree.get("scr_c")],
                inc="ve")
            if DBG_ATT and Tq == 0:
                P.dma("sync", ydbg[24][0:1, :], sb["rowstat"][0:1, 0:512], waits=[dsb], inc="sd"); P.dma("sync", ydbg[24][1:2, :], sb["rowstat"][0:1, 512:1024], waits=[dr2], inc="sd")
                P.dma("sync", ydbg[45], sb["scr_c"][:, :], waits=[rB], inc="sd")
                for fcd in range(16):
                    P.dma("gpsimd", ydbg[25 + fcd], abf(R_QT + fcd * 256, 0, 512),
                          waits=[qb_tok], inc="wd")
            pfree["pd0"] = rB
            nu_tok = None
            for ec in range(16):
                slot = ec % 2
                vb = P.dma("sync",
                           bass.AP(ar_tb, (R_VT + slot * 1024) * 2,
                                   [[NCOLS * 2, 128], [128, J], [1, 128]]),
                           bass.AP(v_scr.tensor, ec * 262144,
                                   [[128, 128], [16384, J], [1, 128]]),
                           waits=[proj_done, vb_free[slot]], inc="sd")
                if DBG_ATT and Tq == 0 and ec == 5:
                    P.dma("gpsimd", ydbg[46],
                          abf(R_VT + slot * 1024, 0, 512), waits=[vb], inc="wd")
                psn = "ps0" if ec % 2 == 0 else "ps1"
                ps = PSAP[psn]
                mm = None
                for jt in range(J):
                    cb = jt - 4 * Tq
                    c_lo = 128 * cb if cb >= 0 else 0
                    mm = P.op("tensor", lambda e, jt=jt, slot=slot, c_lo=c_lo, ps=ps: e.matmul(
                        ps[:, c_lo:512], abf(R_VT + slot * 1024, 128 * jt, 128),
                        abf(R_KT + jt * 256, c_lo, 512 - c_lo),
                        start=(jt == 0), stop=False),
                        waits=([vb, sT_toks[J - 1], pfree.get(psn)] if jt == 0 else ()),
                        inc=None)
                mm = P.op("tensor", lambda e, ec=ec, ps=ps: e.matmul(
                    ps[:, 0:512], sb["bvbf"][0:1, 128 * ec:128 * ec + 128],
                    sb["denbf"][0:1, 0:512], start=False, stop=True),
                    waits=[denbf_tok], inc="te")
                vb_free[slot] = mm
                nu_tok = P.op("vector", lambda e, ec=ec, ps=ps: e.tensor_tensor(
                    abf(R_XBF + ec * 256, 0, 512), ps[:, :], sb["scr_c"][:, :],
                    ALU.mult), waits=[mm, rB], inc="ve")
                pfree[psn] = nu_tok
                if DBG_ATT and Tq == 0:
                    P.dma("gpsimd", ydbg[8 + ec], abf(R_XBF + ec * 256, 0, 512),
                          waits=[nu_tok], inc="wd")
            pfree["scr_c"] = nu_tok
            qb_free = nu_tok
            sT_free = nu_tok
            oo = None
            for dt_ in range(8):
                psn = "ps2" if dt_ % 2 == 0 else "ps3"
                ps = PSAP[psn]
                mm = None
                for kt in range(16):
                    mm = P.op("tensor", lambda e, kt=kt, dt_=dt_, ps=ps: e.matmul(
                        ps[:, :], WSLOT(wa_slots[kt], 128 * dt_, 128),
                        abf(R_XBF + kt * 256, 0, 512),
                        start=(kt == 0), stop=(kt == 15)),
                        waits=([nu_tok, wa_tok, pfree.get(psn)] if kt == 0 else ()),
                        inc=("te" if kt == 15 else None))
                oo = P.op("vector", lambda e, dt_=dt_, qg=qg, ps=ps: e.scalar_tensor_tensor(
                    S_(dt_, 1 + qg, 512), ps[:, :], bcol("bo", dt_),
                    S_(dt_, 1 + qg, 512), ALU.add, ALU.add), waits=[mm], inc="ve")
                pfree[psn] = oo
            at_tok = oo
        pfree["r_w"] = at_tok
        return at_tok

    cur = deps0
    tok = None
    if "ln0" in parts:
        _, tok = emit_ln(False, "stream", cur)
        cur = [tok]
    elif "ffl0" in parts:
        _, xbf_tok = emit_ln(False, "stream", cur)
        tok = emit_f_layer(0, xbf_tok, xbf_tok)
        cur = [tok]
    mts = None
    if "ff1" in parts:
        tok, mts = emit_ffmod(cur)
        cur = [tok]
    if "att" in parts:
        tok = emit_A(cur)
        cur = [tok]
        mts = None
    if "ff2" in parts:
        tok, mts = emit_ffmod(cur)
        cur = [tok]

    out_toks = []
    for dt_ in range(8):
        w = [mts[dt_]] if mts is not None else cur
        eng = "sync" if dt_ % 2 == 0 else "gpsimd"
        out_toks.append(P.dma(eng, y[dt_], S_(dt_, 1, T), waits=w,
                              inc="sd" if eng == "sync" else "wd"))
    P.op("sync", lambda e: e.nop(), waits=out_toks)
    P.op("gpsimd", lambda e: e.nop(), waits=out_toks)

    P.finalize(st)
    st.close()
    return nc


_CACHE = {}


def kernel(**inputs):
    h = host_prepare(inputs)
    key = "full"
    if key not in _CACHE:
        _CACHE[key] = build_program(h)
    nc = _CACHE[key]
    common = dict(
        wq=h["wq"], wk=h["wk"], wv=h["wv"], woA=h["woA"],
        wqkv=h["wqkv"], woF=h["woF"], w1r=h["w1r"], w2p=h["w2p"],
        blob=h["blob"], qsks=h["qsks"], bvrow=h["bvrow"],
        tcq=h["tcq"], tsq=h["tsq"], tck=h["tck"], tsk=h["tsk"],
        mask_l=h["mask_l"], mask_r=h["mask_r"],
        triu01=h["triu01"],
        ident=h["ident"])
    in_maps = [dict(common, xin=h["x_cores"][b]) for b in range(NCORES)]
    res = run_bass_kernel_spmd(nc, in_maps, core_ids=list(range(NCORES)))
    out = np.empty((NCORES, T, D), dtype=np.float32)
    for b in range(NCORES):
        yb = res.results[b]["y"]
        out[b] = yb.reshape(D, T).T
    return out


# version 42
# speedup vs baseline: 1.0002x; 1.0002x over previous
import numpy as np
import ml_dtypes
from contextlib import ExitStack

import concourse.bass as bass
import concourse.mybir as mybir
from concourse.bass_utils import run_bass_kernel_spmd

F32 = mybir.dt.float32
BF16 = mybir.dt.bfloat16
AF = mybir.ActivationFunctionType
ALU = mybir.AluOpType
AX = mybir.AxisListType

NCORES = 8
T = 2048
TG = T + 1
D = 1024
DI = 2048
NH = 8
DH = 64
WIN = 512
HID = 2730
NPAIR = 22
HIDP = NPAIR * 128
EPS_LN = 1e-5
EPS_LA = 1e-6
BIG = 3.0e38

ENGINES = ("sync", "gpsimd", "tensor", "vector", "scalar")


class Prog:
    def __init__(self, nc):
        self.nc = nc
        self.stream = {e: [] for e in ENGINES}
        self.semh = {}
        self.semc = {}
        self.last_wait = {}

    def sem(self, name, stack):
        self.semh[name] = stack.enter_context(self.nc.semaphore(name))
        self.semc[name] = 0

    def tok(self, name):
        return (name, self.semc[name])

    def op(self, eng, build, waits=(), inc=None, inck=1):
        ws = []
        for t in waits:
            if t is None:
                continue
            s, v = t
            if v <= 0:
                continue
            key = (eng, s)
            if self.last_wait.get(key, -1) >= v:
                continue
            self.last_wait[key] = v
            ws.append((s, v))
        token = None
        if inc is not None:
            self.semc[inc] += inck
            token = (inc, self.semc[inc])
        semh = self.semh

        def f(e, ws=tuple(ws), build=build, inc=inc, inck=inck):
            for s, v in ws:
                e.wait_ge(semh[s], v)
            inst = build(e)
            if inc is not None:
                inst.then_inc(semh[inc], inck)
        self.stream[eng].append(f)
        return token

    def dma(self, eng, out, in_, waits=(), inc="sd"):
        return self.op(eng, lambda e: e.dma_start(out=out, in_=in_), waits=waits,
                       inc=inc, inck=16)

    def finalize(self, stack):
        blk = stack.enter_context(self.nc.Block())
        streams = self.stream

        @blk.sync
        def _(e):
            for f in streams["sync"]:
                f(e)

        @blk.gpsimd
        def _(e):
            for f in streams["gpsimd"]:
                f(e)

        @blk.tensor
        def _(e):
            for f in streams["tensor"]:
                f(e)

        @blk.vector
        def _(e):
            for f in streams["vector"]:
                f(e)

        @blk.scalar
        def _(e):
            for f in streams["scalar"]:
                f(e)


def build_constants():
    c = {}
    half = np.arange(0, DH, 2, dtype=np.float64)
    inv_freq = 1.0 / 10000.0 ** (half / DH)
    t = np.arange(T, dtype=np.float64)
    fr = t[:, None] * inv_freq[None, :]
    fr = np.concatenate([fr, fr], axis=1)
    base = (half + 0.4 * DH) / (1.4 * DH)
    power = (t - T // 2) / float(WIN // 2)
    sc = base[None, :] ** power[:, None]
    sc = np.concatenate([sc, sc], axis=1)
    sign = np.where(np.arange(DH) < 32, -1.0, 1.0)[None, :]
    cq = 8.0 * np.cos(fr) * sc
    sq = 8.0 * np.sin(fr) * sc * sign
    ck = np.cos(fr) / sc
    sk = np.sin(fr) / sc * sign

    def tmajor(a):
        return np.ascontiguousarray(
            a.reshape(16, 128, DH).transpose(1, 0, 2).reshape(128, 16 * DH)
        ).astype(np.float32)

    c["tcq"], c["tsq"], c["tck"], c["tsk"] = map(tmajor, (cq, sq, ck, sk))
    r = np.arange(128)[:, None]
    cc = np.arange(128)[None, :]
    c["mask_l"] = np.where(cc < r, -BIG, 0.0).astype(np.float32)
    c["mask_r"] = np.where(cc > r, -BIG, 0.0).astype(np.float32)
    c["mask_lr"] = np.concatenate([c["mask_l"], c["mask_r"]], axis=1)
    c["triu01"] = (cc >= r).astype(ml_dtypes.bfloat16)
    c["ident"] = np.eye(128, dtype=ml_dtypes.bfloat16)
    return c


def build_vecblob(w):
    idx = {}
    cols = []

    def put(name, cols_list):
        idx[name] = len(cols)
        cols.extend(cols_list)

    def chunks(v, n):
        v = np.asarray(v, dtype=np.float32).ravel()
        v = np.pad(v, (0, n * 128 - v.size))
        return [v[i * 128:(i + 1) * 128] for i in range(n)]

    put("bq", chunks(w["L0_attn_bq"], 16))
    put("bk", chunks(w["L0_attn_bk"], 16))
    put("bo", chunks(w["L0_attn_bo"], 8))
    for l in range(2):
        put(f"ln_g{l}", chunks(w["L0_ffn_ln_g"][l], 8))
        put(f"ln_b{l}", chunks(w["L0_ffn_ln_b"][l], 8))
        b1 = np.asarray(w["L0_ffn_b1"][l], dtype=np.float32)
        put(f"b1a{l}", chunks(np.pad(b1[:HID], (0, HIDP - HID)), NPAIR))
        put(f"b1g{l}", chunks(np.pad(b1[HID:], (0, HIDP - HID)), NPAIR))
        put(f"b2{l}", chunks(w["L0_ffn_b2"][l], 8))
    blob = np.stack(cols, axis=1).astype(np.float32)
    return blob, idx


def host_prepare(inputs):
    h = {}
    x = np.asarray(inputs["L0_attn_x"], dtype=np.float32)
    xs = []
    for b in range(NCORES):
        xt = np.zeros((D, TG), dtype=np.float32)
        xt[:, 1:] = x[b].T
        xs.append(np.ascontiguousarray(xt.reshape(8, 128, TG)))
    h["x_cores"] = xs

    h["wq"] = np.ascontiguousarray(inputs["L0_attn_Wq"], dtype=np.float32)
    h["wk"] = np.ascontiguousarray(inputs["L0_attn_Wk"], dtype=np.float32)
    h["wv"] = np.ascontiguousarray(inputs["L0_attn_Wv"], dtype=np.float32)
    h["woA"] = np.ascontiguousarray(inputs["L0_attn_Wo"], dtype=np.float32)
    h["wqkv"] = np.ascontiguousarray(inputs["L0_ffn_wqkv"], dtype=np.float32)
    h["woF"] = np.ascontiguousarray(inputs["L0_ffn_wo"], dtype=np.float32)

    w1 = np.asarray(inputs["L0_ffn_w1"], dtype=np.float32)
    w1r = np.zeros((2, NPAIR, 128, 2048), np.float32)
    for l in range(2):
        w1a = np.zeros((D, HIDP), np.float32)
        w1g = np.zeros((D, HIDP), np.float32)
        w1a[:, :HID] = w1[l][:, :HID]
        w1g[:, :HID] = w1[l][:, HID:]
        for p in range(NPAIR):
            blk = np.concatenate([w1a[:, 128 * p:128 * p + 128],
                                  w1g[:, 128 * p:128 * p + 128]], axis=1)
            w1r[l, p] = blk.reshape(8, 128, 256).transpose(1, 0, 2).reshape(128, 2048)
    h["w1r"] = np.ascontiguousarray(w1r)

    w2 = np.asarray(inputs["L0_ffn_w2"], dtype=np.float32)
    w2p = np.zeros((2, HIDP, D), np.float32)
    w2p[:, :HID, :] = w2
    h["w2p"] = np.ascontiguousarray(w2p)

    h["qsks"] = np.concatenate([
        np.asarray(inputs["L0_ffn_q_scale"], np.float32),
        np.asarray(inputs["L0_ffn_k_scale"], np.float32)], axis=1).reshape(1, 256)
    h["bvrow"] = np.asarray(inputs["L0_attn_bv"], np.float32).reshape(1, 2048)

    h["blob"], h["blob_idx"] = build_vecblob(inputs)
    h.update(build_constants())
    return h


DBG_ATT = False
DBG_F = False


def build_program(h, parts=("ff1", "att", "ff2")):
    nc = bass.Bass("TRN2", target_bir_lowering=False)
    P = Prog(nc)
    st = ExitStack()

    dr = {}

    def din(name, shape, dt=F32):
        dr[name] = nc.dram_tensor(name, shape, dt, kind="ExternalInput").ap()

    din("xin", [8, 128, TG])
    din("wq", [D, DI]); din("wk", [D, DI]); din("wv", [D, DI]); din("woA", [DI, D])
    din("wqkv", [2, D, 1536]); din("woF", [2, 512, D])
    din("w1r", [2, NPAIR, 128, 2048]); din("w2p", [2, HIDP, D])
    nb = h["blob"].shape[1]
    din("blob", [128, nb])
    din("qsks", [1, 256]); din("bvrow", [1, 2048])
    din("tcq", [128, 1024]); din("tsq", [128, 1024])
    din("tck", [128, 1024]); din("tsk", [128, 1024])
    din("mask_l", [128, 128]); din("mask_r", [128, 128])
    din("triu01", [128, 128], BF16); din("ident", [128, 128], BF16)
    y = nc.dram_tensor("y", [8, 128, T], F32, kind="ExternalOutput").ap()
    ydbg = (nc.dram_tensor("ydbg", [48, 128, 512], F32, kind="ExternalOutput").ap()
            if (DBG_ATT or DBG_F) else None)
    xo_scr = nc.dram_tensor("xo_scr", [8, 128, TG], F32, kind="Internal").ap()
    q_scr = nc.dram_tensor("q_scr", [16, 128, T], BF16, kind="Internal").ap()
    k_scr = nc.dram_tensor("k_scr", [16, 16, 128, 128], BF16, kind="Internal").ap()
    v_scr = nc.dram_tensor("v_scr", [16, 16, 128, 128], BF16, kind="Internal").ap()

    for s in ("wd", "sd", "te", "ve", "ac", "ge"):
        P.sem(s, st)

    sb = {}

    def stile(name, shape, dt=F32):
        sb[name] = st.enter_context(nc.sbuf_tensor("sb_" + name, shape, dt))

    stile("ident", [128, 128], BF16)
    stile("mask_l", [128, 128]); stile("mask_r", [128, 128])
    stile("triu01", [128, 128], BF16)
    stile("tcq", [128, 1024], BF16); stile("tsq", [128, 1024], BF16)
    stile("tck", [128, 1024], BF16); stile("tsk", [128, 1024], BF16)
    stile("blob", [128, nb])
    stile("qsks", [1, 256])
    stile("ones_col", [128, 1]); stile("ones_col_bf", [128, 1], BF16)
    stile("ones_row", [1, 128]); stile("ones_row_bf", [1, 128], BF16)
    stile("rowbf", [1, 1024], BF16)
    stile("bvbf", [1, 2048], BF16); stile("denbf", [1, 512], BF16)
    stile("qsB", [128, 64]); stile("ksB", [128, 64])
    stile("rn2", [128, 32]); stile("eps24", [128, 1])
    stile("rowstat", [1, 1024]); stile("rowstat2", [1, 512])
    stile("nmax", [128, 16]); stile("nmax2", [128, 16])
    stile("rsum", [128, 16]); stile("rsum2", [128, 16]); stile("rrec", [128, 16])
    stile("scr_a", [128, 512]); stile("scr_b", [128, 512])
    stile("scr_c", [128, 512])
    stile("qf", [128, 2048], BF16); stile("kf", [128, 2048], BF16)
    stile("diag", [128, 1024], BF16)

    NCOLS = 42000
    st.enter_context(nc.sbuf_tensor("arena_pad", [128, 183], F32))
    arena = st.enter_context(nc.sbuf_tensor("arena", [128, NCOLS], F32))
    ar_t = arena[:, :].tensor
    ar_tb = arena[:, :].bitcast(BF16).tensor

    def af32(col, n):
        assert 0 <= col and col + n <= NCOLS, (col, n)
        return bass.AP(ar_t, col, [[NCOLS, 128], [1, n]])

    def abf(col32, e0, n, pbase=0, pn=128):
        assert 0 <= col32 and col32 * 2 + e0 + n <= NCOLS * 2, (col32, e0, n)
        return bass.AP(ar_tb, col32 * 2 + e0 + pbase * NCOLS * 2,
                       [[NCOLS * 2, pn], [1, n]])

    R_S = 0
    R_XBF = 16392
    R_W = R_XBF + 8200
    R_QT = R_W + 5120
    R_KT = R_QT + 4096
    R_VT = R_KT + 4096
    assert R_VT + 4096 <= NCOLS

    def S_(dt_, c0, n):
        return af32(R_S + dt_ * TG + c0, n)

    def XBF_(dt_, c0, n):
        return abf(R_XBF + dt_ * 1025, c0, n)

    def QT_(fs, c0, n, pbase=0, pn=128):
        return abf(R_QT + fs * 1024, c0, n, pbase, pn)

    def KT_(fs, c0, n, pbase=0, pn=128):
        return abf(R_KT + fs * 1024, c0, n, pbase, pn)

    def VT_(tt, c0, n):
        return abf(R_VT + tt * 256, c0, n)

    def AOUT_(pair, c0, n, pbase=0, pn=128):
        return abf(R_XBF + pair * 1024, c0, n, pbase, pn)

    def WSLOT(col32, e0, n):
        return abf(col32, e0, n)

    def AN(e0, n):
        return abf(R_W, e0, n)

    def ATT(e0, n):
        return abf(R_W + 1280, e0, n)

    STG = None

    PS = [st.enter_context(nc.psum_tensor(f"ps{i}", [128, 512], F32)) for i in range(4)]
    PD = [st.enter_context(nc.psum_tensor(f"pd{i}", [128, 1024], F32)) for i in range(2)]
    PSN = ["ps0", "ps1", "ps2", "ps3", "pd0", "pd1"]
    PSAP = {"ps0": PS[0], "ps1": PS[1], "ps2": PS[2], "ps3": PS[3],
            "pd0": PD[0], "pd1": PD[1]}
    pfree = {}

    bidx = h["blob_idx"]

    def bcol(name, i=0):
        return sb["blob"][:, bidx[name] + i: bidx[name] + i + 1]

    P.dma("sync", sb["ident"][:, :], dr["ident"])
    P.dma("sync", sb["mask_l"][:, :], dr["mask_l"])
    P.dma("sync", sb["mask_r"][:, :], dr["mask_r"])
    P.dma("sync", sb["triu01"][:, :], dr["triu01"])
    P.dma("sync", sb["blob"][:, :], dr["blob"])
    P.dma("sync", sb["qsks"][:, :], dr["qsks"])
    cst_tok = P.dma("gpsimd", sb["bvbf"][0:1, :], dr["bvrow"], inc="wd")
    for nm in ("tcq", "tsq", "tck", "tsk"):
        tab_tok = P.dma("gpsimd", sb[nm][:, :], dr[nm], inc="wd")
    xin_toks = [P.dma("sync", S_(dt_, 0, TG), dr["xin"][dt_]) for dt_ in range(8)]
    P.op("vector", lambda e: e.memset(sb["ones_col"][:, :], 1.0))
    P.op("vector", lambda e: e.memset(sb["ones_col_bf"][:, :], 1.0))
    P.op("vector", lambda e: e.memset(sb["ones_row_bf"][:, :], 1.0))
    P.op("vector", lambda e: e.memset(sb["eps24"][:, :], 1e-24))
    ones_tok = P.op("vector", lambda e: e.memset(sb["ones_row"][:, :], 1.0), inc="ve")

    def scrbf(name):
        t = sb[name][:, :].bitcast(BF16).tensor
        return bass.AP(t, 0, [[1024, 128], [1, 512]])

    def emit_ln(shift, mode, deps):
        deps = [d for d in deps if d is not None]
        c0s = [0 if (shift and dt_ >= 4) else 1 for dt_ in range(8)]
        spill_toks = [None] * 8
        if mode == "stream":
            for dt_ in range(8):
                eng = "sync" if dt_ % 2 == 0 else "gpsimd"
                spill_toks[dt_] = P.dma(eng, xo_scr[dt_], S_(dt_, 0, TG),
                                        waits=deps,
                                        inc="sd" if eng == "sync" else "wd")
        xbf_tok = None
        f32_tok = None
        p3s_tok = None
        for tc in reversed(range(4)):
            cast_toks = [None] * 8
            for dt_ in range(8):
                if dt_ % 2 == 0:
                    cast_toks[dt_] = P.op("vector", lambda e, dt_=dt_, tc=tc: e.tensor_copy(
                        XBF_(dt_, c0s[dt_] + tc * 512, 512),
                        S_(dt_, c0s[dt_] + tc * 512, 512)),
                        waits=deps, inc="ve")
                else:
                    cast_toks[dt_] = P.op("scalar", lambda e, dt_=dt_, tc=tc: e.activation(
                        XBF_(dt_, c0s[dt_] + tc * 512, 512),
                        S_(dt_, c0s[dt_] + tc * 512, 512), AF.Copy),
                        waits=deps, inc="ac")
            w_ps2 = deps + [pfree.get("ps2")]
            w_ps3 = deps + [pfree.get("ps3")]
            t_sum = None
            t_ssq = None
            for dt_ in range(8):
                sn = "scr_a" if dt_ % 2 == 0 else "scr_b"
                t_sq = P.op("scalar", lambda e, dt_=dt_, tc=tc, sn=sn: e.activation(
                    scrbf(sn), S_(dt_, c0s[dt_] + tc * 512, 512), AF.Square),
                    waits=deps + [pfree.get(sn)], inc="ac")
                t_sum = P.op("tensor", lambda e, dt_=dt_, tc=tc: e.matmul(
                    PS[2][0:1, :], sb["ones_col_bf"][:, :],
                    XBF_(dt_, c0s[dt_] + tc * 512, 512),
                    start=(dt_ == 0), stop=(dt_ == 7)),
                    waits=[cast_toks[dt_]] + (w_ps2 if dt_ == 0 else []),
                    inc=("te" if dt_ == 7 else None))
                t_ssq = P.op("tensor", lambda e, dt_=dt_, sn=sn: e.matmul(
                    PS[3][0:1, :], sb["ones_col_bf"][:, :], scrbf(sn),
                    start=(dt_ == 0), stop=(dt_ == 7)),
                    waits=[t_sq] + (w_ps3 if dt_ == 0 else []), inc="te")
                pfree[sn] = t_ssq
            rs = sb["rowstat"][0:1, 0:512]
            rs2 = sb["rowstat2"][0:1, 0:512]
            t1 = P.op("vector", lambda e, rs2=rs2: e.tensor_scalar(
                rs2, PS[2][0:1, :], -1.0 / D, None, ALU.mult),
                waits=[t_sum, pfree.get("rowmath")], inc="ve")
            pfree["ps2"] = t1
            t2 = P.op("vector", lambda e, rs=rs, rs2=rs2: e.tensor_tensor(
                rs, rs2, rs2, ALU.mult), waits=[t1], inc="ve")
            t3 = P.op("vector", lambda e, rs=rs: e.scalar_tensor_tensor(
                rs, rs, -float(D), PS[3][0:1, :], ALU.mult, ALU.add),
                waits=[t2, t_ssq], inc="ve")
            pfree["ps3"] = t3
            t3b = P.op("vector", lambda e, rs=rs: e.tensor_scalar(
                rs, rs, 1.0 / D, float(EPS_LN), ALU.mult, ALU.add),
                waits=[t3], inc="ve")
            t4 = P.op("scalar", lambda e, rs=rs: e.activation(
                rs, rs, AF.Ln), waits=[t3b], inc="ac")
            t5 = P.op("scalar", lambda e, rs=rs: e.activation(
                rs, rs, AF.Exp, scale=-0.5), waits=[t4], inc="ac")
            if mode == "stream":
                tA = P.op("tensor", lambda e, rs=rs: e.matmul(
                    PD[0][:, 0:512], sb["ones_row"][:, :], rs, start=True, stop=True),
                    waits=[t5, ones_tok, pfree.get("pd0")], inc="te")
                tB = P.op("tensor", lambda e, rs2=rs2: e.matmul(
                    PD[1][:, 0:512], sb["ones_row"][:, :], rs2, start=True, stop=True),
                    waits=[t1, pfree.get("pd1")], inc="te")
            else:
                tbf1 = P.op("vector", lambda e, rs=rs: e.tensor_copy(
                    sb["rowbf"][0:1, 0:512], rs),
                    waits=[t5, pfree.get("rowbf")], inc="ve")
                tbf2 = P.op("vector", lambda e, rs2=rs2: e.tensor_copy(
                    sb["rowbf"][0:1, 512:1024], rs2), waits=[t1], inc="ve")
                tA = P.op("tensor", lambda e: e.matmul(
                    PD[0][:, 0:512], sb["ones_row_bf"][:, :],
                    sb["rowbf"][0:1, 0:512], start=True, stop=True),
                    waits=[tbf1, ones_tok, pfree.get("pd0")], inc="te")
                tB = P.op("tensor", lambda e: e.matmul(
                    PD[1][:, 0:512], sb["ones_row_bf"][:, :],
                    sb["rowbf"][0:1, 512:1024], start=True, stop=True),
                    waits=[tbf2, pfree.get("pd1")], inc="te")
                pfree["rowbf"] = tB
            pfree["rowmath"] = tB
            for dt_ in range(8):
                p1 = P.op("vector", lambda e, dt_=dt_, tc=tc: e.tensor_tensor(
                    sb["scr_b"][:, :], S_(dt_, c0s[dt_] + tc * 512, 512),
                    PD[1][:, 0:512], ALU.add),
                    waits=[tB, pfree.get("scr_b")], inc="ve")
                if mode == "stream":
                    p2 = P.op("vector", lambda e, dt_=dt_, tc=tc: e.tensor_tensor(
                        S_(dt_, 1 + tc * 512, 512), sb["scr_b"][:, :],
                        PD[0][:, 0:512], ALU.mult),
                        waits=[tA, p1, spill_toks[dt_]], inc="ve")
                    f32_tok = p2
                    if dt_ % 2 == 0:
                        p2 = P.op("vector", lambda e, dt_=dt_, tc=tc: e.tensor_copy(
                            XBF_(dt_, 1 + tc * 512, 512), S_(dt_, 1 + tc * 512, 512)),
                            waits=[p2], inc="ve")
                    else:
                        p2 = P.op("scalar", lambda e, dt_=dt_, tc=tc: e.activation(
                            XBF_(dt_, 1 + tc * 512, 512), S_(dt_, 1 + tc * 512, 512),
                            AF.Copy), waits=[p2], inc="ac")
                        p3s_tok = p2
                elif mode == "bf":
                    p2 = P.op("vector", lambda e, dt_=dt_, tc=tc: e.tensor_tensor(
                        XBF_(dt_, 1 + tc * 512, 512), sb["scr_b"][:, :],
                        PD[0][:, 0:512], ALU.mult),
                        waits=[tA, p1], inc="ve")
                else:
                    l = int(mode.split(":")[1])
                    p2 = P.op("vector", lambda e, dt_=dt_, tc=tc: e.tensor_tensor(
                        sb["scr_c"][:, :], sb["scr_b"][:, :], PD[0][:, 0:512],
                        ALU.mult), waits=[tA, p1, pfree.get("scr_c")], inc="ve")
                    p2 = P.op("vector", lambda e, dt_=dt_, tc=tc, l=l: e.tensor_scalar(
                        XBF_(dt_, 1 + tc * 512, 512), sb["scr_c"][:, :],
                        bcol(f"ln_g{l}", dt_), bcol(f"ln_b{l}", dt_),
                        ALU.mult, ALU.add), waits=[p2], inc="ve")
                    pfree["scr_c"] = p2
                pfree["scr_b"] = p2
                xbf_tok = p2
            pfree["pd0"] = xbf_tok
            pfree["pd1"] = xbf_tok
        if mode == "stream":
            for dt_ in range(8):
                xbf_tok = P.op("vector", lambda e, dt_=dt_: e.tensor_copy(
                    XBF_(dt_, 0, 1), S_(dt_, 0, 1)),
                    waits=deps + [p3s_tok], inc="ve")
        return f32_tok, xbf_tok

    def emit_f_layer(l, xbf_tok, stream_tok):
        wqk_toks = []
        for kt in range(8):
            wqk_toks.append(P.dma(
                "gpsimd", WSLOT(R_W + kt * 512, 0, 1024),
                dr["wqkv"][l][kt * 128:(kt + 1) * 128, 0:1024],
                waits=[pfree.get("r_w")], inc="wd"))
        wqk_tok = wqk_toks[-1]
        tq1 = P.op("tensor", lambda e: e.matmul(
            PD[0][:, 0:64], sb["ones_row"][:, :],
            sb["qsks"][0:1, 128 * l:128 * l + 64], start=True, stop=True),
            waits=[ones_tok, cst_tok, pfree.get("pd0")], inc="te")
        tq2 = P.op("vector", lambda e: e.tensor_copy(sb["qsB"][:, :], PD[0][:, 0:64]),
                   waits=[tq1], inc="ve")
        tk1 = P.op("tensor", lambda e: e.matmul(
            PD[0][:, 64:128], sb["ones_row"][:, :],
            sb["qsks"][0:1, 128 * l + 64:128 * l + 128], start=True, stop=True),
            waits=[tq2], inc="te")
        tk2 = P.op("vector", lambda e: e.tensor_copy(sb["ksB"][:, :], PD[0][:, 64:128]),
                   waits=[tk1], inc="ve")
        pfree["pd0"] = tk2
        qsb_tok = tk2

        qsw = bass.AP(sb["qsB"][:, :].tensor, 32, [[64, 128], [0, 16], [-32, 2], [1, 32]])
        ksw = bass.AP(sb["ksB"][:, :].tensor, 32, [[64, 128], [0, 16], [-32, 2], [1, 32]])
        qsb_b = bass.AP(sb["qsB"][:, :].tensor, 0, [[64, 128], [0, 16], [1, 64]])
        ksb_b = bass.AP(sb["ksB"][:, :].tensor, 0, [[64, 128], [0, 16], [1, 64]])
        tab_build = None
        for srcnm, mul, tofs, swap in (("tcq", qsb_b, 0, False), ("tsq", qsw, 1024, True),
                                       ("tck", ksb_b, 2048, False), ("tsk", ksw, 3072, True)):
            tcp = P.op("vector", lambda e, srcnm=srcnm, tofs=tofs: e.tensor_copy(
                af32(R_VT + tofs, 1024), sb[srcnm][:, :]),
                waits=[tab_tok, qsb_tok, pfree.get("vtreg")], inc="ve")
            if swap:
                tab_build = P.op("vector", lambda e, mul=mul, tofs=tofs: e.tensor_tensor(
                    bass.AP(ar_t, R_VT + tofs, [[NCOLS, 128], [64, 16], [32, 2], [1, 32]]),
                    bass.AP(ar_t, R_VT + tofs, [[NCOLS, 128], [64, 16], [32, 2], [1, 32]]),
                    mul, ALU.mult), waits=[tcp], inc="ve")
            else:
                tab_build = P.op("vector", lambda e, mul=mul, tofs=tofs: e.tensor_tensor(
                    bass.AP(ar_t, R_VT + tofs, [[NCOLS, 128], [64, 16], [1, 64]]),
                    bass.AP(ar_t, R_VT + tofs, [[NCOLS, 128], [64, 16], [1, 64]]),
                    mul, ALU.mult), waits=[tcp], inc="ve")
        pfree["vtreg"] = None

        tr_evac = None
        last_qk_mm = None
        def halfbf(name, half):
            t = sb[name][:, :].bitcast(BF16).tensor
            return bass.AP(t, 512 * half, [[1024, 128], [1, 512]])

        sq_bufs = [
            (halfbf("scr_a", 0), halfbf("scr_b", 0), halfbf("scr_c", 0),
             "scr_a", "scr_b", "scr_c"),
            (halfbf("scr_a", 1), halfbf("scr_b", 1), halfbf("scr_c", 1),
             "scr_ao", "scr_bo", "scr_co"),
        ]

        def ap3(ap):
            return bass.AP(ap.tensor, ap.offset, [[ap.ap[0][0], 128], [64, 8], [1, 64]])

        for g4 in range(4):
            for ts in range(4):
                tt = g4 * 4 + ts
                par = tt % 2
                bqa, bka, bca, kqa, kka, kca = sq_bufs[par]
                rb0 = 16 * par
                qpsn, kpsn = ("ps0", "ps2") if par == 0 else ("ps1", "ps3")
                qps, kps = PSAP[qpsn], PSAP[kpsn]
                for ci, psn, ps in ((0, qpsn, qps), (1, kpsn, kps)):
                    w0 = [xbf_tok, wqk_tok, pfree.get(psn)]
                    for kt in range(8):
                        c0 = (1 if kt < 4 else 0) + 128 * tt
                        last_qk_mm = P.op("tensor", lambda e, kt=kt, c0=c0, ci=ci, ps=ps: e.matmul(
                            ps[:, :], XBF_(kt, c0, 128),
                            WSLOT(R_W + kt * 512, ci * 512, 512),
                            start=(kt == 0), stop=(kt == 7)),
                            waits=(w0 if kt == 0 else ()),
                            inc=("te" if kt == 7 else None))
                    if ci == 0:
                        qmm = last_qk_mm
                    else:
                        kmm = last_qk_mm
                sqq = P.op("scalar", lambda e, qps=qps, bqa=bqa: e.activation(
                    bqa, qps[:, :], AF.Square),
                    waits=[qmm, pfree.get(kqa)], inc="ac")
                sqk = P.op("scalar", lambda e, kps=kps, bka=bka: e.activation(
                    bka, kps[:, :], AF.Square),
                    waits=[kmm, pfree.get(kka)], inc="ac")
                rq = P.op("vector", lambda e, bqa=bqa, rb0=rb0: e.tensor_reduce(
                    sb["rn2"][:, rb0:rb0 + 8], ap3(bqa),
                    AX.X, ALU.add), waits=[sqq], inc="ve")
                rk = P.op("vector", lambda e, bka=bka, rb0=rb0: e.tensor_reduce(
                    sb["rn2"][:, rb0 + 8:rb0 + 16], ap3(bka),
                    AX.X, ALU.add), waits=[sqk], inc="ve")
                pfree[kqa] = rq
                pfree[kka] = rk
                rr = P.op("scalar", lambda e, rb0=rb0: e.activation(
                    sb["rn2"][:, rb0:rb0 + 16], sb["rn2"][:, rb0:rb0 + 16],
                    AF.Ln, bias=sb["eps24"][:, :]),
                    waits=[rq, rk], inc="ac")
                rr = P.op("scalar", lambda e, rb0=rb0: e.activation(
                    sb["rn2"][:, rb0:rb0 + 16], sb["rn2"][:, rb0:rb0 + 16],
                    AF.Exp, scale=-0.5),
                    waits=[rr], inc="ac")
                outs = []
                for half, tofs, dst, wkey, psx, rnlo in (
                        (0, 0, sb["qf"], "qfw", qps, 0),
                        (1, 2048, sb["kf"], "kfw", kps, 8)):
                    dstap = dst[:, :]
                    t1 = P.op("vector", lambda e, psx=psx, tofs=tofs, tt=tt, bca=bca: e.tensor_tensor(
                        ap3(bca),
                        psx[:, :].rearrange("p (h d) -> p h d", h=8),
                        bass.AP(ar_t, R_VT + tofs + tt * 64, [[NCOLS, 128], [0, 8], [1, 64]]),
                        ALU.mult), waits=[tab_build, pfree.get(kca) if half == 0 else None],
                        inc="ve")
                    t2 = P.op("vector", lambda e, psx=psx, tofs=tofs, tt=tt, ts=ts,
                              dstap=dstap: e.tensor_tensor(
                        bass.AP(dstap.tensor, ts * 512, [[2048, 128], [64, 8], [32, 2], [1, 32]]),
                        bass.AP(psx[:, :].tensor, 32,
                                [[512, 128], [64, 8], [-32, 2], [1, 32]]),
                        bass.AP(ar_t, R_VT + tofs + 1024 + tt * 64,
                                [[NCOLS, 128], [0, 8], [32, 2], [1, 32]]),
                        ALU.mult), waits=[pfree.get(wkey)], inc="ve")
                    t3 = P.op("vector", lambda e, dst=dst, ts=ts, bca=bca: e.tensor_tensor(
                        dst[:, ts * 512:(ts + 1) * 512], dst[:, ts * 512:(ts + 1) * 512],
                        bca, ALU.add), waits=[t1, t2], inc="ve")
                    pfree[psx is qps and qpsn or kpsn] = t3
                    t4 = P.op("vector", lambda e, dst=dst, ts=ts, rnlo=rnlo, rb0=rb0: e.tensor_tensor(
                        dst[:, ts * 512:(ts + 1) * 512].rearrange("p (h d) -> p h d", h=8),
                        dst[:, ts * 512:(ts + 1) * 512].rearrange("p (h d) -> p h d", h=8),
                        sb["rn2"][:, rb0 + rnlo:rb0 + rnlo + 8].broadcast_to((128, 8, DH)),
                        ALU.mult),
                        waits=[t3, rr], inc="ve")
                    outs.append(t4)
                pfree[kca] = outs[1]
                qf_tok, kf_tok = outs
            for srcbuf, dstfn, ready, wkey in ((sb["qf"], QT_, qf_tok, "qfw"),
                                               (sb["kf"], KT_, kf_tok, "kfw")):
                for fs in range(4):
                    tmm = None
                    for ts in range(4):
                        tmm = P.op("tensor", lambda e, fs=fs, ts=ts, srcbuf=srcbuf: e.matmul(
                            PD[1][:, ts * 128:(ts + 1) * 128],
                            srcbuf[:, ts * 512 + fs * 128: ts * 512 + fs * 128 + 128],
                            sb["ident"][:, :], start=(ts == 0), stop=(ts == 3)),
                            waits=([ready, pfree.get("pd1")] if ts == 0 else ()),
                            inc=("te" if ts == 3 else None))
                    tev = P.op("vector", lambda e, fs=fs, g4=g4, dstfn=dstfn: e.tensor_copy(
                        dstfn(fs, g4 * 512, 512), PD[1][:, 0:512]),
                        waits=[tmm], inc="ve")
                    pfree["pd1"] = tev
                pfree[wkey] = tmm
            tr_evac = tev
        qkT_tok = tr_evac
        for k0, k1 in (("scr_a", "scr_ao"), ("scr_b", "scr_bo"),
                       ("scr_c", "scr_co")):
            a, b = pfree.get(k0), pfree.get(k1)
            if a is None or (b is not None and b[1] > a[1]):
                pfree[k0] = b

        wv_toks = []
        for kt in range(8):
            wv_toks.append(P.dma(
                "gpsimd", WSLOT(R_W + kt * 256, 0, 512),
                dr["wqkv"][l][kt * 128:(kt + 1) * 128, 1024:1536],
                waits=[qkT_tok if kt == 0 else None], inc="wd"))
        wv_tok = wv_toks[-1]
        vt_tok = None
        for tt in range(16):
            psn = "ps0" if tt % 2 == 0 else "ps1"
            ps = PSAP[psn]
            w0 = [xbf_tok, wv_tok, pfree.get(psn), qkT_tok]
            vmm = None
            for kt in range(8):
                c0 = (1 if kt < 4 else 0) + 128 * tt
                vmm = P.op("tensor", lambda e, kt=kt, c0=c0, ps=ps: e.matmul(
                    ps[:, :], XBF_(kt, c0, 128), WSLOT(R_W + kt * 256, 0, 512),
                    start=(kt == 0), stop=(kt == 7)),
                    waits=(w0 if kt == 0 else ()), inc=("te" if kt == 7 else None))
            vt_tok = P.op("vector", lambda e, tt=tt, ps=ps: e.tensor_copy(
                VT_(tt, 0, 512), ps[:, :]), waits=[vmm], inc="ve")
            pfree[psn] = vt_tok

        if DBG_F and l == 0:
            for fs in range(4):
                P.dma("gpsimd", ydbg[fs], QT_(fs, 0, 512), waits=[qkT_tok], inc="wd")
                P.dma("gpsimd", ydbg[4 + fs], KT_(fs, 0, 512), waits=[qkT_tok], inc="wd")
            for ttd in range(4):
                P.dma("gpsimd", ydbg[8 + ttd], VT_(ttd, 0, 512), waits=[vt_tok], inc="wd")

        AN_BASE = (R_W, R_W + 1280, R_XBF + 4096, R_XBF + 5376)
        ATT_BASE = (R_XBF + 6656, R_XBF + 7168)

        def ANp(par, e0, n):
            return abf(AN_BASE[par], e0, n)

        def ATTp(par, e0, n):
            return abf(ATT_BASE[par], e0, n)

        ao_tok = None
        prev_ex = [[None] * 4 for _ in range(4)]
        prev_rc = [[None] * 4 for _ in range(4)]
        anfree = [[None] * 4 for _ in range(4)]
        state_attfree = {}
        seq = [(hh, g) for g in range(4) for hh in range(NH)]
        saved = None
        pfree["a_s0"] = pfree["a_s1"] = vt_tok
        pfree["a_s2"] = pfree.get("ps2")
        pfree["a_s3"] = pfree.get("ps3")
        for s in range(4):
            pfree[f"a_x{s}"] = pfree.get("pd0")
        pfree["a_td0"] = pfree.get("pd0")
        pfree["a_td1"] = pfree.get("pd1")
        pfree["a_pv"] = pfree.get("pd1")

        def emit_scores(hh, g, par4):
            pair, rb = hh // 2, 64 * (hh % 2)
            info = {}
            exts = [s for s in range(4) if 512 * g + 128 * s >= WIN]
            for s in range(4):
                qg = 512 * g + 128 * s
                kstart = max(0, qg - WIN)
                klm = min(512, qg + 128 - kstart)
                has_ext = qg >= WIN
                ps = PS[s]
                smm = P.op("tensor", lambda e, pair=pair, rb=rb, qg=qg,
                           kstart=kstart, klm=klm, ps=ps: e.matmul(
                    ps[:, 0:klm], QT_(pair, qg, 128, rb, 64),
                    KT_(pair, kstart, klm, rb, 64), start=True, stop=True),
                    waits=[qkT_tok, pfree.get(f"a_s{s}")], inc="te")
                xmm = None
                if has_ext:
                    first_ext = s == exts[0]
                    xw = ([pfree.get(f"a_x{t}") for t in range(4)]
                          if first_ext else [pfree.get(f"a_x{s}")])
                    xmm = P.op("tensor", lambda e, pair=pair, rb=rb, qg=qg, s=s,
                               st=first_ext, sp=(s == exts[-1]): e.matmul(
                        PD[0][:, 128 * s:128 * s + 128], QT_(pair, qg, 128, rb, 64),
                        KT_(pair, qg, 128, rb, 64), start=st, stop=sp),
                        waits=xw, inc="te")
                info[s] = (qg, kstart, klm, has_ext, smm, xmm, ps)
            return info, exts

        def emit_softmax(hh, g, par4, info, exts, an_toks, an_rdy):
            xmm_last = info[exts[-1]][5] if exts else None
            for s in range(4):
                qg, kstart, klm, has_ext, smm, xmm, ps = info[s]
                if has_ext:
                    cm = P.op("vector", lambda e, ps=ps: e.tensor_tensor(
                        ps[:, 0:128], ps[:, 0:128], sb["mask_l"][:, :], ALU.add),
                        waits=[smm, cst_tok], inc="ve")
                    cmx = P.op("vector", lambda e, s=s: e.tensor_tensor(
                        PD[0][:, 128 * s:128 * s + 128],
                        PD[0][:, 128 * s:128 * s + 128],
                        sb["mask_r"][:, :], ALU.add), waits=[xmm_last], inc="ve")
                else:
                    cm = P.op("vector", lambda e, ps=ps, klm=klm: e.tensor_tensor(
                        ps[:, klm - 128:klm], ps[:, klm - 128:klm],
                        sb["mask_r"][:, :], ALU.add), waits=[smm, cst_tok], inc="ve")
                    cmx = None
                info[s] = info[s] + (cm, cmx)
            nmcs = {}
            for s in range(4):
                qg, kstart, klm, has_ext, smm, xmm, ps, cm, cmx = info[s]
                sc = 4 * par4 + s
                if has_ext:
                    nm1 = P.op("vector", lambda e, ps=ps, sc=sc: e.tensor_reduce(
                        sb["nmax"][:, sc:sc + 1], ps[:, 0:512], AX.X, ALU.max,
                        negate=True), waits=[cm, prev_ex[par4][s]], inc="ve")
                    nm2 = P.op("vector", lambda e, s=s, sc=sc: e.tensor_reduce(
                        sb["nmax2"][:, sc:sc + 1], PD[0][:, 128 * s:128 * s + 128],
                        AX.X, ALU.max, negate=True), waits=[cmx], inc="ve")
                    nmcs[s] = P.op("vector", lambda e, sc=sc: e.tensor_tensor(
                        sb["nmax"][:, sc:sc + 1], sb["nmax"][:, sc:sc + 1],
                        sb["nmax2"][:, sc:sc + 1], ALU.min),
                        waits=[nm1, nm2], inc="ve")
                else:
                    nmcs[s] = P.op("vector", lambda e, ps=ps, klm=klm, sc=sc:
                                   e.tensor_reduce(
                        sb["nmax"][:, sc:sc + 1], ps[:, 0:klm], AX.X, ALU.max,
                        negate=True), waits=[cm, prev_ex[par4][s]], inc="ve")
            ex2s = {}
            for s in range(4):
                qg, kstart, klm, has_ext, smm, xmm, ps, cm, cmx = info[s]
                sc = 4 * par4 + s
                if has_ext:
                    ex1 = P.op("scalar", lambda e, sc=sc, s=s, par4=par4: e.activation(
                        ANp(par4, 640 * s, 512), PS[s][:, :], AF.Exp,
                        bias=sb["nmax"][:, sc:sc + 1], scale=1.0,
                        accum_out=sb["rsum"][:, sc:sc + 1]),
                        waits=[nmcs[s], prev_rc[par4][s], anfree[par4][s]], inc="ac")
                    ex2 = P.op("scalar", lambda e, sc=sc, s=s, par4=par4: e.activation(
                        ANp(par4, 640 * s + 512, 128),
                        PD[0][:, 128 * s:128 * s + 128], AF.Exp,
                        bias=sb["nmax"][:, sc:sc + 1], scale=1.0,
                        accum_out=sb["rsum2"][:, sc:sc + 1]), waits=(), inc="ac")
                    pfree[f"a_s{s}"] = ex1
                    pfree[f"a_x{s}"] = ex2
                    prev_ex[par4][s] = ex2
                    ex2s[s] = (ex1, ex2)
                    an_rdy[s] = ex2
                else:
                    ex1 = P.op("scalar", lambda e, sc=sc, s=s, klm=klm, par4=par4:
                               e.activation(
                        ANp(par4, 640 * s, klm), PS[s][:, 0:klm], AF.Exp,
                        bias=sb["nmax"][:, sc:sc + 1], scale=1.0,
                        accum_out=sb["rsum"][:, sc:sc + 1]),
                        waits=[nmcs[s], prev_rc[par4][s], anfree[par4][s]], inc="ac")
                    pfree[f"a_s{s}"] = ex1
                    prev_ex[par4][s] = ex1
                    ex2s[s] = None
                    an_toks[s] = ex1
                    an_rdy[s] = ex1
            return ex2s

        dgfree = [None, None]

        def emit_vlate(idx, par4, ex2s, an_toks, an_rdy):
            for s in range(4):
                if ex2s[s] is not None:
                    ex1, ex2 = ex2s[s]
                    sc = 4 * par4 + s
                    an_toks[s] = P.op("vector", lambda e, sc=sc: e.tensor_tensor(
                        sb["rsum"][:, sc:sc + 1], sb["rsum"][:, sc:sc + 1],
                        sb["rsum2"][:, sc:sc + 1], ALU.add),
                        waits=[ex1, ex2], inc="ve")
            rc = P.op("vector", lambda e, par4=par4: e.reciprocal(
                sb["rrec"][:, 4 * par4:4 * par4 + 4],
                sb["rsum"][:, 4 * par4:4 * par4 + 4]),
                waits=list(an_toks), inc="ve")
            out = []
            for s in range(4):
                prev_rc[par4][s] = rc
                jj = 4 * (idx % 2) + s
                dg = P.op("vector", lambda e, jj=jj, sc=4 * par4 + s: e.tensor_scalar(
                    sb["diag"][:, 128 * jj:128 * jj + 128], sb["ident"][:, :],
                    sb["rrec"][:, sc:sc + 1], None, ALU.mult),
                    waits=[rc, dgfree[idx % 2]], inc="ve")
                out.append((an_rdy[s], dg))
            return out

        pending_ao = [None]
        ao_by_g = [[] for _ in range(4)]

        def flush_ao():
            nonlocal ao_tok
            if pending_ao[0] is None:
                return
            pair1, rb1, g1, pv_tok = pending_ao[0]
            pending_ao[0] = None
            ao_tok = P.op("vector", lambda e, pair1=pair1, rb1=rb1, g1=g1:
                          e.tensor_copy(
                AOUT_(pair1, 512 * g1, 512, rb1, 64),
                PD[1][0:64, 512:1024]),
                waits=[pv_tok, vt_tok], inc="ve")
            pfree["a_pv"] = ao_tok
            ao_by_g[g1].append(ao_tok)

        def emit_stage_b(idx1, hh1, g1, par1, an1):
            pair1, rb1 = hh1 // 2, 64 * (hh1 % 2)
            qg01 = 512 * g1
            jts = list(range(max(0, 4 * g1 - 4), 4 * g1 + 4))
            if 4 * g1 in jts:
                jts = [4 * g1] + [j for j in jts if j != 4 * g1]
            pv_w0 = [vt_tok, pfree.get("a_pv")]
            pv_tok = None
            last_tmm = None
            attfree = state_attfree
            pend_pv = None

            def emit_pv(args):
                nonlocal pv_tok
                jt, cr0, cr1, so, tev, st0, sp0, slot = args
                pv_tok = P.op("tensor", lambda e, jt=jt, hh1=hh1, cr0=cr0, cr1=cr1,
                              so=so, sl=idx1 % 2, st0=st0, sp0=sp0: e.matmul(
                    PD[1][0:64, 512 + cr0:512 + cr1],
                    VT_(jt, 64 * hh1, 64),
                    ATTp(sl, so + cr0, cr1 - cr0),
                    start=st0, stop=sp0),
                    waits=[tev] + (pv_w0 if st0 else []), inc="te")
                attfree[slot] = pv_tok

            for pos, jt in enumerate(jts):
                ss = []
                for s in range(4):
                    qg = qg01 + 128 * s
                    jstart = max(0, qg - WIN) // 128
                    nkb = (qg + 128 - max(0, qg - WIN)) // 128
                    if jstart <= jt < jstart + nkb:
                        ss.append(s)
                smin, smax = ss[0], ss[-1]
                tdn = "a_td0" if pos % 2 == 0 else "a_td1"
                td = PD[0][:, 512:1024] if pos % 2 == 0 else PD[1][:, 0:512]
                tmm = None
                for i, s in enumerate(ss):
                    qg = qg01 + 128 * s
                    jstart = max(0, qg - WIN) // 128
                    jbl = jt - jstart
                    acol = 640 * s + (512 if (qg >= WIN and jbl == 4) else 128 * jbl)
                    jj = 4 * (idx1 % 2) + s
                    tmm = P.op("tensor", lambda e, s=s, acol=acol, i=i, ns=len(ss),
                               td=td, par1=par1, jj=jj: e.matmul(
                        td[:, 128 * s:128 * s + 128],
                        ANp(par1, acol, 128),
                        sb["diag"][:, 128 * jj:128 * jj + 128],
                        start=(i == 0), stop=(i == ns - 1)),
                        waits=[an1[s][0], an1[s][1]] + ([pfree.get(tdn)] if i == 0 else []),
                        inc=("te" if i == len(ss) - 1 else None))
                last_tmm = tmm
                cr0, cr1 = 128 * smin, 128 * (smax + 1)
                slot = (idx1 % 2, pos % 2)
                tev = P.op("scalar", lambda e, cr0=cr0, cr1=cr1, td=td,
                           so=512 * (pos % 2), sl=idx1 % 2: e.activation(
                    ATTp(sl, so + cr0, cr1 - cr0),
                    td[:, cr0:cr1], AF.Copy),
                    waits=[tmm, attfree.get(slot)], inc="ac")
                pfree[tdn] = tev
                emit_pv((jt, cr0, cr1, 512 * (pos % 2), tev,
                         pos == 0, pos == len(jts) - 1, slot))
            for s in range(4):
                anfree[par1][s] = last_tmm
            dgfree[idx1 % 2] = last_tmm
            pending_ao[0] = (pair1, rb1, g1, pv_tok)

        wo_toks = []
        for kt in range(4):
            wo_toks.append(P.dma("gpsimd", WSLOT(R_W + 2560 + kt * 512, 0, 1024),
                                 dr["woF"][l][kt * 128:(kt + 1) * 128, :],
                                 waits=[last_qk_mm], inc="wd"))
        wo_tok = wo_toks[-1]
        op_tok = None

        def emit_oproj(tc):
            nonlocal op_tok
            aog = ao_by_g[tc][-1]
            for dt_ in range(8):
                bank = f"a_s{dt_ % 4}"
                ps = PS[dt_ % 4]
                omm = None
                for kt in range(4):
                    omm = P.op("tensor", lambda e, kt=kt, dt_=dt_, tc=tc, ps=ps: e.matmul(
                        ps[:, :], WSLOT(R_W + 2560 + kt * 512, 128 * dt_, 128),
                        AOUT_(kt, 512 * tc, 512),
                        start=(kt == 0), stop=(kt == 3)),
                        waits=([aog, wo_tok, pfree.get(bank)] if kt == 0 else ()),
                        inc=("te" if kt == 3 else None))
                op_tok = P.op("vector", lambda e, dt_=dt_, tc=tc, ps=ps: e.tensor_tensor(
                    S_(dt_, 1 + 512 * tc, 512), ps[:, :], S_(dt_, 1 + 512 * tc, 512),
                    ALU.add), waits=[omm, stream_tok], inc="ve")
                pfree[bank] = op_tok

        op_emitted = [False] * 4
        for idx in range(len(seq) + 1):
            if idx < len(seq):
                hh, g = seq[idx]
                par4 = idx % 4
                info, exts = emit_scores(hh, g, par4)
            if saved is not None:
                flush_ao()
                emit_stage_b(*saved)
                saved = None
            if idx < len(seq):
                an_toks = [None] * 4
                an_rdy = [None] * 4
                ex2s = emit_softmax(hh, g, par4, info, exts, an_toks, an_rdy)
                pairs = emit_vlate(idx, par4, ex2s, an_toks, an_rdy)
                saved = (idx, hh, g, par4, pairs)
            for tc in range(4):
                if not op_emitted[tc] and len(ao_by_g[tc]) == 8:
                    emit_oproj(tc)
                    op_emitted[tc] = True
        flush_ao()
        for tc in range(4):
            if not op_emitted[tc] and len(ao_by_g[tc]) == 8:
                emit_oproj(tc)
                op_emitted[tc] = True
        assert all(op_emitted)
        for s in range(4):
            pfree[f"ps{s}"] = pfree.get(f"a_s{s}")
        pfree["pd0"] = P.tok("ac")
        pfree["pd1"] = ao_tok

        _, xln2_tok = emit_ln(True, f"affine:{l}", [op_tok])

        glu_tok = None
        w2_tok = None
        for th in range(2):
            pair_toks = [None, None]
            slot_free = [P.tok("te"), P.tok("te")]
            for p in range(NPAIR + 1):
                if p < NPAIR:
                    slot = p % 2
                    pair_toks[slot] = P.dma(
                        "gpsimd", WSLOT(R_W + slot * 1024, 0, 2048),
                        dr["w1r"][l][p], waits=[slot_free[slot]], inc="wd")
                if p == 0:
                    continue
                pp = p - 1
                slot = pp % 2
                for tci in range(2):
                    tc = 2 * th + tci
                    psa = PSAP["ps0" if tci == 0 else "ps2"]
                    psg = PSAP["ps1" if tci == 0 else "ps3"]
                    psan = "ps0" if tci == 0 else "ps2"
                    psgn = "ps1" if tci == 0 else "ps3"
                    amm = gmm = None
                    for kt in range(8):
                        amm = P.op("tensor", lambda e, kt=kt, tc=tc, slot=slot, psa=psa: e.matmul(
                            psa[:, :], WSLOT(R_W + slot * 1024, 256 * kt, 128),
                            XBF_(kt, 1 + 512 * tc, 512), start=(kt == 0), stop=(kt == 7)),
                            waits=([xln2_tok, pair_toks[slot], pfree.get(psan)]
                                   if kt == 0 else ()),
                            inc=("te" if kt == 7 else None))
                    for kt in range(8):
                        gmm = P.op("tensor", lambda e, kt=kt, tc=tc, slot=slot, psg=psg: e.matmul(
                            psg[:, :], WSLOT(R_W + slot * 1024, 256 * kt + 128, 128),
                            XBF_(kt, 1 + 512 * tc, 512), start=(kt == 0), stop=(kt == 7)),
                            waits=([pfree.get(psgn)] if kt == 0 else ()),
                            inc=("te" if kt == 7 else None))
                    if tci == 1:
                        slot_free[slot] = gmm
                    gg = P.op("scalar", lambda e, pp=pp, l=l, tci=tci, psg=psg: e.activation(
                        sb["scr_a" if tci == 0 else "scr_c"][:, :], psg[:, :], AF.Gelu,
                        bias=bcol(f"b1g{l}", pp), scale=1.0),
                        waits=[gmm, pfree.get("scr_a" if tci == 0 else "scr_c")], inc="ac")
                    glu_tok = P.op("vector", lambda e, pp=pp, l=l, tci=tci, psa=psa: e.scalar_tensor_tensor(
                        abf(R_QT + pp * 512, 512 * tci, 512),
                        psa[:, :], bcol(f"b1a{l}", pp),
                        sb["scr_a" if tci == 0 else "scr_c"][:, :],
                        ALU.add, ALU.mult),
                        waits=[gg, (w2_tok if th == 1 else None)], inc="ve")
                    pfree[psan] = glu_tok
                    pfree[psgn] = glu_tok
                    pfree["scr_a" if tci == 0 else "scr_c"] = glu_tok
            p8 = [PS[0][:, :], PS[1][:, :], PS[2][:, :], PS[3][:, :],
                  PD[0][:, 0:512], PD[0][:, 512:1024], PD[1][:, 0:512],
                  PD[1][:, 512:1024]]
            p8n = ["ps0", "ps1", "ps2", "ps3", "pd0", "pd0", "pd1", "pd1"]
            for tci in range(2):
                tc = 2 * th + tci
                kslot_free = [P.tok("te"), P.tok("te")]
                k_toks = [None, None]
                wmm = None
                for kt in range(NPAIR):
                    slot = kt % 2
                    k_toks[slot] = P.dma(
                        "gpsimd", WSLOT(R_W + 4096 + slot * 512, 0, 1024),
                        dr["w2p"][l][128 * kt:128 * kt + 128, :],
                        waits=[kslot_free[slot],
                               glu_tok if kt == 0 else None], inc="wd")
                    for dt_ in range(8):
                        w0 = [k_toks[slot]]
                        if kt == 0:
                            w0 += [glu_tok] + [pfree.get(n) for n in set(p8n)]
                        wmm = P.op("tensor", lambda e, kt=kt, dt_=dt_, tci=tci,
                                   slot=slot: e.matmul(
                            p8[dt_], WSLOT(R_W + 4096 + slot * 512, 128 * dt_, 128),
                            abf(R_QT + kt * 512, 512 * tci, 512),
                            start=(kt == 0), stop=(kt == NPAIR - 1)),
                            waits=(w0 if dt_ == 0 else ()),
                            inc=("te" if dt_ == 7 else None))
                    kslot_free[slot] = wmm
                for dt_ in range(8):
                    w2_tok = P.op("vector", lambda e, dt_=dt_, tc=tc, l=l:
                                  e.scalar_tensor_tensor(
                        S_(dt_, 1 + 512 * tc, 512), p8[dt_], bcol(f"b2{l}", dt_),
                        S_(dt_, 1 + 512 * tc, 512), ALU.add, ALU.add),
                        waits=[wmm], inc="ve")
                    pfree[p8n[dt_]] = w2_tok
        pfree["r_w"] = w2_tok
        return w2_tok


    deps0 = xin_toks + [cst_tok, ones_tok]
    cur = None

    def emit_ffmod(deps):
        _, xbf_tok = emit_ln(False, "stream", deps)
        stream_tok = xbf_tok
        t = emit_f_layer(0, xbf_tok, stream_tok)
        xb = None
        for dt_ in range(8):
            xb = P.op("vector", lambda e, dt_=dt_: e.tensor_copy(
                XBF_(dt_, 0, TG), S_(dt_, 0, TG)), waits=[t], inc="ve")
        t = emit_f_layer(1, xb, t)
        mt = None
        prev_mt = [None, None]
        mts = [None] * 8
        for dt_ in range(8):
            so = R_W + 2304 * (dt_ % 2)
            ft = P.dma("sync", af32(so, TG), xo_scr[dt_],
                       waits=[t, prev_mt[dt_ % 2]], inc="sd")
            mt = P.op("vector", lambda e, dt_=dt_, so=so: e.scalar_tensor_tensor(
                S_(dt_, 0, TG), S_(dt_, 0, TG), 0.5, af32(so, TG),
                ALU.mult, ALU.add), waits=[ft, t], inc="ve")
            prev_mt[dt_ % 2] = mt
            mts[dt_] = mt
        return mt, mts

    def emit_A(deps):
        _, xbf_tok = emit_ln(False, "bf", deps)
        def proj_w_load(name, tok0):
            toks = []
            for kt in range(8):
                dst = (WSLOT(R_W + kt * 1024, 0, 2048) if kt < 5 else
                       WSLOT(R_QT + (kt - 5) * 1024, 0, 2048))
                toks.append(P.dma("gpsimd", dst, dr[name][kt * 128:(kt + 1) * 128, :],
                                  waits=[tok0], inc="wd"))
            return toks[-1], [
                (WSLOT(R_W + kt * 1024, 0, 2048) if kt < 5 else
                 WSLOT(R_QT + (kt - 5) * 1024, 0, 2048)) for kt in range(8)]

        last_mm = None
        spill_toks = []
        for which, wname, scr, bias in (("q", "wq", q_scr, "bq"), ("k", "wk", k_scr, "bk")):
            wtok, wslots = proj_w_load(wname, xbf_tok if which == "q" else last_mm)
            for fc in range(16):
                stg = sb["qf"] if which == "q" else sb["kf"]
                ev = None
                for tc in range(4):
                    psn = PSN[tc % 2]
                    ps = PSAP[psn]
                    mm = None
                    for kt in range(8):
                        mm = P.op("tensor", lambda e, kt=kt, fc=fc, tc=tc, ps=ps,
                                  wslots=wslots: e.matmul(
                            ps[:, :],
                            bass.AP(wslots[kt].tensor, wslots[kt].offset + 128 * fc,
                                    [[NCOLS * 2, 128], [1, 128]]),
                            XBF_(kt, 1 + 512 * tc, 512),
                            start=(kt == 0), stop=(kt == 7)),
                            waits=([wtok, xbf_tok, pfree.get(psn)] if kt == 0 else ()),
                            inc=("te" if kt == 7 else None))
                    last_mm = mm
                    ee = P.op("scalar", lambda e, fc=fc, ps=ps, bias=bias: e.activation(
                        sb["scr_a"][:, :], ps[:, :], AF.Exp, bias=bcol(bias, fc)),
                        waits=[mm, pfree.get("scr_a")], inc="ac")
                    rr = P.op("scalar", lambda e, fc=fc, ps=ps, bias=bias: e.activation(
                        sb["scr_b"][:, :], ps[:, :], AF.Relu, bias=bcol(bias, fc)),
                        waits=[ee, pfree.get("scr_b")], inc="ac")
                    ev = P.op("vector", lambda e, tc=tc, stg=stg: e.scalar_tensor_tensor(
                        stg[:, 512 * tc:512 * tc + 512], sb["scr_a"][:, :], 1.0,
                        sb["scr_b"][:, :], ALU.min, ALU.add),
                        waits=[rr, spill_toks[-1] if spill_toks else None],
                        inc="ve")
                    pfree[psn] = ev
                    pfree["scr_a"] = ev
                    pfree["scr_b"] = ev
                if which == "q":
                    spill_toks.append(P.dma("sync", q_scr[fc], stg[:, :], waits=[ev]))
                else:
                    dst = bass.AP(k_scr.tensor, fc * 16384,
                                  [[128, 128], [262144, 16], [1, 128]])
                    src = stg[:, :].rearrange("p (j c) -> p j c", j=16)
                    spill_toks.append(P.dma("sync", dst, src, waits=[ev]))
        wtok, wslots = proj_w_load("wv", last_mm)
        for tt in range(16):
            ev = None
            for ec in range(4):
                psn = PSN[ec % 2]
                ps = PSAP[psn]
                mm = None
                for kt in range(8):
                    mm = P.op("tensor", lambda e, kt=kt, tt=tt, ec=ec, ps=ps,
                              wslots=wslots: e.matmul(
                        ps[:, :], XBF_(kt, 1 + 128 * tt, 128),
                        bass.AP(wslots[kt].tensor, wslots[kt].offset + 512 * ec,
                                [[NCOLS * 2, 128], [1, 512]]),
                        start=(kt == 0), stop=(kt == 7)),
                        waits=([wtok, pfree.get(psn)] if kt == 0 else ()),
                        inc=("te" if kt == 7 else None))
                last_mm = mm
                ev = P.op("vector", lambda e, ec=ec, ps=ps: e.tensor_copy(
                    sb["qf"][:, 512 * ec:512 * ec + 512], ps[:, :]),
                    waits=[mm, spill_toks[-1] if ec == 0 else None], inc="ve")
                pfree[psn] = ev
            dst = bass.AP(v_scr.tensor, tt * 16384, [[128, 128], [262144, 16], [1, 128]])
            src = sb["qf"][:, :].rearrange("p (b c) -> p b c", b=16)
            spill_toks.append(P.dma("sync", dst, src, waits=[ev]))
        proj_done = spill_toks[-1]
        if DBG_ATT:
            P.dma("gpsimd", ydbg[47].rearrange("p (j c) -> p j c", j=4),
                  bass.AP(v_scr.tensor, 5 * 262144, [[128, 128], [16384, 4], [1, 128]]),
                  waits=[proj_done], inc="wd")

        wa_slots = []
        wa_toks = []
        for kt in range(16):
            col = (R_W + kt * 512) if kt < 10 else (R_XBF + 4096 + (kt - 10) * 512)
            wa_slots.append(col)
            wa_toks.append(P.dma("gpsimd", WSLOT(col, 0, 1024),
                                 dr["woA"][kt * 128:(kt + 1) * 128, :],
                                 waits=[last_mm if kt == 0 else None], inc="wd"))
        wa_tok = wa_toks[-1]

        at_tok = None
        kb_free = [None, None]
        vb_free = [None, None]
        qb_free = None
        sT_free = None
        for Tq in range(4):
            qg = 512 * Tq
            J = 4 * Tq + 4
            qb_toks = []
            for fc in range(16):
                qb_toks.append(P.dma("sync",
                                     abf(R_QT + fc * 256, 0, 512),
                                     q_scr[fc][:, qg:qg + 512],
                                     waits=[proj_done, qb_free], inc="sd"))
            qb_tok = qb_toks[-1]
            den_mm = None
            sT_toks = [None] * J
            for jt in range(J):
                slot = jt % 2
                kb = P.dma("sync",
                           bass.AP(ar_tb, (R_VT + 2048 + slot * 1024) * 2,
                                   [[NCOLS * 2, 128], [128, 16], [1, 128]]),
                           bass.AP(k_scr.tensor, jt * 262144,
                                   [[128, 128], [16384, 16], [1, 128]]),
                           waits=[proj_done, kb_free[slot]], inc="sd")
                cb = jt - 4 * Tq
                c_lo = 128 * cb if cb >= 0 else 0
                pdn = "pd0" if jt % 2 == 0 else "pd1"
                pd = PSAP[pdn]
                mm = None
                for kt in range(16):
                    mm = P.op("tensor", lambda e, kt=kt, slot=slot, c_lo=c_lo, pd=pd: e.matmul(
                        pd[:, c_lo:512],
                        abf(R_VT + 2048 + slot * 1024, 128 * kt, 128),
                        abf(R_QT + kt * 256, c_lo, 512 - c_lo),
                        start=(kt == 0), stop=(kt == 15)),
                        waits=([kb, qb_tok, pfree.get(pdn), sT_free] if kt == 0 else ()),
                        inc=("te" if kt == 15 else None))
                kb_free[slot] = mm
                tev = P.op("vector", lambda e, jt=jt, c_lo=c_lo, pd=pd: e.tensor_copy(
                    abf(R_KT + jt * 256, c_lo, 512 - c_lo), pd[:, c_lo:512]),
                    waits=[mm], inc="ve")
                pfree[pdn] = tev
                if cb >= 0:
                    tev = P.op("vector", lambda e, jt=jt, cb=cb: e.tensor_tensor(
                        abf(R_KT + jt * 256, 128 * cb, 128),
                        abf(R_KT + jt * 256, 128 * cb, 128),
                        sb["triu01"][:, :], ALU.mult), waits=[tev], inc="ve")
                sT_toks[jt] = tev
                if DBG_ATT and Tq == 0:
                    P.dma("gpsimd", ydbg[jt], abf(R_KT + jt * 256, 0, 512),
                          waits=[tev], inc="wd")
                den_mm = P.op("tensor", lambda e, jt=jt, c_lo=c_lo: e.matmul(
                    PS[2][0:1, c_lo:512], sb["ones_col_bf"][:, :],
                    abf(R_KT + jt * 256, c_lo, 512 - c_lo),
                    start=(jt == 0), stop=False),
                    waits=[tev] + ([pfree.get("ps2")] if jt == 0 else []), inc="te")
            epsrow = P.op("vector", lambda e: e.memset(
                sb["rowbf"][0:1, 0:512], float(EPS_LA)),
                waits=[pfree.get("rowbf")], inc="ve")
            den_mm = P.op("tensor", lambda e: e.matmul(
                PS[2][0:1, 0:512], sb["ones_col_bf"][0:1, 0:1],
                sb["rowbf"][0:1, 0:512], start=False, stop=True),
                waits=[den_mm, epsrow], inc="te")
            pfree["rowbf"] = den_mm
            dsb = P.op("vector", lambda e: e.tensor_copy(
                sb["rowstat"][0:1, 0:512], PS[2][0:1, :]), waits=[den_mm], inc="ve")
            denbf_tok = P.op("vector", lambda e: e.tensor_copy(
                sb["denbf"][0:1, :], PS[2][0:1, :]), waits=[den_mm], inc="ve")
            pfree["ps2"] = denbf_tok
            dr2 = P.op("vector", lambda e: e.reciprocal(
                sb["rowstat"][0:1, 512:1024], sb["rowstat"][0:1, 0:512]),
                waits=[dsb], inc="ve")
            bc = P.op("tensor", lambda e: e.matmul(
                PD[0][:, 0:512], sb["ones_row"][:, :], sb["rowstat"][0:1, 512:1024],
                start=True, stop=True), waits=[dr2, pfree.get("pd0")], inc="te")
            rB = P.op("vector", lambda e: e.tensor_copy(
                sb["scr_c"][:, :], PD[0][:, 0:512]), waits=[bc, pfree.get("scr_c")],
                inc="ve")
            if DBG_ATT and Tq == 0:
                P.dma("sync", ydbg[24][0:1, :], sb["rowstat"][0:1, 0:512], waits=[dsb], inc="sd"); P.dma("sync", ydbg[24][1:2, :], sb["rowstat"][0:1, 512:1024], waits=[dr2], inc="sd")
                P.dma("sync", ydbg[45], sb["scr_c"][:, :], waits=[rB], inc="sd")
                for fcd in range(16):
                    P.dma("gpsimd", ydbg[25 + fcd], abf(R_QT + fcd * 256, 0, 512),
                          waits=[qb_tok], inc="wd")
            pfree["pd0"] = rB
            nu_tok = None
            for ec in range(16):
                slot = ec % 2
                vb = P.dma("sync",
                           bass.AP(ar_tb, (R_VT + slot * 1024) * 2,
                                   [[NCOLS * 2, 128], [128, J], [1, 128]]),
                           bass.AP(v_scr.tensor, ec * 262144,
                                   [[128, 128], [16384, J], [1, 128]]),
                           waits=[proj_done, vb_free[slot]], inc="sd")
                if DBG_ATT and Tq == 0 and ec == 5:
                    P.dma("gpsimd", ydbg[46],
                          abf(R_VT + slot * 1024, 0, 512), waits=[vb], inc="wd")
                psn = "ps0" if ec % 2 == 0 else "ps1"
                ps = PSAP[psn]
                mm = None
                for jt in range(J):
                    cb = jt - 4 * Tq
                    c_lo = 128 * cb if cb >= 0 else 0
                    mm = P.op("tensor", lambda e, jt=jt, slot=slot, c_lo=c_lo, ps=ps: e.matmul(
                        ps[:, c_lo:512], abf(R_VT + slot * 1024, 128 * jt, 128),
                        abf(R_KT + jt * 256, c_lo, 512 - c_lo),
                        start=(jt == 0), stop=False),
                        waits=([vb, sT_toks[J - 1], pfree.get(psn)] if jt == 0 else ()),
                        inc=None)
                mm = P.op("tensor", lambda e, ec=ec, ps=ps: e.matmul(
                    ps[:, 0:512], sb["bvbf"][0:1, 128 * ec:128 * ec + 128],
                    sb["denbf"][0:1, 0:512], start=False, stop=True),
                    waits=[denbf_tok], inc="te")
                vb_free[slot] = mm
                nu_tok = P.op("vector", lambda e, ec=ec, ps=ps: e.tensor_tensor(
                    abf(R_XBF + ec * 256, 0, 512), ps[:, :], sb["scr_c"][:, :],
                    ALU.mult), waits=[mm, rB], inc="ve")
                pfree[psn] = nu_tok
                if DBG_ATT and Tq == 0:
                    P.dma("gpsimd", ydbg[8 + ec], abf(R_XBF + ec * 256, 0, 512),
                          waits=[nu_tok], inc="wd")
            pfree["scr_c"] = nu_tok
            qb_free = nu_tok
            sT_free = nu_tok
            oo = None
            for dt_ in range(8):
                psn = "ps2" if dt_ % 2 == 0 else "ps3"
                ps = PSAP[psn]
                mm = None
                for kt in range(16):
                    mm = P.op("tensor", lambda e, kt=kt, dt_=dt_, ps=ps: e.matmul(
                        ps[:, :], WSLOT(wa_slots[kt], 128 * dt_, 128),
                        abf(R_XBF + kt * 256, 0, 512),
                        start=(kt == 0), stop=(kt == 15)),
                        waits=([nu_tok, wa_tok, pfree.get(psn)] if kt == 0 else ()),
                        inc=("te" if kt == 15 else None))
                oo = P.op("vector", lambda e, dt_=dt_, qg=qg, ps=ps: e.scalar_tensor_tensor(
                    S_(dt_, 1 + qg, 512), ps[:, :], bcol("bo", dt_),
                    S_(dt_, 1 + qg, 512), ALU.add, ALU.add), waits=[mm], inc="ve")
                pfree[psn] = oo
            at_tok = oo
        pfree["r_w"] = at_tok
        return at_tok

    cur = deps0
    tok = None
    if "ln0" in parts:
        _, tok = emit_ln(False, "stream", cur)
        cur = [tok]
    elif "ffl0" in parts:
        _, xbf_tok = emit_ln(False, "stream", cur)
        tok = emit_f_layer(0, xbf_tok, xbf_tok)
        cur = [tok]
    mts = None
    if "ff1" in parts:
        tok, mts = emit_ffmod(cur)
        cur = [tok]
    if "att" in parts:
        tok = emit_A(cur)
        cur = [tok]
        mts = None
    if "ff2" in parts:
        tok, mts = emit_ffmod(cur)
        cur = [tok]

    out_toks = []
    for dt_ in range(8):
        w = [mts[dt_]] if mts is not None else cur
        eng = "sync" if dt_ % 2 == 0 else "gpsimd"
        out_toks.append(P.dma(eng, y[dt_], S_(dt_, 1, T), waits=w,
                              inc="sd" if eng == "sync" else "wd"))
    P.op("sync", lambda e: e.nop(), waits=out_toks)
    P.op("gpsimd", lambda e: e.nop(), waits=out_toks)

    P.finalize(st)
    st.close()
    return nc


_CACHE = {}


def kernel(**inputs):
    h = host_prepare(inputs)
    key = "full"
    if key not in _CACHE:
        _CACHE[key] = build_program(h)
    nc = _CACHE[key]
    common = dict(
        wq=h["wq"], wk=h["wk"], wv=h["wv"], woA=h["woA"],
        wqkv=h["wqkv"], woF=h["woF"], w1r=h["w1r"], w2p=h["w2p"],
        blob=h["blob"], qsks=h["qsks"], bvrow=h["bvrow"],
        tcq=h["tcq"], tsq=h["tsq"], tck=h["tck"], tsk=h["tsk"],
        mask_l=h["mask_l"], mask_r=h["mask_r"],
        triu01=h["triu01"],
        ident=h["ident"])
    in_maps = [dict(common, xin=h["x_cores"][b]) for b in range(NCORES)]
    res = run_bass_kernel_spmd(nc, in_maps, core_ids=list(range(NCORES)))
    out = np.empty((NCORES, T, D), dtype=np.float32)
    for b in range(NCORES):
        yb = res.results[b]["y"]
        out[b] = yb.reshape(D, T).T
    return out


# version 43
# speedup vs baseline: 1.0174x; 1.0172x over previous
import numpy as np
import ml_dtypes
from contextlib import ExitStack

import concourse.bass as bass
import concourse.mybir as mybir
from concourse.bass_utils import run_bass_kernel_spmd

F32 = mybir.dt.float32
BF16 = mybir.dt.bfloat16
AF = mybir.ActivationFunctionType
ALU = mybir.AluOpType
AX = mybir.AxisListType

NCORES = 8
T = 2048
TG = T + 1
D = 1024
DI = 2048
NH = 8
DH = 64
WIN = 512
HID = 2730
NPAIR = 22
HIDP = NPAIR * 128
EPS_LN = 1e-5
EPS_LA = 1e-6
BIG = 3.0e38

ENGINES = ("sync", "gpsimd", "tensor", "vector", "scalar")


class Prog:
    def __init__(self, nc):
        self.nc = nc
        self.stream = {e: [] for e in ENGINES}
        self.semh = {}
        self.semc = {}
        self.last_wait = {}

    def sem(self, name, stack):
        self.semh[name] = stack.enter_context(self.nc.semaphore(name))
        self.semc[name] = 0

    def tok(self, name):
        return (name, self.semc[name])

    def op(self, eng, build, waits=(), inc=None, inck=1):
        ws = []
        for t in waits:
            if t is None:
                continue
            s, v = t
            if v <= 0:
                continue
            key = (eng, s)
            if self.last_wait.get(key, -1) >= v:
                continue
            self.last_wait[key] = v
            ws.append((s, v))
        token = None
        if inc is not None:
            self.semc[inc] += inck
            token = (inc, self.semc[inc])
        semh = self.semh

        def f(e, ws=tuple(ws), build=build, inc=inc, inck=inck):
            for s, v in ws:
                e.wait_ge(semh[s], v)
            inst = build(e)
            if inc is not None:
                inst.then_inc(semh[inc], inck)
        self.stream[eng].append(f)
        return token

    def dma(self, eng, out, in_, waits=(), inc="sd"):
        return self.op(eng, lambda e: e.dma_start(out=out, in_=in_), waits=waits,
                       inc=inc, inck=16)

    def finalize(self, stack):
        blk = stack.enter_context(self.nc.Block())
        streams = self.stream

        @blk.sync
        def _(e):
            for f in streams["sync"]:
                f(e)

        @blk.gpsimd
        def _(e):
            for f in streams["gpsimd"]:
                f(e)

        @blk.tensor
        def _(e):
            for f in streams["tensor"]:
                f(e)

        @blk.vector
        def _(e):
            for f in streams["vector"]:
                f(e)

        @blk.scalar
        def _(e):
            for f in streams["scalar"]:
                f(e)


def build_constants():
    c = {}
    half = np.arange(0, DH, 2, dtype=np.float64)
    inv_freq = 1.0 / 10000.0 ** (half / DH)
    t = np.arange(T, dtype=np.float64)
    fr = t[:, None] * inv_freq[None, :]
    fr = np.concatenate([fr, fr], axis=1)
    base = (half + 0.4 * DH) / (1.4 * DH)
    power = (t - T // 2) / float(WIN // 2)
    sc = base[None, :] ** power[:, None]
    sc = np.concatenate([sc, sc], axis=1)
    sign = np.where(np.arange(DH) < 32, -1.0, 1.0)[None, :]
    cq = 8.0 * np.cos(fr) * sc
    sq = 8.0 * np.sin(fr) * sc * sign
    ck = np.cos(fr) / sc
    sk = np.sin(fr) / sc * sign

    def tmajor(a):
        return np.ascontiguousarray(
            a.reshape(16, 128, DH).transpose(1, 0, 2).reshape(128, 16 * DH)
        ).astype(np.float32)

    c["tcq"], c["tsq"], c["tck"], c["tsk"] = map(tmajor, (cq, sq, ck, sk))
    r = np.arange(128)[:, None]
    cc = np.arange(128)[None, :]
    c["mask_l"] = np.where(cc < r, -BIG, 0.0).astype(np.float32)
    c["mask_r"] = np.where(cc > r, -BIG, 0.0).astype(np.float32)
    c["mask_lr"] = np.concatenate([c["mask_l"], c["mask_r"]], axis=1)
    c["triu01"] = (cc >= r).astype(ml_dtypes.bfloat16)
    c["ident"] = np.eye(128, dtype=ml_dtypes.bfloat16)
    return c


def build_vecblob(w):
    idx = {}
    cols = []

    def put(name, cols_list):
        idx[name] = len(cols)
        cols.extend(cols_list)

    def chunks(v, n):
        v = np.asarray(v, dtype=np.float32).ravel()
        v = np.pad(v, (0, n * 128 - v.size))
        return [v[i * 128:(i + 1) * 128] for i in range(n)]

    put("bq", chunks(w["L0_attn_bq"], 16))
    put("bk", chunks(w["L0_attn_bk"], 16))
    put("bo", chunks(w["L0_attn_bo"], 8))
    for l in range(2):
        put(f"ln_g{l}", chunks(w["L0_ffn_ln_g"][l], 8))
        put(f"ln_b{l}", chunks(w["L0_ffn_ln_b"][l], 8))
        b1 = np.asarray(w["L0_ffn_b1"][l], dtype=np.float32)
        put(f"b1a{l}", chunks(np.pad(b1[:HID], (0, HIDP - HID)), NPAIR))
        put(f"b1g{l}", chunks(np.pad(b1[HID:], (0, HIDP - HID)), NPAIR))
        put(f"b2{l}", chunks(w["L0_ffn_b2"][l], 8))
    blob = np.stack(cols, axis=1).astype(np.float32)
    return blob, idx


def host_prepare(inputs):
    h = {}
    x = np.asarray(inputs["L0_attn_x"], dtype=np.float32)
    xs = []
    for b in range(NCORES):
        xt = np.zeros((D, TG), dtype=np.float32)
        xt[:, 1:] = x[b].T
        xs.append(np.ascontiguousarray(xt.reshape(8, 128, TG)))
    h["x_cores"] = xs

    h["wq"] = np.ascontiguousarray(inputs["L0_attn_Wq"], dtype=np.float32)
    h["wk"] = np.ascontiguousarray(inputs["L0_attn_Wk"], dtype=np.float32)
    h["wv"] = np.ascontiguousarray(inputs["L0_attn_Wv"], dtype=np.float32)
    h["woA"] = np.ascontiguousarray(inputs["L0_attn_Wo"], dtype=np.float32)
    h["wqkv"] = np.ascontiguousarray(inputs["L0_ffn_wqkv"], dtype=np.float32)
    h["woF"] = np.ascontiguousarray(inputs["L0_ffn_wo"], dtype=np.float32)

    w1 = np.asarray(inputs["L0_ffn_w1"], dtype=np.float32)
    w1r = np.zeros((2, NPAIR, 128, 2048), np.float32)
    for l in range(2):
        w1a = np.zeros((D, HIDP), np.float32)
        w1g = np.zeros((D, HIDP), np.float32)
        w1a[:, :HID] = w1[l][:, :HID]
        w1g[:, :HID] = w1[l][:, HID:]
        for p in range(NPAIR):
            blk = np.concatenate([w1a[:, 128 * p:128 * p + 128],
                                  w1g[:, 128 * p:128 * p + 128]], axis=1)
            w1r[l, p] = blk.reshape(8, 128, 256).transpose(1, 0, 2).reshape(128, 2048)
    h["w1r"] = np.ascontiguousarray(w1r)

    w2 = np.asarray(inputs["L0_ffn_w2"], dtype=np.float32)
    w2p = np.zeros((2, HIDP, D), np.float32)
    w2p[:, :HID, :] = w2
    h["w2p"] = np.ascontiguousarray(w2p)

    h["qsks"] = np.concatenate([
        np.asarray(inputs["L0_ffn_q_scale"], np.float32),
        np.asarray(inputs["L0_ffn_k_scale"], np.float32)], axis=1).reshape(1, 256)
    h["bvrow"] = np.asarray(inputs["L0_attn_bv"], np.float32).reshape(1, 2048)

    h["blob"], h["blob_idx"] = build_vecblob(inputs)
    h.update(build_constants())
    return h


DBG_ATT = False
DBG_F = False


def build_program(h, parts=("ff1", "att", "ff2")):
    nc = bass.Bass("TRN2", target_bir_lowering=False)
    P = Prog(nc)
    st = ExitStack()

    dr = {}

    def din(name, shape, dt=F32):
        dr[name] = nc.dram_tensor(name, shape, dt, kind="ExternalInput").ap()

    din("xin", [8, 128, TG])
    din("wq", [D, DI]); din("wk", [D, DI]); din("wv", [D, DI]); din("woA", [DI, D])
    din("wqkv", [2, D, 1536]); din("woF", [2, 512, D])
    din("w1r", [2, NPAIR, 128, 2048]); din("w2p", [2, HIDP, D])
    nb = h["blob"].shape[1]
    din("blob", [128, nb])
    din("qsks", [1, 256]); din("bvrow", [1, 2048])
    din("tcq", [128, 1024]); din("tsq", [128, 1024])
    din("tck", [128, 1024]); din("tsk", [128, 1024])
    din("mask_l", [128, 128]); din("mask_r", [128, 128])
    din("triu01", [128, 128], BF16); din("ident", [128, 128], BF16)
    y = nc.dram_tensor("y", [8, 128, T], F32, kind="ExternalOutput").ap()
    ydbg = (nc.dram_tensor("ydbg", [48, 128, 512], F32, kind="ExternalOutput").ap()
            if (DBG_ATT or DBG_F) else None)
    xo_scr = nc.dram_tensor("xo_scr", [8, 128, TG], F32, kind="Internal").ap()
    q_scr = nc.dram_tensor("q_scr", [16, 128, T], BF16, kind="Internal").ap()
    k_scr = nc.dram_tensor("k_scr", [16, 16, 128, 128], BF16, kind="Internal").ap()
    v_scr = nc.dram_tensor("v_scr", [16, 16, 128, 128], BF16, kind="Internal").ap()

    for s in ("wd", "sd", "te", "ve", "ac", "ge"):
        P.sem(s, st)

    sb = {}

    def stile(name, shape, dt=F32):
        sb[name] = st.enter_context(nc.sbuf_tensor("sb_" + name, shape, dt))

    stile("ident", [128, 128], BF16)
    stile("mask_l", [128, 128]); stile("mask_r", [128, 128])
    stile("triu01", [128, 128], BF16)
    stile("tcq", [128, 1024], BF16); stile("tsq", [128, 1024], BF16)
    stile("tck", [128, 1024], BF16); stile("tsk", [128, 1024], BF16)
    stile("blob", [128, nb])
    stile("qsks", [1, 256])
    stile("ones_col", [128, 1]); stile("ones_col_bf", [128, 1], BF16)
    stile("ones_row", [1, 128]); stile("ones_row_bf", [1, 128], BF16)
    stile("rowbf", [1, 1024], BF16)
    stile("bvbf", [1, 2048], BF16); stile("denbf", [1, 512], BF16)
    stile("qsB", [128, 64]); stile("ksB", [128, 64])
    stile("rn2", [128, 32]); stile("eps24", [128, 1])
    stile("rowstat", [1, 1024]); stile("rowstat2", [1, 512])
    stile("nmax", [128, 16]); stile("nmax2", [128, 16])
    stile("rsum", [128, 16]); stile("rsum2", [128, 16]); stile("rrec", [128, 16])
    stile("scr_a", [128, 512]); stile("scr_b", [128, 512])
    stile("scr_c", [128, 512])
    stile("qf", [128, 2048], BF16); stile("kf", [128, 2048], BF16)
    stile("diag", [128, 1024], BF16)

    NCOLS = 42000
    st.enter_context(nc.sbuf_tensor("arena_pad", [128, 183], F32))
    arena = st.enter_context(nc.sbuf_tensor("arena", [128, NCOLS], F32))
    ar_t = arena[:, :].tensor
    ar_tb = arena[:, :].bitcast(BF16).tensor

    def af32(col, n):
        assert 0 <= col and col + n <= NCOLS, (col, n)
        return bass.AP(ar_t, col, [[NCOLS, 128], [1, n]])

    def abf(col32, e0, n, pbase=0, pn=128):
        assert 0 <= col32 and col32 * 2 + e0 + n <= NCOLS * 2, (col32, e0, n)
        return bass.AP(ar_tb, col32 * 2 + e0 + pbase * NCOLS * 2,
                       [[NCOLS * 2, pn], [1, n]])

    R_S = 0
    R_XBF = 16392
    R_W = R_XBF + 8200
    R_QT = R_W + 5120
    R_KT = R_QT + 4096
    R_VT = R_KT + 4096
    assert R_VT + 4096 <= NCOLS

    def S_(dt_, c0, n):
        return af32(R_S + dt_ * TG + c0, n)

    def XBF_(dt_, c0, n):
        return abf(R_XBF + dt_ * 1025, c0, n)

    def QT_(fs, c0, n, pbase=0, pn=128):
        return abf(R_QT + fs * 1024, c0, n, pbase, pn)

    def KT_(fs, c0, n, pbase=0, pn=128):
        return abf(R_KT + fs * 1024, c0, n, pbase, pn)

    def VT_(tt, c0, n):
        return abf(R_VT + tt * 256, c0, n)

    def AOUT_(pair, c0, n, pbase=0, pn=128):
        return abf(R_XBF + pair * 1024, c0, n, pbase, pn)

    def WSLOT(col32, e0, n):
        return abf(col32, e0, n)

    def AN(e0, n):
        return abf(R_W, e0, n)

    def ATT(e0, n):
        return abf(R_W + 1280, e0, n)

    STG = None

    PS = [st.enter_context(nc.psum_tensor(f"ps{i}", [128, 512], F32)) for i in range(4)]
    PD = [st.enter_context(nc.psum_tensor(f"pd{i}", [128, 1024], F32)) for i in range(2)]
    PSN = ["ps0", "ps1", "ps2", "ps3", "pd0", "pd1"]
    PSAP = {"ps0": PS[0], "ps1": PS[1], "ps2": PS[2], "ps3": PS[3],
            "pd0": PD[0], "pd1": PD[1]}
    pfree = {}

    bidx = h["blob_idx"]

    def bcol(name, i=0):
        return sb["blob"][:, bidx[name] + i: bidx[name] + i + 1]

    P.dma("sync", sb["ident"][:, :], dr["ident"])
    P.dma("sync", sb["mask_l"][:, :], dr["mask_l"])
    P.dma("sync", sb["mask_r"][:, :], dr["mask_r"])
    P.dma("sync", sb["triu01"][:, :], dr["triu01"])
    P.dma("sync", sb["blob"][:, :], dr["blob"])
    P.dma("sync", sb["qsks"][:, :], dr["qsks"])
    cst_tok = P.dma("gpsimd", sb["bvbf"][0:1, :], dr["bvrow"], inc="wd")
    for nm in ("tcq", "tsq", "tck", "tsk"):
        tab_tok = P.dma("gpsimd", sb[nm][:, :], dr[nm], inc="wd")
    xin_toks = [P.dma("sync", S_(dt_, 0, TG), dr["xin"][dt_]) for dt_ in range(8)]
    P.op("vector", lambda e: e.memset(sb["ones_col"][:, :], 1.0))
    P.op("vector", lambda e: e.memset(sb["ones_col_bf"][:, :], 1.0))
    P.op("vector", lambda e: e.memset(sb["ones_row_bf"][:, :], 1.0))
    P.op("vector", lambda e: e.memset(sb["eps24"][:, :], 1e-24))
    ones_tok = P.op("vector", lambda e: e.memset(sb["ones_row"][:, :], 1.0), inc="ve")

    def scrbf(name):
        t = sb[name][:, :].bitcast(BF16).tensor
        return bass.AP(t, 0, [[1024, 128], [1, 512]])

    def emit_ln(shift, mode, deps):
        deps = [d for d in deps if d is not None]
        c0s = [0 if (shift and dt_ >= 4) else 1 for dt_ in range(8)]
        spill_toks = [None] * 8
        if mode == "stream":
            for dt_ in range(8):
                eng = "sync" if dt_ % 2 == 0 else "gpsimd"
                spill_toks[dt_] = P.dma(eng, xo_scr[dt_], S_(dt_, 0, TG),
                                        waits=deps,
                                        inc="sd" if eng == "sync" else "wd")
        xbf_tok = None
        f32_tok = None
        for tc in reversed(range(4)):
            cast_toks = [None] * 8
            for dt_ in range(8):
                if dt_ % 2 == 0:
                    cast_toks[dt_] = P.op("vector", lambda e, dt_=dt_, tc=tc: e.tensor_copy(
                        XBF_(dt_, c0s[dt_] + tc * 512, 512),
                        S_(dt_, c0s[dt_] + tc * 512, 512)),
                        waits=deps, inc="ve")
                else:
                    cast_toks[dt_] = P.op("scalar", lambda e, dt_=dt_, tc=tc: e.activation(
                        XBF_(dt_, c0s[dt_] + tc * 512, 512),
                        S_(dt_, c0s[dt_] + tc * 512, 512), AF.Copy),
                        waits=deps, inc="ac")
            w_ps2 = deps + [pfree.get("ps2")]
            w_ps3 = deps + [pfree.get("ps3")]
            t_sum = None
            t_ssq = None
            for dt_ in range(8):
                sn = "scr_a" if dt_ % 2 == 0 else "scr_b"
                t_sq = P.op("scalar", lambda e, dt_=dt_, tc=tc, sn=sn: e.activation(
                    scrbf(sn), S_(dt_, c0s[dt_] + tc * 512, 512), AF.Square),
                    waits=deps + [pfree.get(sn)], inc="ac")
                t_sum = P.op("tensor", lambda e, dt_=dt_, tc=tc: e.matmul(
                    PS[2][0:1, :], sb["ones_col_bf"][:, :],
                    XBF_(dt_, c0s[dt_] + tc * 512, 512),
                    start=(dt_ == 0), stop=(dt_ == 7)),
                    waits=[cast_toks[dt_]] + (w_ps2 if dt_ == 0 else []),
                    inc=("te" if dt_ == 7 else None))
                t_ssq = P.op("tensor", lambda e, dt_=dt_, sn=sn: e.matmul(
                    PS[3][0:1, :], sb["ones_col_bf"][:, :], scrbf(sn),
                    start=(dt_ == 0), stop=(dt_ == 7)),
                    waits=[t_sq] + (w_ps3 if dt_ == 0 else []), inc="te")
                pfree[sn] = t_ssq
            rs = sb["rowstat"][0:1, 0:512]
            rs2 = sb["rowstat2"][0:1, 0:512]
            t1 = P.op("vector", lambda e, rs2=rs2: e.tensor_scalar(
                rs2, PS[2][0:1, :], -1.0 / D, None, ALU.mult),
                waits=[t_sum, pfree.get("rowmath")], inc="ve")
            pfree["ps2"] = t1
            t2 = P.op("vector", lambda e, rs=rs, rs2=rs2: e.tensor_tensor(
                rs, rs2, rs2, ALU.mult), waits=[t1], inc="ve")
            t3 = P.op("vector", lambda e, rs=rs: e.scalar_tensor_tensor(
                rs, rs, -float(D), PS[3][0:1, :], ALU.mult, ALU.add),
                waits=[t2, t_ssq], inc="ve")
            pfree["ps3"] = t3
            t3b = P.op("vector", lambda e, rs=rs: e.tensor_scalar(
                rs, rs, 1.0 / D, float(EPS_LN), ALU.mult, ALU.add),
                waits=[t3], inc="ve")
            t4 = P.op("scalar", lambda e, rs=rs: e.activation(
                rs, rs, AF.Ln), waits=[t3b], inc="ac")
            t5 = P.op("scalar", lambda e, rs=rs: e.activation(
                rs, rs, AF.Exp, scale=-0.5), waits=[t4], inc="ac")
            if mode == "stream":
                tA = P.op("tensor", lambda e, rs=rs: e.matmul(
                    PD[0][:, 0:512], sb["ones_row"][:, :], rs, start=True, stop=True),
                    waits=[t5, ones_tok, pfree.get("pd0")], inc="te")
                tB = P.op("tensor", lambda e, rs2=rs2: e.matmul(
                    PD[1][:, 0:512], sb["ones_row"][:, :], rs2, start=True, stop=True),
                    waits=[t1, pfree.get("pd1")], inc="te")
            else:
                tbf1 = P.op("vector", lambda e, rs=rs: e.tensor_copy(
                    sb["rowbf"][0:1, 0:512], rs),
                    waits=[t5, pfree.get("rowbf")], inc="ve")
                tbf2 = P.op("vector", lambda e, rs2=rs2: e.tensor_copy(
                    sb["rowbf"][0:1, 512:1024], rs2), waits=[t1], inc="ve")
                tA = P.op("tensor", lambda e: e.matmul(
                    PD[0][:, 0:512], sb["ones_row_bf"][:, :],
                    sb["rowbf"][0:1, 0:512], start=True, stop=True),
                    waits=[tbf1, ones_tok, pfree.get("pd0")], inc="te")
                tB = P.op("tensor", lambda e: e.matmul(
                    PD[1][:, 0:512], sb["ones_row_bf"][:, :],
                    sb["rowbf"][0:1, 512:1024], start=True, stop=True),
                    waits=[tbf2, pfree.get("pd1")], inc="te")
                pfree["rowbf"] = tB
            pfree["rowmath"] = tB
            for dt_ in range(8):
                p1 = P.op("vector", lambda e, dt_=dt_, tc=tc: e.tensor_tensor(
                    sb["scr_b"][:, :], S_(dt_, c0s[dt_] + tc * 512, 512),
                    PD[1][:, 0:512], ALU.add),
                    waits=[tB, pfree.get("scr_b")], inc="ve")
                if mode == "stream":
                    p2 = P.op("vector", lambda e, dt_=dt_, tc=tc: e.tensor_tensor(
                        S_(dt_, 1 + tc * 512, 512), sb["scr_b"][:, :],
                        PD[0][:, 0:512], ALU.mult),
                        waits=[tA, p1, spill_toks[dt_]], inc="ve")
                    f32_tok = p2
                    p2 = P.op("vector", lambda e, dt_=dt_, tc=tc: e.tensor_copy(
                        XBF_(dt_, 1 + tc * 512, 512), S_(dt_, 1 + tc * 512, 512)),
                        waits=[p2], inc="ve")
                elif mode == "bf":
                    p2 = P.op("vector", lambda e, dt_=dt_, tc=tc: e.tensor_tensor(
                        XBF_(dt_, 1 + tc * 512, 512), sb["scr_b"][:, :],
                        PD[0][:, 0:512], ALU.mult),
                        waits=[tA, p1], inc="ve")
                else:
                    l = int(mode.split(":")[1])
                    p2 = P.op("vector", lambda e, dt_=dt_, tc=tc: e.tensor_tensor(
                        sb["scr_c"][:, :], sb["scr_b"][:, :], PD[0][:, 0:512],
                        ALU.mult), waits=[tA, p1, pfree.get("scr_c")], inc="ve")
                    p2 = P.op("vector", lambda e, dt_=dt_, tc=tc, l=l: e.tensor_scalar(
                        XBF_(dt_, 1 + tc * 512, 512), sb["scr_c"][:, :],
                        bcol(f"ln_g{l}", dt_), bcol(f"ln_b{l}", dt_),
                        ALU.mult, ALU.add), waits=[p2], inc="ve")
                    pfree["scr_c"] = p2
                pfree["scr_b"] = p2
                xbf_tok = p2
            pfree["pd0"] = xbf_tok
            pfree["pd1"] = xbf_tok
        if mode == "stream":
            for dt_ in range(8):
                xbf_tok = P.op("vector", lambda e, dt_=dt_: e.tensor_copy(
                    XBF_(dt_, 0, 1), S_(dt_, 0, 1)), waits=deps, inc="ve")
        return f32_tok, xbf_tok

    def emit_f_layer(l, xbf_tok, stream_tok):
        wqk_toks = []
        for kt in range(8):
            wqk_toks.append(P.dma(
                "gpsimd", WSLOT(R_W + kt * 512, 0, 1024),
                dr["wqkv"][l][kt * 128:(kt + 1) * 128, 0:1024],
                waits=[pfree.get("r_w")], inc="wd"))
        wqk_tok = wqk_toks[-1]
        tq1 = P.op("tensor", lambda e: e.matmul(
            PD[0][:, 0:64], sb["ones_row"][:, :],
            sb["qsks"][0:1, 128 * l:128 * l + 64], start=True, stop=True),
            waits=[ones_tok, cst_tok, pfree.get("pd0")], inc="te")
        tq2 = P.op("vector", lambda e: e.tensor_copy(sb["qsB"][:, :], PD[0][:, 0:64]),
                   waits=[tq1], inc="ve")
        tk1 = P.op("tensor", lambda e: e.matmul(
            PD[0][:, 64:128], sb["ones_row"][:, :],
            sb["qsks"][0:1, 128 * l + 64:128 * l + 128], start=True, stop=True),
            waits=[tq2], inc="te")
        tk2 = P.op("vector", lambda e: e.tensor_copy(sb["ksB"][:, :], PD[0][:, 64:128]),
                   waits=[tk1], inc="ve")
        pfree["pd0"] = tk2
        qsb_tok = tk2

        qsw = bass.AP(sb["qsB"][:, :].tensor, 32, [[64, 128], [0, 16], [-32, 2], [1, 32]])
        ksw = bass.AP(sb["ksB"][:, :].tensor, 32, [[64, 128], [0, 16], [-32, 2], [1, 32]])
        qsb_b = bass.AP(sb["qsB"][:, :].tensor, 0, [[64, 128], [0, 16], [1, 64]])
        ksb_b = bass.AP(sb["ksB"][:, :].tensor, 0, [[64, 128], [0, 16], [1, 64]])
        tab_build = None
        for srcnm, mul, tofs, swap in (("tcq", qsb_b, 0, False), ("tsq", qsw, 1024, True),
                                       ("tck", ksb_b, 2048, False), ("tsk", ksw, 3072, True)):
            tcp = P.op("vector", lambda e, srcnm=srcnm, tofs=tofs: e.tensor_copy(
                af32(R_VT + tofs, 1024), sb[srcnm][:, :]),
                waits=[tab_tok, qsb_tok, pfree.get("vtreg")], inc="ve")
            if swap:
                tab_build = P.op("vector", lambda e, mul=mul, tofs=tofs: e.tensor_tensor(
                    bass.AP(ar_t, R_VT + tofs, [[NCOLS, 128], [64, 16], [32, 2], [1, 32]]),
                    bass.AP(ar_t, R_VT + tofs, [[NCOLS, 128], [64, 16], [32, 2], [1, 32]]),
                    mul, ALU.mult), waits=[tcp], inc="ve")
            else:
                tab_build = P.op("vector", lambda e, mul=mul, tofs=tofs: e.tensor_tensor(
                    bass.AP(ar_t, R_VT + tofs, [[NCOLS, 128], [64, 16], [1, 64]]),
                    bass.AP(ar_t, R_VT + tofs, [[NCOLS, 128], [64, 16], [1, 64]]),
                    mul, ALU.mult), waits=[tcp], inc="ve")
        pfree["vtreg"] = None

        tr_evac = None
        last_qk_mm = None
        def halfbf(name, half):
            t = sb[name][:, :].bitcast(BF16).tensor
            return bass.AP(t, 512 * half, [[1024, 128], [1, 512]])

        sq_bufs = [
            (halfbf("scr_a", 0), halfbf("scr_b", 0), halfbf("scr_c", 0),
             "scr_a", "scr_b", "scr_c"),
            (halfbf("scr_a", 1), halfbf("scr_b", 1), halfbf("scr_c", 1),
             "scr_ao", "scr_bo", "scr_co"),
        ]

        def ap3(ap):
            return bass.AP(ap.tensor, ap.offset, [[ap.ap[0][0], 128], [64, 8], [1, 64]])

        for g4 in range(4):
            for ts in range(4):
                tt = g4 * 4 + ts
                par = tt % 2
                bqa, bka, bca, kqa, kka, kca = sq_bufs[par]
                rb0 = 16 * par
                qpsn, kpsn = ("ps0", "ps2") if par == 0 else ("ps1", "ps3")
                qps, kps = PSAP[qpsn], PSAP[kpsn]
                for ci, psn, ps in ((0, qpsn, qps), (1, kpsn, kps)):
                    w0 = [xbf_tok, wqk_tok, pfree.get(psn)]
                    for kt in range(8):
                        c0 = (1 if kt < 4 else 0) + 128 * tt
                        last_qk_mm = P.op("tensor", lambda e, kt=kt, c0=c0, ci=ci, ps=ps: e.matmul(
                            ps[:, :], XBF_(kt, c0, 128),
                            WSLOT(R_W + kt * 512, ci * 512, 512),
                            start=(kt == 0), stop=(kt == 7)),
                            waits=(w0 if kt == 0 else ()),
                            inc=("te" if kt == 7 else None))
                    if ci == 0:
                        qmm = last_qk_mm
                    else:
                        kmm = last_qk_mm
                sqq = P.op("scalar", lambda e, qps=qps, bqa=bqa: e.activation(
                    bqa, qps[:, :], AF.Square),
                    waits=[qmm, pfree.get(kqa)], inc="ac")
                sqk = P.op("scalar", lambda e, kps=kps, bka=bka: e.activation(
                    bka, kps[:, :], AF.Square),
                    waits=[kmm, pfree.get(kka)], inc="ac")
                rq = P.op("vector", lambda e, bqa=bqa, rb0=rb0: e.tensor_reduce(
                    sb["rn2"][:, rb0:rb0 + 8], ap3(bqa),
                    AX.X, ALU.add), waits=[sqq], inc="ve")
                rk = P.op("vector", lambda e, bka=bka, rb0=rb0: e.tensor_reduce(
                    sb["rn2"][:, rb0 + 8:rb0 + 16], ap3(bka),
                    AX.X, ALU.add), waits=[sqk], inc="ve")
                pfree[kqa] = rq
                pfree[kka] = rk
                rr = P.op("scalar", lambda e, rb0=rb0: e.activation(
                    sb["rn2"][:, rb0:rb0 + 16], sb["rn2"][:, rb0:rb0 + 16],
                    AF.Ln, bias=sb["eps24"][:, :]),
                    waits=[rq, rk], inc="ac")
                rr = P.op("scalar", lambda e, rb0=rb0: e.activation(
                    sb["rn2"][:, rb0:rb0 + 16], sb["rn2"][:, rb0:rb0 + 16],
                    AF.Exp, scale=-0.5),
                    waits=[rr], inc="ac")
                outs = []
                for half, tofs, dst, wkey, psx, rnlo in (
                        (0, 0, sb["qf"], "qfw", qps, 0),
                        (1, 2048, sb["kf"], "kfw", kps, 8)):
                    dstap = dst[:, :]
                    t1 = P.op("vector", lambda e, psx=psx, tofs=tofs, tt=tt, bca=bca: e.tensor_tensor(
                        ap3(bca),
                        psx[:, :].rearrange("p (h d) -> p h d", h=8),
                        bass.AP(ar_t, R_VT + tofs + tt * 64, [[NCOLS, 128], [0, 8], [1, 64]]),
                        ALU.mult), waits=[tab_build, pfree.get(kca) if half == 0 else None],
                        inc="ve")
                    t2 = P.op("vector", lambda e, psx=psx, tofs=tofs, tt=tt, ts=ts,
                              dstap=dstap: e.tensor_tensor(
                        bass.AP(dstap.tensor, ts * 512, [[2048, 128], [64, 8], [32, 2], [1, 32]]),
                        bass.AP(psx[:, :].tensor, 32,
                                [[512, 128], [64, 8], [-32, 2], [1, 32]]),
                        bass.AP(ar_t, R_VT + tofs + 1024 + tt * 64,
                                [[NCOLS, 128], [0, 8], [32, 2], [1, 32]]),
                        ALU.mult), waits=[pfree.get(wkey)], inc="ve")
                    t3 = P.op("vector", lambda e, dst=dst, ts=ts, bca=bca: e.tensor_tensor(
                        dst[:, ts * 512:(ts + 1) * 512], dst[:, ts * 512:(ts + 1) * 512],
                        bca, ALU.add), waits=[t1, t2], inc="ve")
                    pfree[psx is qps and qpsn or kpsn] = t3
                    t4 = P.op("vector", lambda e, dst=dst, ts=ts, rnlo=rnlo, rb0=rb0: e.tensor_tensor(
                        dst[:, ts * 512:(ts + 1) * 512].rearrange("p (h d) -> p h d", h=8),
                        dst[:, ts * 512:(ts + 1) * 512].rearrange("p (h d) -> p h d", h=8),
                        sb["rn2"][:, rb0 + rnlo:rb0 + rnlo + 8].broadcast_to((128, 8, DH)),
                        ALU.mult),
                        waits=[t3, rr], inc="ve")
                    outs.append(t4)
                pfree[kca] = outs[1]
                qf_tok, kf_tok = outs
            for srcbuf, dstfn, ready, wkey in ((sb["qf"], QT_, qf_tok, "qfw"),
                                               (sb["kf"], KT_, kf_tok, "kfw")):
                for fs in range(4):
                    tmm = None
                    for ts in range(4):
                        tmm = P.op("tensor", lambda e, fs=fs, ts=ts, srcbuf=srcbuf: e.matmul(
                            PD[1][:, ts * 128:(ts + 1) * 128],
                            srcbuf[:, ts * 512 + fs * 128: ts * 512 + fs * 128 + 128],
                            sb["ident"][:, :], start=(ts == 0), stop=(ts == 3)),
                            waits=([ready, pfree.get("pd1")] if ts == 0 else ()),
                            inc=("te" if ts == 3 else None))
                    tev = P.op("vector", lambda e, fs=fs, g4=g4, dstfn=dstfn: e.tensor_copy(
                        dstfn(fs, g4 * 512, 512), PD[1][:, 0:512]),
                        waits=[tmm], inc="ve")
                    pfree["pd1"] = tev
                pfree[wkey] = tmm
            tr_evac = tev
        qkT_tok = tr_evac
        for k0, k1 in (("scr_a", "scr_ao"), ("scr_b", "scr_bo"),
                       ("scr_c", "scr_co")):
            a, b = pfree.get(k0), pfree.get(k1)
            if a is None or (b is not None and b[1] > a[1]):
                pfree[k0] = b

        wv_toks = []
        for kt in range(8):
            wv_toks.append(P.dma(
                "gpsimd", WSLOT(R_W + kt * 256, 0, 512),
                dr["wqkv"][l][kt * 128:(kt + 1) * 128, 1024:1536],
                waits=[qkT_tok if kt == 0 else None], inc="wd"))
        wv_tok = wv_toks[-1]
        vt_tok = None
        for tt in range(16):
            psn = "ps0" if tt % 2 == 0 else "ps1"
            ps = PSAP[psn]
            w0 = [xbf_tok, wv_tok, pfree.get(psn), qkT_tok]
            vmm = None
            for kt in range(8):
                c0 = (1 if kt < 4 else 0) + 128 * tt
                vmm = P.op("tensor", lambda e, kt=kt, c0=c0, ps=ps: e.matmul(
                    ps[:, :], XBF_(kt, c0, 128), WSLOT(R_W + kt * 256, 0, 512),
                    start=(kt == 0), stop=(kt == 7)),
                    waits=(w0 if kt == 0 else ()), inc=("te" if kt == 7 else None))
            vt_tok = P.op("vector", lambda e, tt=tt, ps=ps: e.tensor_copy(
                VT_(tt, 0, 512), ps[:, :]), waits=[vmm], inc="ve")
            pfree[psn] = vt_tok

        if DBG_F and l == 0:
            for fs in range(4):
                P.dma("gpsimd", ydbg[fs], QT_(fs, 0, 512), waits=[qkT_tok], inc="wd")
                P.dma("gpsimd", ydbg[4 + fs], KT_(fs, 0, 512), waits=[qkT_tok], inc="wd")
            for ttd in range(4):
                P.dma("gpsimd", ydbg[8 + ttd], VT_(ttd, 0, 512), waits=[vt_tok], inc="wd")

        AN_BASE = (R_W, R_W + 1280, R_XBF + 4096, R_XBF + 5376)
        ATT_BASE = (R_XBF + 6656, R_XBF + 7168)

        def ANp(par, e0, n):
            return abf(AN_BASE[par], e0, n)

        def ATTp(par, e0, n):
            return abf(ATT_BASE[par], e0, n)

        ao_tok = None
        prev_ex = [[None] * 4 for _ in range(4)]
        prev_rc = [[None] * 4 for _ in range(4)]
        anfree = [[None] * 4 for _ in range(4)]
        state_attfree = {}
        seq = [(hh, g) for g in range(4) for hh in range(NH)]
        saved = None
        pfree["a_s0"] = pfree["a_s1"] = vt_tok
        pfree["a_s2"] = pfree.get("ps2")
        pfree["a_s3"] = pfree.get("ps3")
        for s in range(4):
            pfree[f"a_x{s}"] = pfree.get("pd0")
        pfree["a_td0"] = pfree.get("pd0")
        pfree["a_td1"] = pfree.get("pd1")
        pfree["a_pv"] = pfree.get("pd1")

        def emit_scores(hh, g, par4):
            pair, rb = hh // 2, 64 * (hh % 2)
            info = {}
            exts = [s for s in range(4) if 512 * g + 128 * s >= WIN]
            for s in range(4):
                qg = 512 * g + 128 * s
                kstart = max(0, qg - WIN)
                klm = min(512, qg + 128 - kstart)
                has_ext = qg >= WIN
                ps = PS[s]
                smm = P.op("tensor", lambda e, pair=pair, rb=rb, qg=qg,
                           kstart=kstart, klm=klm, ps=ps: e.matmul(
                    ps[:, 0:klm], QT_(pair, qg, 128, rb, 64),
                    KT_(pair, kstart, klm, rb, 64), start=True, stop=True),
                    waits=[qkT_tok, pfree.get(f"a_s{s}")], inc="te")
                xmm = None
                if has_ext:
                    first_ext = s == exts[0]
                    xw = ([pfree.get(f"a_x{t}") for t in range(4)]
                          if first_ext else [pfree.get(f"a_x{s}")])
                    xmm = P.op("tensor", lambda e, pair=pair, rb=rb, qg=qg, s=s,
                               st=first_ext, sp=(s == exts[-1]): e.matmul(
                        PD[0][:, 128 * s:128 * s + 128], QT_(pair, qg, 128, rb, 64),
                        KT_(pair, qg, 128, rb, 64), start=st, stop=sp),
                        waits=xw, inc="te")
                info[s] = (qg, kstart, klm, has_ext, smm, xmm, ps)
            return info, exts

        def emit_softmax(hh, g, par4, info, exts, an_toks, an_rdy):
            xmm_last = info[exts[-1]][5] if exts else None
            for s in range(4):
                qg, kstart, klm, has_ext, smm, xmm, ps = info[s]
                if has_ext:
                    cm = P.op("vector", lambda e, ps=ps: e.tensor_tensor(
                        ps[:, 0:128], ps[:, 0:128], sb["mask_l"][:, :], ALU.add),
                        waits=[smm, cst_tok], inc="ve")
                    cmx = P.op("vector", lambda e, s=s: e.tensor_tensor(
                        PD[0][:, 128 * s:128 * s + 128],
                        PD[0][:, 128 * s:128 * s + 128],
                        sb["mask_r"][:, :], ALU.add), waits=[xmm_last], inc="ve")
                else:
                    cm = P.op("vector", lambda e, ps=ps, klm=klm: e.tensor_tensor(
                        ps[:, klm - 128:klm], ps[:, klm - 128:klm],
                        sb["mask_r"][:, :], ALU.add), waits=[smm, cst_tok], inc="ve")
                    cmx = None
                info[s] = info[s] + (cm, cmx)
            nmcs = {}
            for s in range(4):
                qg, kstart, klm, has_ext, smm, xmm, ps, cm, cmx = info[s]
                sc = 4 * par4 + s
                if has_ext:
                    nm1 = P.op("vector", lambda e, ps=ps, sc=sc: e.tensor_reduce(
                        sb["nmax"][:, sc:sc + 1], ps[:, 0:512], AX.X, ALU.max,
                        negate=True), waits=[cm, prev_ex[par4][s]], inc="ve")
                    nm2 = P.op("vector", lambda e, s=s, sc=sc: e.tensor_reduce(
                        sb["nmax2"][:, sc:sc + 1], PD[0][:, 128 * s:128 * s + 128],
                        AX.X, ALU.max, negate=True), waits=[cmx], inc="ve")
                    nmcs[s] = P.op("vector", lambda e, sc=sc: e.tensor_tensor(
                        sb["nmax"][:, sc:sc + 1], sb["nmax"][:, sc:sc + 1],
                        sb["nmax2"][:, sc:sc + 1], ALU.min),
                        waits=[nm1, nm2], inc="ve")
                else:
                    nmcs[s] = P.op("vector", lambda e, ps=ps, klm=klm, sc=sc:
                                   e.tensor_reduce(
                        sb["nmax"][:, sc:sc + 1], ps[:, 0:klm], AX.X, ALU.max,
                        negate=True), waits=[cm, prev_ex[par4][s]], inc="ve")
            ex2s = {}
            for s in range(4):
                qg, kstart, klm, has_ext, smm, xmm, ps, cm, cmx = info[s]
                sc = 4 * par4 + s
                if has_ext:
                    ex1 = P.op("scalar", lambda e, sc=sc, s=s, par4=par4: e.activation(
                        ANp(par4, 640 * s, 512), PS[s][:, :], AF.Exp,
                        bias=sb["nmax"][:, sc:sc + 1], scale=1.0,
                        accum_out=sb["rsum"][:, sc:sc + 1]),
                        waits=[nmcs[s], prev_rc[par4][s], anfree[par4][s]], inc="ac")
                    ex2 = P.op("scalar", lambda e, sc=sc, s=s, par4=par4: e.activation(
                        ANp(par4, 640 * s + 512, 128),
                        PD[0][:, 128 * s:128 * s + 128], AF.Exp,
                        bias=sb["nmax"][:, sc:sc + 1], scale=1.0,
                        accum_out=sb["rsum2"][:, sc:sc + 1]), waits=(), inc="ac")
                    pfree[f"a_s{s}"] = ex1
                    pfree[f"a_x{s}"] = ex2
                    prev_ex[par4][s] = ex2
                    ex2s[s] = (ex1, ex2)
                    an_rdy[s] = ex2
                else:
                    ex1 = P.op("scalar", lambda e, sc=sc, s=s, klm=klm, par4=par4:
                               e.activation(
                        ANp(par4, 640 * s, klm), PS[s][:, 0:klm], AF.Exp,
                        bias=sb["nmax"][:, sc:sc + 1], scale=1.0,
                        accum_out=sb["rsum"][:, sc:sc + 1]),
                        waits=[nmcs[s], prev_rc[par4][s], anfree[par4][s]], inc="ac")
                    pfree[f"a_s{s}"] = ex1
                    prev_ex[par4][s] = ex1
                    ex2s[s] = None
                    an_toks[s] = ex1
                    an_rdy[s] = ex1
            return ex2s

        dgfree = [None, None]

        def emit_vlate(idx, par4, ex2s, an_toks, an_rdy):
            for s in range(4):
                if ex2s[s] is not None:
                    ex1, ex2 = ex2s[s]
                    sc = 4 * par4 + s
                    an_toks[s] = P.op("vector", lambda e, sc=sc: e.tensor_tensor(
                        sb["rsum"][:, sc:sc + 1], sb["rsum"][:, sc:sc + 1],
                        sb["rsum2"][:, sc:sc + 1], ALU.add),
                        waits=[ex1, ex2], inc="ve")
            rc = P.op("vector", lambda e, par4=par4: e.reciprocal(
                sb["rrec"][:, 4 * par4:4 * par4 + 4],
                sb["rsum"][:, 4 * par4:4 * par4 + 4]),
                waits=list(an_toks), inc="ve")
            out = []
            for s in range(4):
                prev_rc[par4][s] = rc
                jj = 4 * (idx % 2) + s
                dg = P.op("vector", lambda e, jj=jj, sc=4 * par4 + s: e.tensor_scalar(
                    sb["diag"][:, 128 * jj:128 * jj + 128], sb["ident"][:, :],
                    sb["rrec"][:, sc:sc + 1], None, ALU.mult),
                    waits=[rc, dgfree[idx % 2]], inc="ve")
                out.append((an_rdy[s], dg))
            return out

        pending_ao = [None]
        ao_by_g = [[] for _ in range(4)]

        def flush_ao():
            nonlocal ao_tok
            if pending_ao[0] is None:
                return
            pair1, rb1, g1, pv_tok = pending_ao[0]
            pending_ao[0] = None
            ao_tok = P.op("scalar", lambda e, pair1=pair1, rb1=rb1, g1=g1:
                          e.activation(
                AOUT_(pair1, 512 * g1, 512, rb1, 64),
                PD[1][0:64, 512:1024], AF.Copy),
                waits=[pv_tok, vt_tok], inc="ac")
            pfree["a_pv"] = ao_tok
            ao_by_g[g1].append(ao_tok)

        def emit_stage_b(idx1, hh1, g1, par1, an1):
            pair1, rb1 = hh1 // 2, 64 * (hh1 % 2)
            qg01 = 512 * g1
            jts = list(range(max(0, 4 * g1 - 4), 4 * g1 + 4))
            if 4 * g1 in jts:
                jts = [4 * g1] + [j for j in jts if j != 4 * g1]
            pv_w0 = [vt_tok, pfree.get("a_pv")]
            pv_tok = None
            last_tmm = None
            attfree = state_attfree
            pend_pv = None

            def emit_pv(args):
                nonlocal pv_tok
                jt, cr0, cr1, so, tev, st0, sp0, slot = args
                pv_tok = P.op("tensor", lambda e, jt=jt, hh1=hh1, cr0=cr0, cr1=cr1,
                              so=so, sl=idx1 % 2, st0=st0, sp0=sp0: e.matmul(
                    PD[1][0:64, 512 + cr0:512 + cr1],
                    VT_(jt, 64 * hh1, 64),
                    ATTp(sl, so + cr0, cr1 - cr0),
                    start=st0, stop=sp0),
                    waits=[tev] + (pv_w0 if st0 else []), inc="te")
                attfree[slot] = pv_tok

            for pos, jt in enumerate(jts):
                ss = []
                for s in range(4):
                    qg = qg01 + 128 * s
                    jstart = max(0, qg - WIN) // 128
                    nkb = (qg + 128 - max(0, qg - WIN)) // 128
                    if jstart <= jt < jstart + nkb:
                        ss.append(s)
                smin, smax = ss[0], ss[-1]
                tdn = "a_td0" if pos % 2 == 0 else "a_td1"
                td = PD[0][:, 512:1024] if pos % 2 == 0 else PD[1][:, 0:512]
                tmm = None
                for i, s in enumerate(ss):
                    qg = qg01 + 128 * s
                    jstart = max(0, qg - WIN) // 128
                    jbl = jt - jstart
                    acol = 640 * s + (512 if (qg >= WIN and jbl == 4) else 128 * jbl)
                    jj = 4 * (idx1 % 2) + s
                    tmm = P.op("tensor", lambda e, s=s, acol=acol, i=i, ns=len(ss),
                               td=td, par1=par1, jj=jj: e.matmul(
                        td[:, 128 * s:128 * s + 128],
                        ANp(par1, acol, 128),
                        sb["diag"][:, 128 * jj:128 * jj + 128],
                        start=(i == 0), stop=(i == ns - 1)),
                        waits=[an1[s][0], an1[s][1]] + ([pfree.get(tdn)] if i == 0 else []),
                        inc=("te" if i == len(ss) - 1 else None))
                last_tmm = tmm
                cr0, cr1 = 128 * smin, 128 * (smax + 1)
                slot = (idx1 % 2, pos % 2)
                tev = P.op("scalar", lambda e, cr0=cr0, cr1=cr1, td=td,
                           so=512 * (pos % 2), sl=idx1 % 2: e.activation(
                    ATTp(sl, so + cr0, cr1 - cr0),
                    td[:, cr0:cr1], AF.Copy),
                    waits=[tmm, attfree.get(slot)], inc="ac")
                pfree[tdn] = tev
                emit_pv((jt, cr0, cr1, 512 * (pos % 2), tev,
                         pos == 0, pos == len(jts) - 1, slot))
            for s in range(4):
                anfree[par1][s] = last_tmm
            dgfree[idx1 % 2] = last_tmm
            pending_ao[0] = (pair1, rb1, g1, pv_tok)

        wo_toks = []
        for kt in range(4):
            wo_toks.append(P.dma("gpsimd", WSLOT(R_W + 2560 + kt * 512, 0, 1024),
                                 dr["woF"][l][kt * 128:(kt + 1) * 128, :],
                                 waits=[last_qk_mm], inc="wd"))
        wo_tok = wo_toks[-1]
        op_tok = None

        def emit_oproj(tc):
            nonlocal op_tok
            aog = ao_by_g[tc][-1]
            for dt_ in range(8):
                bank = f"a_s{dt_ % 4}"
                ps = PS[dt_ % 4]
                omm = None
                for kt in range(4):
                    omm = P.op("tensor", lambda e, kt=kt, dt_=dt_, tc=tc, ps=ps: e.matmul(
                        ps[:, :], WSLOT(R_W + 2560 + kt * 512, 128 * dt_, 128),
                        AOUT_(kt, 512 * tc, 512),
                        start=(kt == 0), stop=(kt == 3)),
                        waits=([aog, wo_tok, pfree.get(bank)] if kt == 0 else ()),
                        inc=("te" if kt == 3 else None))
                op_tok = P.op("vector", lambda e, dt_=dt_, tc=tc, ps=ps: e.tensor_tensor(
                    S_(dt_, 1 + 512 * tc, 512), ps[:, :], S_(dt_, 1 + 512 * tc, 512),
                    ALU.add), waits=[omm, stream_tok], inc="ve")
                pfree[bank] = op_tok

        op_emitted = [False] * 4
        for idx in range(len(seq) + 1):
            if idx < len(seq):
                hh, g = seq[idx]
                par4 = idx % 4
                info, exts = emit_scores(hh, g, par4)
            if saved is not None:
                flush_ao()
                emit_stage_b(*saved)
                saved = None
            if idx < len(seq):
                an_toks = [None] * 4
                an_rdy = [None] * 4
                ex2s = emit_softmax(hh, g, par4, info, exts, an_toks, an_rdy)
                pairs = emit_vlate(idx, par4, ex2s, an_toks, an_rdy)
                saved = (idx, hh, g, par4, pairs)
            for tc in range(4):
                if not op_emitted[tc] and len(ao_by_g[tc]) == 8:
                    emit_oproj(tc)
                    op_emitted[tc] = True
        flush_ao()
        for tc in range(4):
            if not op_emitted[tc] and len(ao_by_g[tc]) == 8:
                emit_oproj(tc)
                op_emitted[tc] = True
        assert all(op_emitted)
        for s in range(4):
            pfree[f"ps{s}"] = pfree.get(f"a_s{s}")
        pfree["pd0"] = P.tok("ac")
        pfree["pd1"] = ao_tok

        _, xln2_tok = emit_ln(True, f"affine:{l}", [op_tok])

        glu_tok = None
        w2_tok = None
        for th in range(2):
            pair_toks = [None, None]
            slot_free = [P.tok("te"), P.tok("te")]
            for p in range(NPAIR + 1):
                if p < NPAIR:
                    slot = p % 2
                    pair_toks[slot] = P.dma(
                        "gpsimd", WSLOT(R_W + slot * 1024, 0, 2048),
                        dr["w1r"][l][p], waits=[slot_free[slot]], inc="wd")
                if p == 0:
                    continue
                pp = p - 1
                slot = pp % 2
                for tci in range(2):
                    tc = 2 * th + tci
                    psa = PSAP["ps0" if tci == 0 else "ps2"]
                    psg = PSAP["ps1" if tci == 0 else "ps3"]
                    psan = "ps0" if tci == 0 else "ps2"
                    psgn = "ps1" if tci == 0 else "ps3"
                    amm = gmm = None
                    for kt in range(8):
                        amm = P.op("tensor", lambda e, kt=kt, tc=tc, slot=slot, psa=psa: e.matmul(
                            psa[:, :], WSLOT(R_W + slot * 1024, 256 * kt, 128),
                            XBF_(kt, 1 + 512 * tc, 512), start=(kt == 0), stop=(kt == 7)),
                            waits=([xln2_tok, pair_toks[slot], pfree.get(psan)]
                                   if kt == 0 else ()),
                            inc=("te" if kt == 7 else None))
                    for kt in range(8):
                        gmm = P.op("tensor", lambda e, kt=kt, tc=tc, slot=slot, psg=psg: e.matmul(
                            psg[:, :], WSLOT(R_W + slot * 1024, 256 * kt + 128, 128),
                            XBF_(kt, 1 + 512 * tc, 512), start=(kt == 0), stop=(kt == 7)),
                            waits=([pfree.get(psgn)] if kt == 0 else ()),
                            inc=("te" if kt == 7 else None))
                    if tci == 1:
                        slot_free[slot] = gmm
                    gg = P.op("scalar", lambda e, pp=pp, l=l, tci=tci, psg=psg: e.activation(
                        sb["scr_a" if tci == 0 else "scr_c"][:, :], psg[:, :], AF.Gelu,
                        bias=bcol(f"b1g{l}", pp), scale=1.0),
                        waits=[gmm, pfree.get("scr_a" if tci == 0 else "scr_c")], inc="ac")
                    glu_tok = P.op("vector", lambda e, pp=pp, l=l, tci=tci, psa=psa: e.scalar_tensor_tensor(
                        abf(R_QT + pp * 512, 512 * tci, 512),
                        psa[:, :], bcol(f"b1a{l}", pp),
                        sb["scr_a" if tci == 0 else "scr_c"][:, :],
                        ALU.add, ALU.mult),
                        waits=[gg, (w2_tok if th == 1 else None)], inc="ve")
                    pfree[psan] = glu_tok
                    pfree[psgn] = glu_tok
                    pfree["scr_a" if tci == 0 else "scr_c"] = glu_tok
            p8 = [PS[0][:, :], PS[1][:, :], PS[2][:, :], PS[3][:, :],
                  PD[0][:, 0:512], PD[0][:, 512:1024], PD[1][:, 0:512],
                  PD[1][:, 512:1024]]
            p8n = ["ps0", "ps1", "ps2", "ps3", "pd0", "pd0", "pd1", "pd1"]
            for tci in range(2):
                tc = 2 * th + tci
                kslot_free = [P.tok("te"), P.tok("te")]
                k_toks = [None, None]
                wmm = None
                for kt in range(NPAIR):
                    slot = kt % 2
                    k_toks[slot] = P.dma(
                        "gpsimd", WSLOT(R_W + 4096 + slot * 512, 0, 1024),
                        dr["w2p"][l][128 * kt:128 * kt + 128, :],
                        waits=[kslot_free[slot],
                               glu_tok if kt == 0 else None], inc="wd")
                    for dt_ in range(8):
                        w0 = [k_toks[slot]]
                        if kt == 0:
                            w0 += [glu_tok] + [pfree.get(n) for n in set(p8n)]
                        wmm = P.op("tensor", lambda e, kt=kt, dt_=dt_, tci=tci,
                                   slot=slot: e.matmul(
                            p8[dt_], WSLOT(R_W + 4096 + slot * 512, 128 * dt_, 128),
                            abf(R_QT + kt * 512, 512 * tci, 512),
                            start=(kt == 0), stop=(kt == NPAIR - 1)),
                            waits=(w0 if dt_ == 0 else ()),
                            inc=("te" if dt_ == 7 else None))
                    kslot_free[slot] = wmm
                for dt_ in range(8):
                    w2_tok = P.op("vector", lambda e, dt_=dt_, tc=tc, l=l:
                                  e.scalar_tensor_tensor(
                        S_(dt_, 1 + 512 * tc, 512), p8[dt_], bcol(f"b2{l}", dt_),
                        S_(dt_, 1 + 512 * tc, 512), ALU.add, ALU.add),
                        waits=[wmm], inc="ve")
                    pfree[p8n[dt_]] = w2_tok
        pfree["r_w"] = w2_tok
        return w2_tok


    deps0 = xin_toks + [cst_tok, ones_tok]
    cur = None

    def emit_ffmod(deps):
        _, xbf_tok = emit_ln(False, "stream", deps)
        stream_tok = xbf_tok
        t = emit_f_layer(0, xbf_tok, stream_tok)
        xb = None
        for dt_ in range(8):
            xb = P.op("vector", lambda e, dt_=dt_: e.tensor_copy(
                XBF_(dt_, 0, TG), S_(dt_, 0, TG)), waits=[t], inc="ve")
        t = emit_f_layer(1, xb, t)
        mt = None
        prev_mt = [None, None]
        mts = [None] * 8
        for dt_ in range(8):
            so = R_W + 2304 * (dt_ % 2)
            ft = P.dma("sync", af32(so, TG), xo_scr[dt_],
                       waits=[t, prev_mt[dt_ % 2]], inc="sd")
            mt = P.op("vector", lambda e, dt_=dt_, so=so: e.scalar_tensor_tensor(
                S_(dt_, 0, TG), S_(dt_, 0, TG), 0.5, af32(so, TG),
                ALU.mult, ALU.add), waits=[ft, t], inc="ve")
            prev_mt[dt_ % 2] = mt
            mts[dt_] = mt
        return mt, mts

    def emit_A(deps):
        _, xbf_tok = emit_ln(False, "bf", deps)
        def proj_w_load(name, tok0):
            toks = []
            for kt in range(8):
                dst = (WSLOT(R_W + kt * 1024, 0, 2048) if kt < 5 else
                       WSLOT(R_QT + (kt - 5) * 1024, 0, 2048))
                toks.append(P.dma("gpsimd", dst, dr[name][kt * 128:(kt + 1) * 128, :],
                                  waits=[tok0], inc="wd"))
            return toks[-1], [
                (WSLOT(R_W + kt * 1024, 0, 2048) if kt < 5 else
                 WSLOT(R_QT + (kt - 5) * 1024, 0, 2048)) for kt in range(8)]

        last_mm = None
        spill_toks = []
        for which, wname, scr, bias in (("q", "wq", q_scr, "bq"), ("k", "wk", k_scr, "bk")):
            wtok, wslots = proj_w_load(wname, xbf_tok if which == "q" else last_mm)
            for fc in range(16):
                stg = sb["qf"] if which == "q" else sb["kf"]
                ev = None
                for tc in range(4):
                    psn = PSN[tc % 2]
                    ps = PSAP[psn]
                    mm = None
                    for kt in range(8):
                        mm = P.op("tensor", lambda e, kt=kt, fc=fc, tc=tc, ps=ps,
                                  wslots=wslots: e.matmul(
                            ps[:, :],
                            bass.AP(wslots[kt].tensor, wslots[kt].offset + 128 * fc,
                                    [[NCOLS * 2, 128], [1, 128]]),
                            XBF_(kt, 1 + 512 * tc, 512),
                            start=(kt == 0), stop=(kt == 7)),
                            waits=([wtok, xbf_tok, pfree.get(psn)] if kt == 0 else ()),
                            inc=("te" if kt == 7 else None))
                    last_mm = mm
                    ee = P.op("scalar", lambda e, fc=fc, ps=ps, bias=bias: e.activation(
                        sb["scr_a"][:, :], ps[:, :], AF.Exp, bias=bcol(bias, fc)),
                        waits=[mm, pfree.get("scr_a")], inc="ac")
                    rr = P.op("scalar", lambda e, fc=fc, ps=ps, bias=bias: e.activation(
                        sb["scr_b"][:, :], ps[:, :], AF.Relu, bias=bcol(bias, fc)),
                        waits=[ee, pfree.get("scr_b")], inc="ac")
                    ev = P.op("vector", lambda e, tc=tc, stg=stg: e.scalar_tensor_tensor(
                        stg[:, 512 * tc:512 * tc + 512], sb["scr_a"][:, :], 1.0,
                        sb["scr_b"][:, :], ALU.min, ALU.add),
                        waits=[rr, spill_toks[-1] if spill_toks else None],
                        inc="ve")
                    pfree[psn] = ev
                    pfree["scr_a"] = ev
                    pfree["scr_b"] = ev
                if which == "q":
                    spill_toks.append(P.dma("sync", q_scr[fc], stg[:, :], waits=[ev]))
                else:
                    dst = bass.AP(k_scr.tensor, fc * 16384,
                                  [[128, 128], [262144, 16], [1, 128]])
                    src = stg[:, :].rearrange("p (j c) -> p j c", j=16)
                    spill_toks.append(P.dma("sync", dst, src, waits=[ev]))
        wtok, wslots = proj_w_load("wv", last_mm)
        for tt in range(16):
            ev = None
            for ec in range(4):
                psn = PSN[ec % 2]
                ps = PSAP[psn]
                mm = None
                for kt in range(8):
                    mm = P.op("tensor", lambda e, kt=kt, tt=tt, ec=ec, ps=ps,
                              wslots=wslots: e.matmul(
                        ps[:, :], XBF_(kt, 1 + 128 * tt, 128),
                        bass.AP(wslots[kt].tensor, wslots[kt].offset + 512 * ec,
                                [[NCOLS * 2, 128], [1, 512]]),
                        start=(kt == 0), stop=(kt == 7)),
                        waits=([wtok, pfree.get(psn)] if kt == 0 else ()),
                        inc=("te" if kt == 7 else None))
                last_mm = mm
                ev = P.op("vector", lambda e, ec=ec, ps=ps: e.tensor_copy(
                    sb["qf"][:, 512 * ec:512 * ec + 512], ps[:, :]),
                    waits=[mm, spill_toks[-1] if ec == 0 else None], inc="ve")
                pfree[psn] = ev
            dst = bass.AP(v_scr.tensor, tt * 16384, [[128, 128], [262144, 16], [1, 128]])
            src = sb["qf"][:, :].rearrange("p (b c) -> p b c", b=16)
            spill_toks.append(P.dma("sync", dst, src, waits=[ev]))
        proj_done = spill_toks[-1]
        if DBG_ATT:
            P.dma("gpsimd", ydbg[47].rearrange("p (j c) -> p j c", j=4),
                  bass.AP(v_scr.tensor, 5 * 262144, [[128, 128], [16384, 4], [1, 128]]),
                  waits=[proj_done], inc="wd")

        wa_slots = []
        wa_toks = []
        for kt in range(16):
            col = (R_W + kt * 512) if kt < 10 else (R_XBF + 4096 + (kt - 10) * 512)
            wa_slots.append(col)
            wa_toks.append(P.dma("gpsimd", WSLOT(col, 0, 1024),
                                 dr["woA"][kt * 128:(kt + 1) * 128, :],
                                 waits=[last_mm if kt == 0 else None], inc="wd"))
        wa_tok = wa_toks[-1]

        at_tok = None
        kb_free = [None, None]
        vb_free = [None, None]
        qb_free = None
        sT_free = None
        for Tq in range(4):
            qg = 512 * Tq
            J = 4 * Tq + 4
            qb_toks = []
            for fc in range(16):
                qb_toks.append(P.dma("sync",
                                     abf(R_QT + fc * 256, 0, 512),
                                     q_scr[fc][:, qg:qg + 512],
                                     waits=[proj_done, qb_free], inc="sd"))
            qb_tok = qb_toks[-1]
            den_mm = None
            sT_toks = [None] * J
            for jt in range(J):
                slot = jt % 2
                kb = P.dma("sync",
                           bass.AP(ar_tb, (R_VT + 2048 + slot * 1024) * 2,
                                   [[NCOLS * 2, 128], [128, 16], [1, 128]]),
                           bass.AP(k_scr.tensor, jt * 262144,
                                   [[128, 128], [16384, 16], [1, 128]]),
                           waits=[proj_done, kb_free[slot]], inc="sd")
                cb = jt - 4 * Tq
                c_lo = 128 * cb if cb >= 0 else 0
                pdn = "pd0" if jt % 2 == 0 else "pd1"
                pd = PSAP[pdn]
                mm = None
                for kt in range(16):
                    mm = P.op("tensor", lambda e, kt=kt, slot=slot, c_lo=c_lo, pd=pd: e.matmul(
                        pd[:, c_lo:512],
                        abf(R_VT + 2048 + slot * 1024, 128 * kt, 128),
                        abf(R_QT + kt * 256, c_lo, 512 - c_lo),
                        start=(kt == 0), stop=(kt == 15)),
                        waits=([kb, qb_tok, pfree.get(pdn), sT_free] if kt == 0 else ()),
                        inc=("te" if kt == 15 else None))
                kb_free[slot] = mm
                tev = P.op("vector", lambda e, jt=jt, c_lo=c_lo, pd=pd: e.tensor_copy(
                    abf(R_KT + jt * 256, c_lo, 512 - c_lo), pd[:, c_lo:512]),
                    waits=[mm], inc="ve")
                pfree[pdn] = tev
                if cb >= 0:
                    tev = P.op("vector", lambda e, jt=jt, cb=cb: e.tensor_tensor(
                        abf(R_KT + jt * 256, 128 * cb, 128),
                        abf(R_KT + jt * 256, 128 * cb, 128),
                        sb["triu01"][:, :], ALU.mult), waits=[tev], inc="ve")
                sT_toks[jt] = tev
                if DBG_ATT and Tq == 0:
                    P.dma("gpsimd", ydbg[jt], abf(R_KT + jt * 256, 0, 512),
                          waits=[tev], inc="wd")
                den_mm = P.op("tensor", lambda e, jt=jt, c_lo=c_lo: e.matmul(
                    PS[2][0:1, c_lo:512], sb["ones_col_bf"][:, :],
                    abf(R_KT + jt * 256, c_lo, 512 - c_lo),
                    start=(jt == 0), stop=False),
                    waits=[tev] + ([pfree.get("ps2")] if jt == 0 else []), inc="te")
            epsrow = P.op("vector", lambda e: e.memset(
                sb["rowbf"][0:1, 0:512], float(EPS_LA)),
                waits=[pfree.get("rowbf")], inc="ve")
            den_mm = P.op("tensor", lambda e: e.matmul(
                PS[2][0:1, 0:512], sb["ones_col_bf"][0:1, 0:1],
                sb["rowbf"][0:1, 0:512], start=False, stop=True),
                waits=[den_mm, epsrow], inc="te")
            pfree["rowbf"] = den_mm
            dsb = P.op("vector", lambda e: e.tensor_copy(
                sb["rowstat"][0:1, 0:512], PS[2][0:1, :]), waits=[den_mm], inc="ve")
            denbf_tok = P.op("vector", lambda e: e.tensor_copy(
                sb["denbf"][0:1, :], PS[2][0:1, :]), waits=[den_mm], inc="ve")
            pfree["ps2"] = denbf_tok
            dr2 = P.op("vector", lambda e: e.reciprocal(
                sb["rowstat"][0:1, 512:1024], sb["rowstat"][0:1, 0:512]),
                waits=[dsb], inc="ve")
            bc = P.op("tensor", lambda e: e.matmul(
                PD[0][:, 0:512], sb["ones_row"][:, :], sb["rowstat"][0:1, 512:1024],
                start=True, stop=True), waits=[dr2, pfree.get("pd0")], inc="te")
            rB = P.op("vector", lambda e: e.tensor_copy(
                sb["scr_c"][:, :], PD[0][:, 0:512]), waits=[bc, pfree.get("scr_c")],
                inc="ve")
            if DBG_ATT and Tq == 0:
                P.dma("sync", ydbg[24][0:1, :], sb["rowstat"][0:1, 0:512], waits=[dsb], inc="sd"); P.dma("sync", ydbg[24][1:2, :], sb["rowstat"][0:1, 512:1024], waits=[dr2], inc="sd")
                P.dma("sync", ydbg[45], sb["scr_c"][:, :], waits=[rB], inc="sd")
                for fcd in range(16):
                    P.dma("gpsimd", ydbg[25 + fcd], abf(R_QT + fcd * 256, 0, 512),
                          waits=[qb_tok], inc="wd")
            pfree["pd0"] = rB
            nu_tok = None
            for ec in range(16):
                slot = ec % 2
                vb = P.dma("sync",
                           bass.AP(ar_tb, (R_VT + slot * 1024) * 2,
                                   [[NCOLS * 2, 128], [128, J], [1, 128]]),
                           bass.AP(v_scr.tensor, ec * 262144,
                                   [[128, 128], [16384, J], [1, 128]]),
                           waits=[proj_done, vb_free[slot]], inc="sd")
                if DBG_ATT and Tq == 0 and ec == 5:
                    P.dma("gpsimd", ydbg[46],
                          abf(R_VT + slot * 1024, 0, 512), waits=[vb], inc="wd")
                psn = "ps0" if ec % 2 == 0 else "ps1"
                ps = PSAP[psn]
                mm = None
                for jt in range(J):
                    cb = jt - 4 * Tq
                    c_lo = 128 * cb if cb >= 0 else 0
                    mm = P.op("tensor", lambda e, jt=jt, slot=slot, c_lo=c_lo, ps=ps: e.matmul(
                        ps[:, c_lo:512], abf(R_VT + slot * 1024, 128 * jt, 128),
                        abf(R_KT + jt * 256, c_lo, 512 - c_lo),
                        start=(jt == 0), stop=False),
                        waits=([vb, sT_toks[J - 1], pfree.get(psn)] if jt == 0 else ()),
                        inc=None)
                mm = P.op("tensor", lambda e, ec=ec, ps=ps: e.matmul(
                    ps[:, 0:512], sb["bvbf"][0:1, 128 * ec:128 * ec + 128],
                    sb["denbf"][0:1, 0:512], start=False, stop=True),
                    waits=[denbf_tok], inc="te")
                vb_free[slot] = mm
                nu_tok = P.op("vector", lambda e, ec=ec, ps=ps: e.tensor_tensor(
                    abf(R_XBF + ec * 256, 0, 512), ps[:, :], sb["scr_c"][:, :],
                    ALU.mult), waits=[mm, rB], inc="ve")
                pfree[psn] = nu_tok
                if DBG_ATT and Tq == 0:
                    P.dma("gpsimd", ydbg[8 + ec], abf(R_XBF + ec * 256, 0, 512),
                          waits=[nu_tok], inc="wd")
            pfree["scr_c"] = nu_tok
            qb_free = nu_tok
            sT_free = nu_tok
            oo = None
            for dt_ in range(8):
                psn = "ps2" if dt_ % 2 == 0 else "ps3"
                ps = PSAP[psn]
                mm = None
                for kt in range(16):
                    mm = P.op("tensor", lambda e, kt=kt, dt_=dt_, ps=ps: e.matmul(
                        ps[:, :], WSLOT(wa_slots[kt], 128 * dt_, 128),
                        abf(R_XBF + kt * 256, 0, 512),
                        start=(kt == 0), stop=(kt == 15)),
                        waits=([nu_tok, wa_tok, pfree.get(psn)] if kt == 0 else ()),
                        inc=("te" if kt == 15 else None))
                oo = P.op("vector", lambda e, dt_=dt_, qg=qg, ps=ps: e.scalar_tensor_tensor(
                    S_(dt_, 1 + qg, 512), ps[:, :], bcol("bo", dt_),
                    S_(dt_, 1 + qg, 512), ALU.add, ALU.add), waits=[mm], inc="ve")
                pfree[psn] = oo
            at_tok = oo
        pfree["r_w"] = at_tok
        return at_tok

    cur = deps0
    tok = None
    if "ln0" in parts:
        _, tok = emit_ln(False, "stream", cur)
        cur = [tok]
    elif "ffl0" in parts:
        _, xbf_tok = emit_ln(False, "stream", cur)
        tok = emit_f_layer(0, xbf_tok, xbf_tok)
        cur = [tok]
    mts = None
    if "ff1" in parts:
        tok, mts = emit_ffmod(cur)
        cur = [tok]
    if "att" in parts:
        tok = emit_A(cur)
        cur = [tok]
        mts = None
    if "ff2" in parts:
        tok, mts = emit_ffmod(cur)
        cur = [tok]

    out_toks = []
    for dt_ in range(8):
        w = [mts[dt_]] if mts is not None else cur
        eng = "sync" if dt_ % 2 == 0 else "gpsimd"
        out_toks.append(P.dma(eng, y[dt_], S_(dt_, 1, T), waits=w,
                              inc="sd" if eng == "sync" else "wd"))
    P.op("sync", lambda e: e.nop(), waits=out_toks)
    P.op("gpsimd", lambda e: e.nop(), waits=out_toks)

    P.finalize(st)
    st.close()
    return nc


_CACHE = {}


def kernel(**inputs):
    h = host_prepare(inputs)
    key = "full"
    if key not in _CACHE:
        _CACHE[key] = build_program(h)
    nc = _CACHE[key]
    common = dict(
        wq=h["wq"], wk=h["wk"], wv=h["wv"], woA=h["woA"],
        wqkv=h["wqkv"], woF=h["woF"], w1r=h["w1r"], w2p=h["w2p"],
        blob=h["blob"], qsks=h["qsks"], bvrow=h["bvrow"],
        tcq=h["tcq"], tsq=h["tsq"], tck=h["tck"], tsk=h["tsk"],
        mask_l=h["mask_l"], mask_r=h["mask_r"],
        triu01=h["triu01"],
        ident=h["ident"])
    in_maps = [dict(common, xin=h["x_cores"][b]) for b in range(NCORES)]
    res = run_bass_kernel_spmd(nc, in_maps, core_ids=list(range(NCORES)))
    out = np.empty((NCORES, T, D), dtype=np.float32)
    for b in range(NCORES):
        yb = res.results[b]["y"]
        out[b] = yb.reshape(D, T).T
    return out
